# version 01
# baseline (speedup 1.0000x reference)
import math

import numpy as np
import concourse.bass as bass
import concourse.mybir as mybir
from concourse.bass_utils import run_bass_kernel_spmd

F32 = mybir.dt.float32
BF16 = mybir.dt.bfloat16
I32 = mybir.dt.int32
AF = mybir.ActivationFunctionType
ALU = mybir.AluOpType
AX = mybir.AxisListType

NCORES = 8
S = 4096
D = 1024
C = 8
TB = 512
NTB = S // TB
E = 8
DFF = 4096
H = 16
HD = 64
VD0 = 512
CDIM = 2 * D + H * VD0
LN_EPS = 1e-5
DEBUG_L0F = False
LEAKY = [AF.Prelu]

ENGINES = ("tensor", "scalar", "vector", "gpsimd", "sync")
BASE_SEMS = ("sT", "sA", "sV", "sG")


class Ev:
    __slots__ = ("sem", "val")

    def __init__(self, sem, val):
        self.sem = sem
        self.val = val


class Prog:
    def __init__(self, pace=True):
        self.pace_on = pace
        self.q = {e: [] for e in ENGINES}
        self.cnt = {s: 0 for s in BASE_SEMS}
        self._mmg = 0
        self._mmc = 0

    def snap(self):
        return [Ev(s, v) for s, v in self.cnt.items() if v > 0]

    def op(self, eng, fn, waits=(), inc=None, inc_by=1, is_dma=False):
        if eng == "tensor":
            self.flush_pace()
        ws = [(w.sem, w.val) for w in waits if w is not None]
        if not is_dma and eng in ("vector", "scalar", "gpsimd"):
            sem = {"vector": "sV", "scalar": "sA", "gpsimd": "sG"}[eng]
            prev = self.cnt[sem]
            if prev > 0:
                ws.append((sem, prev))
            self.cnt[sem] += 1
            ev_self = Ev(sem, self.cnt[sem])
            if inc is not None and inc != sem:
                self.cnt[inc] += inc_by
                ev = Ev(inc, self.cnt[inc])
                self.q[eng].append((fn, ws, [(sem, 1), (inc, inc_by)]))
                return ev
            self.q[eng].append((fn, ws, [(sem, 1)]))
            return ev_self
        ev = None
        incs = []
        if inc is not None:
            self.cnt[inc] += inc_by
            ev = Ev(inc, self.cnt[inc])
            incs = [(inc, inc_by)]
        self.q[eng].append((fn, ws, incs))
        return ev

    def dma(self, queue, out, in_, waits=(), cls="g"):
        sem = "d_" + cls
        if sem not in self.cnt:
            self.cnt[sem] = 0

        def fn(e, out=out, in_=in_):
            return e.dma_start(out=out, in_=in_)
        return self.op(queue, fn, waits, inc=sem, inc_by=16, is_dma=True)

    def flush_pace(self):
        if self._mmg:
            cyc = self._mmc + 160
            if self.pace_on:
                self.q["tensor"].append(
                    (lambda e, cyc=cyc: e.nop(cycle_cnt=cyc), [], []))
            self._mmg = 0
            self._mmc = 0

    def _pe(self, fn, waits, inc, cost):
        ws = [(w.sem, w.val) for w in waits if w is not None]
        ev = None
        incs = []
        if inc is not None:
            self.cnt[inc] += 1
            ev = Ev(inc, self.cnt[inc])
            incs = [(inc, 1)]
        self.q["tensor"].append((fn, ws, incs))
        self._mmg += 1
        self._mmc += cost
        if self._mmg >= 3:
            self.flush_pace()
        return ev

    def mm(self, out, lhsT, rhs, start, stop, waits=(), inc=None,
           cost=None, tile_position=None):
        if cost is None:
            n = rhs.shape[-1]
            cost = n * (4 if lhsT.dtype == F32 else 1)

        def fn(e, out=out, lhsT=lhsT, rhs=rhs, start=start, stop=stop,
               tp=tile_position):
            return e.matmul(out, lhsT, rhs, start=start, stop=stop,
                            tile_position=tp)
        return self._pe(fn, waits, inc, cost)

    def tr(self, out, in_, ident, waits=(), inc=None):
        def fn(e, out=out, in_=in_, ident=ident):
            return e.transpose(out, in_, ident)
        return self._pe(fn, waits, inc, 700)

    def emit(self, nc, sems):
        self.flush_pace()
        with nc.Block() as block:
            def make(eng_name):
                def body(eng):
                    hw = {}
                    for fn, waits, incs in self.q[eng_name]:
                        wm = {}
                        for s, v in waits:
                            if wm.get(s, 0) < v:
                                wm[s] = v
                        for s, v in wm.items():
                            if hw.get(s, 0) < v:
                                eng.wait_ge(sems[s], v)
                                hw[s] = v
                        instr = fn(eng)
                        for s, by in incs:
                            instr = instr.then_inc(sems[s], by)
                return body
            block.tensor(make("tensor"))
            block.scalar(make("scalar"))
            block.vector(make("vector"))
            block.gpsimd(make("gpsimd"))
            block.sync(make("sync"))



class Build:
    CP = {"eps": 0, "mhalf": 1, "oct": 2, "ropea": 3, "halfpi": 4,
          "inv2pi": 5}

    def __init__(self, detect_races=False, pace=True):
        nc = bass.Bass("TRN2", target_bir_lowering=False,
                       detect_race_conditions=detect_races)
        self.nc = nc
        self.P = Prog(pace=pace)
        d = {}

        def inp(name, shape):
            d[name] = nc.dram_tensor(name, list(shape), F32,
                                     kind="ExternalInput").ap()
        inp("x_in", (D, S))
        inp("w_cT", (D, CDIM))
        inp("bc_p", (128, 16))
        inp("bc_v", (1, H * VD0))
        inp("w_oT", (H * VD0, D))
        inp("bo_p", (128, 8))
        inp("cn_w1", (D, D)); inp("cn_w2", (D, 512))
        inp("cn_w3", (512, 256)); inp("cn_w4", (256, 1))
        inp("gc_w1", (D, 512)); inp("gc_w2", (512, 256)); inp("gc_w3", (256, 1))
        inp("l0f_b", (128, 24))
        inp("wq", (D, D)); inp("wk", (D, D)); inp("wv", (D, D)); inp("w0", (D, D))
        inp("qkvb_p", (128, 32))
        inp("qkvb_rows", (4, D))
        inp("wr", (D, E)); inp("br_row", (1, E))
        inp("w1e", (E, D, DFF)); inp("w2e", (E, DFF, D))
        inp("b1e_p", (E, 128, DFF // 128)); inp("b2e_p", (E, 128, D // 128))
        self.inp = d
        self.y_out = nc.dram_tensor("y_out", [D, S], F32,
                                    kind="ExternalOutput").ap()
        self.dbg = nc.dram_tensor("dbg", [4, S], F32,
                                  kind="ExternalOutput").ap()

        T = nc.dram_tensor
        self.xs = T("xs", [D, S], F32, kind="Internal").ap()
        self.xns = T("xns", [D, S], F32, kind="Internal").ap()
        self.alts = T("alts", [D, S], F32, kind="Internal").ap()
        self.qks = T("qks", [2 * D, S], BF16, kind="Internal").ap()
        self.vs = T("vs", [S, H * VD0], BF16, kind="Internal").ap()
        self.bcvs = T("bcvs", [128, H * VD0], F32, kind="Internal").ap()
        self.qbias_bc = T("qbias_bc", [128, 3 * D], F32, kind="Internal").ap()
        self.c_dram = T("c_dram", [E, S], F32, kind="Internal").ap()

        A = nc.alloc_sbuf_tensor
        self.A16 = A("A16", [128, 86016], BF16)
        self.A32 = A("A32", [128, 7168], F32)
        self.SI = A("SI", [128, 512], I32)
        self.SIF = A("SIF", [128, 512], F32)
        self.cpool = A("cpool", [128, 8], F32)
        self.ones_b = A("ones_b", [128, 1], BF16)
        self.onerow_f = A("onerow_f", [1, 128], F32)
        self.ident_b = A("ident_b", [128, 128], BF16)
        self.ident_f = A("ident_f", [128, 128], F32)
        self.pc = A("pc", [128, 2], F32)
        self.pci = A("pci", [128, 2], I32)
        self.bias = A("biaspool", [128, 88], F32)
        self.b1t = A("b1t", [128, 32], F32)
        self.wr_t = A("wr_t", [128, 64], F32)
        self.br_b = A("br_b", [128, 8], F32)
        self.c_t = A("c_t", [8, 128], F32)
        self.tcol = A("tcol", [128, 1], F32)
        self.irow = A("irow", [1, 512], F32)
        self.wrow = A("wrow", [1, 512], F32)

        self.pb = [nc.alloc_psum_tensor(f"pb{i}", [128, 512], F32)
                   for i in range(8)]
        self.bfree = [None] * 8

        self.sems = {}

    def a16(self, off, cols):
        assert off + cols <= 86016, (off, cols)
        return self.A16[:, off:off + cols]

    def a32(self, off, cols):
        assert off + cols <= 5120, (off, cols)
        return self.A32[:, off:off + cols]

    def row(self, off, cols=512):
        assert 5120 <= off and off + cols <= 7168, (off, cols)
        return self.A32[0:1, off:off + cols]

    def cp(self, name, p=128):
        i = Build.CP[name]
        return self.cpool[:, i:i + 1] if p == 128 else self.cpool[0:1, i:i + 1]


R_LN = 5120
R_FFN = 5120
R_ATT = 5632
R_CE = 6144
R_BR = 6656


def emit_preamble(b: Build):
    P = b.P
    vals = {"eps": LN_EPS, "mhalf": -0.5, "oct": 0.125,
            "ropea": -2.0 * math.log(10000.0) / 1024.0,
            "halfpi": math.pi / 2.0, "inv2pi": 1.0 / (2 * math.pi)}
    for name, i in Build.CP.items():
        P.op("vector", lambda e, i=i, v=vals[name]:
             e.memset(b.cpool[:, i:i + 1], float(v)))
    P.op("vector", lambda e: e.memset(b.onerow_f[:], 1.0))
    ev_ones = P.op("vector", lambda e: e.memset(b.ones_b[:], 1.0), inc="sV")
    ev_ir = P.op("vector", lambda e: e.memset(b.irow[0:1, 0:1], 0.0))
    k = 1
    while k < 512:
        ev_ir = P.op("vector", lambda e, k=k: e.tensor_scalar(
            b.irow[0:1, k:2 * k], b.irow[0:1, 0:k], float(k), None, ALU.add))
        k *= 2
    ev_wrow = P.op("scalar", lambda e: e.activation(
        b.wrow[:], b.irow[:], AF.Exp, scale=b.cp("ropea", p=1)),
        waits=[ev_ones, ev_ir], inc="sA")

    ev_i0 = P.op("gpsimd", lambda e: e.iota(
        b.pci[:, 0:1], pattern=[[0, 1]], base=0, channel_multiplier=1),
        inc="sG")
    P.op("vector", lambda e: e.tensor_copy(b.pc[:, 0:1], b.pci[:, 0:1]),
         waits=[ev_i0])
    ev_pc = P.op("vector", lambda e: e.tensor_copy(b.pc[:, 1:2], b.pci[:, 0:1]),
                 inc="sV")

    ev1 = P.op("gpsimd", lambda e: e.memset(b.ident_b[:], 1.0), inc="sG")
    ev2 = P.op("gpsimd", lambda e: e.affine_select(
        b.ident_b[:], b.ident_b[:], pattern=[[-1, 128]],
        compare_op=ALU.is_equal, fill=0.0, base=0, channel_multiplier=1),
        waits=[ev1], inc="sG")
    ev3 = P.op("gpsimd", lambda e: e.memset(b.ident_f[:], 1.0),
               waits=[ev2], inc="sG")
    ev_id = P.op("gpsimd", lambda e: e.affine_select(
        b.ident_f[:], b.ident_f[:], pattern=[[-1, 128]],
        compare_op=ALU.is_equal, fill=0.0, base=0, channel_multiplier=1),
        waits=[ev3], inc="sG")

    evb = []
    evb.append(P.dma("sync", b.bias[:, 0:16], b.inp["bc_p"][:], cls="pre"))
    evb.append(P.dma("sync", b.bias[:, 16:24], b.inp["bo_p"][:], cls="pre"))
    evb.append(P.dma("sync", b.bias[:, 24:48], b.inp["l0f_b"][:], cls="pre"))
    evb.append(P.dma("sync", b.bias[:, 48:80], b.inp["qkvb_p"][:],
                     cls="pre"))
    evb.append(P.dma("sync",
                     b.wr_t[:].rearrange("p (c e) -> p c e", e=8),
                     b.inp["wr"].rearrange("(c p) e -> p c e", p=128),
                     cls="pre"))
    evb.append(P.dma("sync", b.row(R_BR, 8), b.inp["br_row"][:], cls="pre"))
    evb = evb[-1:]
    bank = b.pb[7]
    evm = P.mm(bank[:, 0:8], b.onerow_f[:], b.row(R_BR, 8), True, True,
               waits=evb + [ev_pc], inc="sT", cost=64)
    evc = P.op("vector", lambda e: e.tensor_copy(b.br_b[:], bank[:, 0:8]),
               waits=[evm], inc="sV")
    b.bfree[7] = evc

    evr = P.dma("sync", b.row(R_ATT, 512), b.inp["bc_v"][0:1, 0:512],
                cls="prr")
    prev = None
    for vb in range(16):
        bk = b.pb[6 if vb % 2 == 0 else 7]
        evr2 = evr
        evm = P.mm(bk[:, :], b.onerow_f[:], b.row(R_ATT, 512), True, True,
                   waits=[evr2, b.bfree[6 if vb % 2 == 0 else 7]],
                   inc="sT", cost=2048)
        evc = P.op("vector", lambda e, bk=bk: e.tensor_copy(
            b.a32(4096, 512), bk[:, :]), waits=[evm, prev], inc="sV")
        b.bfree[6 if vb % 2 == 0 else 7] = evc
        evs = P.dma("sync", b.bcvs[:, vb * 512:(vb + 1) * 512],
                    b.a32(4096, 512), waits=[evc], cls="pst")
        prev = evs
        if vb < 15:
            evr = P.dma("sync", b.row(R_ATT, 512),
                        b.inp["bc_v"][0:1, (vb + 1) * 512:(vb + 2) * 512],
                        waits=[evm], cls="prr")
    for j in range(6):
        proj, fh = divmod(j, 2)
        bk = b.pb[6 if j % 2 == 0 else 7]
        evr = P.dma("sync", b.row(R_ATT, 512),
                    b.inp["qkvb_rows"][proj:proj + 1, fh * 512:(fh + 1) * 512],
                    waits=[prev], cls="prr")
        evm = P.mm(bk[:, :], b.onerow_f[:], b.row(R_ATT, 512), True, True,
                   waits=[evr, b.bfree[6 if j % 2 == 0 else 7]],
                   inc="sT", cost=2048)
        evc = P.op("vector", lambda e, bk=bk: e.tensor_copy(
            b.a32(4096, 512), bk[:, :]), waits=[evm], inc="sV")
        b.bfree[6 if j % 2 == 0 else 7] = evc
        evs = P.dma("sync", b.qbias_bc[:, j * 512:(j + 1) * 512],
                    b.a32(4096, 512), waits=[evc], cls="pst")
        prev = evs
    return [ev_wrow, ev_id, prev]



def emit_ln(b: Build, src):
    P = b.P
    gate = [P.snap(), P.snap()]
    for tb in range(NTB):
        ss = tb % 2
        t0 = tb * TB
        g = gate[ss]
        xb = [b.a16(ss * 4096 + c * 512, 512) for c in range(C)]
        sq2 = [b.a16(8192 + ss * 1024 + j * 512, 512) for j in range(2)]
        bank = b.pb[ss]
        bankq = b.pb[2 + ss]
        bbank = b.pb[4 + ss]
        ld = [P.dma("gpsimd", xb[c],
                    src[c * 128:(c + 1) * 128, t0:t0 + TB], waits=g,
                    cls=f"lnxb{ss}")
              for c in range(C)]
        ldall = ld[-1]
        mmq = {}
        mmv = []
        for c in range(C):
            mmv.append(P.mm(bank[0:1, :], b.ones_b[:], xb[c],
                            start=(c == 0), stop=(c == C - 1),
                            waits=[ldall] + ([b.bfree[ss]] if c == 0 else []),
                            inc="sT", cost=512))
            sq = sq2[c % 2]
            evq = P.op("scalar", lambda e, sq=sq, x=xb[c]:
                       e.activation(sq, x, AF.Square),
                       waits=[ldall, mmq.get(c % 2)] + (g if c == 0 else []),
                       inc="sA")
            mq = P.mm(bankq[0:1, :], b.ones_b[:], sq,
                      start=(c == 0), stop=(c == C - 1),
                      waits=[evq] +
                      ([b.bfree[2 + ss]] if c == 0 else []),
                      inc="sT", cost=512)
            mmq[c % 2] = mq
            mmv.append(mq)
        ev_st = mmv[-1]
        mrow = b.row(R_LN + ss * 1024)
        vrow = b.row(R_LN + ss * 1024 + 512)
        rrow = mrow
        e1 = P.op("vector", lambda e, mrow=mrow, bank=bank: e.tensor_scalar(
            mrow, bank[0:1, :], 1.0 / D, None, ALU.mult),
            waits=[ev_st] + g, inc="sV")
        P.op("vector", lambda e, mrow=mrow, vrow=vrow:
             e.tensor_mul(vrow, mrow, mrow))
        e3 = P.op("vector", lambda e, vrow=vrow, bankq=bankq:
                  e.scalar_tensor_tensor(vrow, bankq[0:1, :], 1.0 / D, vrow,
                                         ALU.mult, ALU.subtract), inc="sV")
        P.op("scalar", lambda e, vrow=vrow: e.activation(
            vrow, vrow, AF.Ln, bias=b.cp("eps", p=1)), waits=[e3])
        b.bfree[ss] = e1
        b.bfree[2 + ss] = e3
        mb = b.a32(2048 + ss * 1024, 512)
        rb = b.a32(2048 + ss * 1024 + 512, 512)
        em1 = P.mm(bbank[:, :], b.onerow_f[:], mrow, True, True,
                   waits=[e1, b.bfree[4 + ss]], inc="sT", cost=2048)
        e5 = P.op("scalar", lambda e, vrow=vrow, rrow=rrow: e.activation(
            rrow, vrow, AF.Exp, scale=b.cp("mhalf", p=1)),
            waits=[em1], inc="sA")
        eb1 = P.op("vector", lambda e, mb=mb, bbank=bbank:
                   e.tensor_copy(mb, bbank[:, :]), waits=[em1] + g, inc="sV")
        em2 = P.mm(bbank[:, :], b.onerow_f[:], rrow, True, True,
                   waits=[e5, eb1], inc="sT", cost=2048)
        eb2 = P.op("vector", lambda e, rb=rb, bbank=bbank:
                   e.tensor_copy(rb, bbank[:, :]), waits=[em2], inc="sV")
        b.bfree[4 + ss] = eb2
        last = None
        rd = {}
        for c in range(C):
            xt = b.a32((ss * 2 + (c % 2)) * 512, 512)
            xnt = b.a32(4096 + ss * 512, 512)
            evx = P.dma("sync", xt, src[c * 128:(c + 1) * 128, t0:t0 + TB],
                        waits=g + [rd.get(c % 2)], cls=f"lnxt{ss}{c % 2}")
            e6 = P.op("vector", lambda e, xnt=xnt, xt=xt, mb=mb:
                      e.tensor_sub(xnt, xt, mb), waits=[evx, eb1, eb2,
                                                        last], inc="sV")
            rd[c % 2] = e6
            e7 = P.op("vector", lambda e, xnt=xnt, rb=rb:
                      e.tensor_mul(xnt, xnt, rb), inc="sV")
            last = P.dma("sync", b.xns[c * 128:(c + 1) * 128, t0:t0 + TB],
                         xnt, waits=[e7], cls=f"lnst{ss}")
        gate[ss] = P.snap()
    return P.snap()



def emit_l0ffn(b: Build, x_src, x_dst, ncycles=3):
    P = b.P
    ip = b.inp
    W1o, W2o, W3o, W4o = 0, 8192, 12288, 13312
    G1o, G2o, G3o = 13440, 17536, 18560
    CURB = 18688
    H1o, H2o, H3o = 26880, 30976, 33024
    GCV = 34048

    entry = P.snap()
    wev = []
    for c in range(C):
        wev.append(P.dma("gpsimd", b.a16(W1o + c * 1024, 1024),
                         ip["cn_w1"][c * 128:(c + 1) * 128, :], waits=entry,
                         cls="l0w"))
        wev.append(P.dma("gpsimd", b.a16(W2o + c * 512, 512),
                         ip["cn_w2"][c * 128:(c + 1) * 128, :], cls="l0w"))
        wev.append(P.dma("gpsimd", b.a16(G1o + c * 512, 512),
                         ip["gc_w1"][c * 128:(c + 1) * 128, :], cls="l0w"))
    for c in range(4):
        wev.append(P.dma("gpsimd", b.a16(W3o + c * 256, 256),
                         ip["cn_w3"][c * 128:(c + 1) * 128, :], cls="l0w"))
        wev.append(P.dma("gpsimd", b.a16(G2o + c * 256, 256),
                         ip["gc_w2"][c * 128:(c + 1) * 128, :], cls="l0w"))
    for c in range(2):
        wev.append(P.dma("gpsimd", b.a16(W4o + c, 1),
                         ip["cn_w4"][c * 128:(c + 1) * 128, :], cls="l0w"))
        wev.append(P.dma("gpsimd", b.a16(G3o + c, 1),
                         ip["gc_w3"][c * 128:(c + 1) * 128, :], cls="l0w"))
    wready = wev[-1]

    def bias_ap(col):
        return b.bias[:, 24 + col:25 + col]

    def bias1(col, row=0):
        return b.bias[row:row + 1, 24 + col:25 + col]

    gacc = b.a32(4608, 8)
    gtmp = b.a32(4616, 8)
    prev = [None, None]
    evr = None
    for c in range(C):
        for tb in range(NTB):
            xt = b.a32((tb % 2) * 512, 512)
            evl = P.dma("sync", xt,
                        b.xns[c * 128:(c + 1) * 128, tb * TB:(tb + 1) * TB],
                        waits=entry + [prev[tb % 2]], cls=f"gfx{tb % 2}")
            tgt = gacc if (tb == 0) else gtmp
            evr = P.op("vector", lambda e, c=c, xt=xt, tgt=tgt:
                       e.reduce_sum(tgt[:, c:c + 1], xt, axis=AX.X),
                       waits=[evl], inc="sV")
            if tb:
                evr = P.op("vector", lambda e, c=c: e.tensor_add(
                    gacc[:, c:c + 1], gacc[:, c:c + 1], gtmp[:, c:c + 1]),
                    inc="sV")
            prev[tb % 2] = evr
    ev_gcv = []
    for c in range(C):
        ev_gcv.append(P.op("vector", lambda e, c=c: e.tensor_scalar(
            b.a16(GCV + c, 1), gacc[:, c:c + 1], 1.0 / S, None, ALU.mult),
            waits=[evr], inc="sV"))
    bank = b.pb[4]
    g1 = b.a16(GCV + 8, 4)
    g2 = b.a16(GCV + 12, 2)
    evm = None
    for m in range(4):
        for c in range(C):
            evm = P.mm(bank[:, m:m + 1], b.a16(G1o + c * 512 + m * 128, 128),
                       b.a16(GCV + c, 1), start=(c == 0), stop=(c == 7),
                       waits=([wready] + ev_gcv + [b.bfree[4]])
                       if (m == 0 and c == 0) else (), inc="sT", cost=64)
    ev_g1 = None
    for m in range(4):
        ev_g1 = P.op("scalar", lambda e, m=m: e.activation(
            g1[:, m:m + 1], bank[:, m:m + 1], AF.Gelu, bias=bias_ap(14 + m)),
            waits=[evm], inc="sA")
    for m in range(2):
        for c in range(4):
            evm = P.mm(bank[:, 8 + m:9 + m],
                       b.a16(G2o + c * 256 + m * 128, 128), g1[:, c:c + 1],
                       start=(c == 0), stop=(c == 3),
                       waits=[ev_g1], inc="sT", cost=64)
    ev_g2 = None
    for m in range(2):
        ev_g2 = P.op("scalar", lambda e, m=m: e.activation(
            g2[:, m:m + 1], bank[:, 8 + m:9 + m], AF.Gelu,
            bias=bias_ap(18 + m)), waits=[evm], inc="sA")
    for c in range(2):
        evm = P.mm(bank[0:1, 12:13], b.a16(G3o + c, 1), g2[:, c:c + 1],
                   start=(c == 0), stop=(c == 1), waits=[ev_g2],
                   inc="sT", cost=16)
    gf = b.row(R_FFN, 1)
    gf07 = b.row(R_FFN + 1, 1)
    a2 = b.row(R_FFN + 2, 1)
    c0 = b.row(R_FFN + 3, 1)
    ev_gf = P.op("scalar", lambda e: e.activation(
        gf, bank[0:1, 12:13], AF.Sigmoid, bias=bias1(21)),
        waits=[evm], inc="sA")
    ev_gf2 = P.op("vector", lambda e: e.tensor_scalar(
        gf07, gf, 0.7, None, ALU.mult), waits=[ev_gf], inc="sV")
    P.op("vector", lambda e: e.tensor_scalar(
        a2, bias1(22), 2.0, None, ALU.mult))
    ev_c0 = P.op("vector", lambda e: e.tensor_scalar(
        c0, bias1(22), -1.0, 0.1, ALU.mult, ALU.add), inc="sV")
    b.bfree[4] = ev_gf

    TWP = [4, 5, 6, 7, 0, 1, 2, 3]
    TWS = [1.0, 1.0, -1.0, -1.0, -1.0, -1.0, 1.0, 1.0]
    gate = [P.snap(), P.snap()]
    out_evs = []
    xf_rd, pf_rd, xr_rd = {}, {}, {}
    for r in range(ncycles):
        cur = b.xns if r % 2 == 0 else b.alts
        nxt = b.alts if r % 2 == 0 else b.xns
        final = (r == ncycles - 1)
        for tb in range(NTB):
            ss = tb % 2
            t0 = tb * TB
            g = gate[ss]
            curb = [b.a16(CURB + ss * 4096 + c * 512, 512) for c in range(C)]
            ld = [P.dma("gpsimd", curb[c],
                        cur[c * 128:(c + 1) * 128, t0:t0 + TB], waits=g,
                        cls=f"cur{ss}")
                  for c in range(C)]
            ld = [ld[-1]] * C
            h1 = [b.a16(H1o + m * 512, 512) for m in range(8)]
            h2 = [b.a16(H2o + m * 512, 512) for m in range(4)]
            h3 = [b.a16(H3o + m * 512, 512) for m in range(2)]
            ev_h1 = []
            for m in range(8):
                bk = b.pb[m % 2]
                mmv = None
                for c in range(C):
                    mmv = P.mm(bk[:, :], b.a16(W1o + c * 1024 + m * 128, 128),
                               curb[c], start=(c == 0), stop=(c == 7),
                               waits=[ld[c]] + ([b.bfree[m % 2], wready,
                                                 ev_c0] if c == 0 else []),
                               inc="sT")
                ev = P.op("scalar", lambda e, m=m, bk=bk: e.activation(
                    h1[m], bk[:, :], AF.Gelu, bias=bias_ap(m)),
                    waits=[mmv], inc="sA")
                ev_h1.append(ev)
                b.bfree[m % 2] = ev
            ev_h2 = []
            for m in range(4):
                bk = b.pb[m % 2]
                mmv = None
                for c in range(C):
                    mmv = P.mm(bk[:, :], b.a16(W2o + c * 512 + m * 128, 128),
                               h1[c], start=(c == 0), stop=(c == 7),
                               waits=[ev_h1[c]] +
                               ([b.bfree[m % 2]] if c == 0 else []),
                               inc="sT")
                ev = P.op("scalar", lambda e, m=m, bk=bk: e.activation(
                    h2[m], bk[:, :], AF.Gelu, bias=bias_ap(8 + m)),
                    waits=[mmv], inc="sA")
                ev_h2.append(ev)
                b.bfree[m % 2] = ev
            ev_h3 = []
            for m in range(2):
                bk = b.pb[m % 2]
                mmv = None
                for c in range(4):
                    mmv = P.mm(bk[:, :], b.a16(W3o + c * 256 + m * 128, 128),
                               h2[c], start=(c == 0), stop=(c == 3),
                               waits=[ev_h2[c]] +
                               ([b.bfree[m % 2]] if c == 0 else []),
                               inc="sT")
                ev = P.op("scalar", lambda e, m=m, bk=bk: e.activation(
                    h3[m], bk[:, :], AF.Gelu, bias=bias_ap(12 + m)),
                    waits=[mmv], inc="sA")
                ev_h3.append(ev)
                b.bfree[m % 2] = ev
            bk = b.pb[2]
            mmv = None
            for c in range(2):
                mmv = P.mm(bk[0:1, :], b.a16(W4o + c, 1), h3[c],
                           start=(c == 0), stop=(c == 1),
                           waits=[ev_h3[c]] +
                           ([b.bfree[2]] if c == 0 else []), inc="sT")
            tf = b.row(R_FFN + 4 + ss * 512)
            ev_tf = P.op("scalar", lambda e, tf=tf, bk=bk: e.activation(
                tf, bk[0:1, :], AF.Sigmoid, bias=bias1(20)),
                waits=[mmv], inc="sA")
            b.bfree[2] = ev_tf
            if DEBUG_L0F and r == 0:
                P.dma("sync", b.dbg[0:1, t0:t0 + TB], tf, waits=[ev_tf],
                      cls="dbg")
            ev_cb = P.op("vector", lambda e, tf=tf: e.tensor_scalar(
                tf, tf, 0.3, gf07, ALU.mult, ALU.add),
                waits=[ev_tf, ev_gf2, ev_c0], inc="sV")
            if DEBUG_L0F and r == 0:
                P.dma("sync", b.dbg[1:2, t0:t0 + TB], tf, waits=[ev_cb],
                      cls="dbg")
            ev_cp = P.op("vector", lambda e, tf=tf: e.tensor_scalar(
                tf, tf, a2, c0, ALU.mult, ALU.add), waits=[ev_cb], inc="sV")
            if DEBUG_L0F and r == 0:
                P.dma("sync", b.dbg[2:3, t0:t0 + TB], tf, waits=[ev_cp],
                      cls="dbg")
            bk3 = b.pb[3]
            coup = b.a32(2048 + ss * 512, 512)
            evbm = P.mm(bk3[:, :], b.onerow_f[:], tf, True, True,
                        waits=[ev_cp, b.bfree[3]], inc="sT", cost=2048)
            ev_cpb = P.op("vector", lambda e, coup=coup, bk3=bk3:
                          e.tensor_copy(coup, bk3[:, :]),
                          waits=[evbm] + g, inc="sV")
            b.bfree[3] = ev_cpb
            for c in range(C):
                pc_, s_ = TWP[c], TWS[c]
                xf = b.a32((ss * 2 + (c % 2)) * 512, 512)
                pf = b.a32(3072 + (c % 2) * 512, 512)
                evx = P.dma("sync", xf,
                            cur[c * 128:(c + 1) * 128, t0:t0 + TB],
                            waits=g + [xf_rd.get((ss, c % 2))],
                            cls=f"txf{ss}{c % 2}")
                evp = P.dma("sync", pf,
                            cur[pc_ * 128:(pc_ + 1) * 128, t0:t0 + TB],
                            waits=g + [pf_rd.get(c % 2)], cls=f"tpf{c % 2}")
                ev1 = P.op("vector", lambda e, pf=pf, coup=coup:
                           e.tensor_mul(pf, pf, coup),
                           waits=[evp, ev_cpb], inc="sV")
                pf_rd[c % 2] = ev1
                if not final:
                    ev2 = P.op("vector", lambda e, xf=xf, pf=pf, s_=s_:
                               e.scalar_tensor_tensor(
                                   xf, pf, s_, xf, ALU.mult, ALU.add),
                               waits=[evx, ev1], inc="sV")
                    evst = P.dma("sync",
                                 nxt[c * 128:(c + 1) * 128, t0:t0 + TB],
                                 xf, waits=[ev2], cls=f"tst{ss}{c % 2}")
                    xf_rd[(ss, c % 2)] = evst
                else:
                    xr = b.a32(4096 + (c % 2) * 512, 512)
                    evxr = P.dma("sync", xr,
                                 x_src[c * 128:(c + 1) * 128, t0:t0 + TB],
                                 waits=g + [xr_rd.get(c % 2)],
                                 cls=f"txr{c % 2}")
                    ev2 = P.op("vector", lambda e, xf=xf, xr=xr:
                               e.scalar_tensor_tensor(
                                   xf, xf, 0.5, xr, ALU.mult, ALU.add),
                               waits=[evx, evxr], inc="sV")
                    xr_rd[c % 2] = ev2
                    ev3 = P.op("vector", lambda e, xf=xf, pf=pf, s_=s_:
                               e.scalar_tensor_tensor(
                                   xf, pf, 0.5 * s_, xf, ALU.mult, ALU.add),
                               waits=[ev1, ev2], inc="sV")
                    evst = P.dma("sync",
                                 x_dst[c * 128:(c + 1) * 128, t0:t0 + TB],
                                 xf, waits=[ev3], cls=f"tst{ss}{c % 2}")
                    xf_rd[(ss, c % 2)] = evst
                if final:
                    out_evs.append(evst)
            gate[ss] = P.snap()
    return out_evs



def emit_l0attn(b: Build, x_src, x_dst):
    P = b.P
    ip = b.inp
    entry = P.snap()

    WQK = 0
    WVB = 16384
    XNB = 24576
    QT = 36864
    VT_ = 37888
    wev = []
    for c in range(C):
        wev.append(P.dma("gpsimd", b.a16(WQK + c * 2048, 2048),
                         ip["w_cT"][c * 128:(c + 1) * 128, 0:2048],
                         waits=entry, cls="l0aw"))
    wready = wev[-1]

    gate = [P.snap(), P.snap()]
    qt_st, vt_st = {}, {}
    for tb in range(NTB):
        ss = tb % 2
        t0 = tb * TB
        g = gate[ss]
        xnb = [b.a16(XNB + ss * 4096 + c * 512, 512) for c in range(C)]
        ld = [P.dma("gpsimd", xnb[c],
                    b.xns[c * 128:(c + 1) * 128, t0:t0 + TB],
                    waits=g + ([wready] if tb < 2 else []),
                    cls=f"axnb{ss}")
              for c in range(C)]
        ld = [ld[-1]] * C
        for oc in range(16):
            bk = b.pb[oc % 2]
            qt = b.a16(QT + (oc % 2) * 512, 512)
            mmv = None
            for c in range(C):
                mmv = P.mm(bk[:, :],
                           b.a16(WQK + c * 2048 + oc * 128, 128), xnb[c],
                           start=(c == 0), stop=(c == 7),
                           waits=[ld[c]] +
                           ([b.bfree[oc % 2]] if c == 0 else []), inc="sT")
            ev = P.op("scalar", lambda e, qt=qt, bk=bk, oc=oc: e.activation(
                qt, bk[:, :], LEAKY[0], bias=b.bias[:, oc:oc + 1],
                **({"alpha": 0.01} if LEAKY[0] == AF.Prelu else {})),
                waits=[mmv, qt_st.get(oc % 2)] + g, inc="sA")
            b.bfree[oc % 2] = ev
            qt_st[oc % 2] = P.dma("sync",
                                  b.qks[oc * 128:(oc + 1) * 128, t0:t0 + TB],
                                  qt, waits=[ev], cls=f"aqst{oc % 2}")
        gate[ss] = P.snap()

    gate = [P.snap(), P.snap()]
    xnb_gate = Ev("sT", P.cnt["sT"]) if P.cnt["sT"] else None
    vt32_rd = {}
    for vb in range(16):
        ss = vb % 2
        g = gate[ss]
        wv = [b.a16(WVB + ss * 4096 + c * 512, 512) for c in range(C)]
        wld = [P.dma("gpsimd", wv[c],
                     ip["w_cT"][c * 128:(c + 1) * 128,
                                2048 + vb * 512:2048 + (vb + 1) * 512],
                     waits=g, cls=f"awv{ss}") for c in range(C)]
        wld = [wld[-1]] * C
        bcv = b.a32(2048 + ss * 512, 512)
        evb = P.dma("sync", bcv, b.bcvs[:, vb * 512:(vb + 1) * 512], waits=g,
                    cls=f"abcv{ss}")
        for tb in range(NTB):
            t0 = tb * TB
            xnb = [b.a16(XNB + c * 512, 512) for c in range(C)]
            ld = [P.dma("gpsimd", xnb[c],
                        b.xns[c * 128:(c + 1) * 128, t0:t0 + TB],
                        waits=g + [xnb_gate], cls="a2xnb")
                  for c in range(C)]
            ld = [ld[-1]] * C
            for tt in range(4):
                bk = b.pb[2 + (tt % 2)]
                vt32 = b.a32(3072 + (tt % 2) * 512, 512)
                vt = b.a16(VT_ + (tt % 2) * 512, 512)
                mmv = None
                for c in range(C):
                    mmv = P.mm(bk[:, :], xnb[c][:, tt * 128:(tt + 1) * 128],
                               wv[c],
                               start=(c == 0), stop=(c == 7),
                               waits=[ld[c], wld[c]] +
                               ([b.bfree[2 + (tt % 2)]] if c == 0 else []),
                               inc="sT")
                e1 = P.op("vector", lambda e, vt32=vt32, bk=bk, bcv=bcv:
                          e.tensor_add(vt32, bk[:, :], bcv),
                          waits=[mmv, evb, vt32_rd.get(tt % 2)] + g,
                          inc="sV")
                b.bfree[2 + (tt % 2)] = e1
                vt32_rd[tt % 2] = None
                e2 = P.op("scalar", lambda e, vt=vt, vt32=vt32:
                          e.activation(vt, vt32, LEAKY[0],
                                       **({"alpha": 0.01}
                                          if LEAKY[0] == AF.Prelu else {})),
                          waits=[e1, vt_st.get(tt % 2)] + g, inc="sA")
                vt_st[tt % 2] = P.dma(
                    "sync", b.vs[t0 + tt * 128:t0 + (tt + 1) * 128,
                                 vb * 512:(vb + 1) * 512], vt, waits=[e2],
                    cls=f"avst{tt % 2}")
                vt32_rd[tt % 2] = e2
            xnb_gate = Ev("sT", P.cnt["sT"])
        gate[ss] = P.snap()

    ATT = 0
    VT = 16384
    KT = 32768
    QTB = 40960
    OUT = 41984
    pgate = P.snap()
    lastp = []
    for qb in range(NTB):
        q0 = qb * TB
        for h in range(H):
            hh = h % 2
            kbuf = b.A16[0:64, KT + hh * 4096:KT + hh * 4096 + 4096]
            qbuf = b.A16[0:64, QTB + hh * 512:QTB + hh * 512 + 512]
            evk = P.dma("sync", kbuf,
                        b.qks[D + h * 64:D + (h + 1) * 64, :], waits=pgate,
                        cls=f"akq{h % 2}")
            evq = P.dma("sync", qbuf,
                        b.qks[h * 64:(h + 1) * 64, q0:q0 + TB], waits=pgate,
                        cls=f"akq{h % 2}")
            vld = [P.dma("gpsimd",
                         b.a16(VT + kc * 512, 512),
                         b.vs[kc * 128:(kc + 1) * 128,
                              h * 512:(h + 1) * 512], waits=pgate,
                         cls="av0")
                   for kc in range(32)]
            vld = [vld[-1]] * 32
            ev_at = []
            for kc in range(32):
                bk = b.pb[kc % 2]
                at = b.a16(ATT + kc * 512, 512)
                mmv = P.mm(bk[:, :], kbuf[:, kc * 128:(kc + 1) * 128], qbuf,
                           True, True,
                           waits=[evk, evq, b.bfree[kc % 2]], inc="sT")
                ev = P.op("scalar", lambda e, at=at, bk=bk: e.activation(
                    at, bk[:, :], AF.Exp, scale=b.cp("oct")),
                    waits=[mmv] + pgate, inc="sA")
                ev_at.append(ev)
                b.bfree[kc % 2] = ev
            zb = b.pb[2]
            mmz = None
            for kc in range(32):
                mmz = P.mm(zb[0:1, :], b.ones_b[:], b.a16(ATT + kc * 512, 512),
                           start=(kc == 0), stop=(kc == 31),
                           waits=[ev_at[kc]] +
                           ([b.bfree[2]] if kc == 0 else []),
                           inc="sT", cost=512)
            zrow = b.row(R_ATT)
            evz = P.op("vector", lambda e, zrow=zrow, zb=zb: e.reciprocal(
                zrow, zb[0:1, :]), waits=[mmz], inc="sV")
            b.bfree[2] = evz
            zbank = b.pb[3]
            evzm = P.mm(zbank[:, :], b.onerow_f[:], zrow, True, True,
                        waits=[evz, b.bfree[3]], inc="sT", cost=2048)
            zinv = b.a32(2048 + (h % 2) * 512, 512)
            evzi = P.op("vector", lambda e, zinv=zinv, zbank=zbank:
                        e.tensor_copy(zinv, zbank[:, :]),
                        waits=[evzm] + pgate, inc="sV")
            b.bfree[3] = evzi
            for vd in range(4):
                bk = b.pb[4 + vd % 2]
                mmv = None
                for kc in range(32):
                    mmv = P.mm(bk[:, :],
                               b.a16(VT + kc * 512 + vd * 128, 128),
                               b.a16(ATT + kc * 512, 512),
                               start=(kc == 0), stop=(kc == 31),
                               waits=[vld[kc], ev_at[kc]] +
                               ([b.bfree[4 + vd % 2]] if kc == 0 else []),
                               inc="sT")
                ev = P.op("vector", lambda e, h=h, vd=vd, bk=bk, zinv=zinv:
                          e.tensor_mul(
                              b.a16(OUT + (h * 4 + vd) * 512, 512),
                              bk[:, :], zinv),
                          waits=[mmv, evzi] + (pgate if h < 2 else []),
                          inc="sV")
                b.bfree[4 + vd % 2] = ev
            pgate = P.snap()
        out_ready = pgate
        evps = {}
        half_mm_done = None
        for half in range(2):
            wq_ld = []
            for blk in range(2):
                wo_blk = b.a16(blk * 16384, 16384)
                wq_ld.append(P.dma(
                    "gpsimd", wo_blk.rearrange("p (cc d) -> p cc d", cc=16),
                    ip["w_oT"][(half * 2 + blk) * 2048:
                               (half * 2 + blk + 1) * 2048, :].rearrange(
                        "(cc p) d -> p cc d", p=128),
                    waits=out_ready + [half_mm_done], cls="awo"))
            wlast = wq_ld[-1]
            for dt in range(8):
                bk = b.pb[dt]
                mmv = None
                for vc in range(32):
                    blk, cc = divmod(vc, 16)
                    mmv = P.mm(bk[:, :],
                               b.a16(blk * 16384 + cc * 1024 + dt * 128,
                                     128),
                               b.a16(OUT + (half * 32 + vc) * 512, 512),
                               start=(half == 0 and vc == 0),
                               stop=(half == 1 and vc == 31),
                               waits=[wlast] +
                               ([b.bfree[dt]] if (half == 0 and vc == 0)
                                else []),
                               inc="sT")
                evps[(half, dt)] = mmv
            half_mm_done = Ev("sT", P.cnt["sT"])
        prev_ev = {}
        for dt in range(8):
            bk = b.pb[dt]
            pj = b.a32(3072 + (dt % 2) * 512, 512)
            evp = P.op("scalar", lambda e, pj=pj, bk=bk, dt=dt: e.activation(
                pj, bk[:, :], LEAKY[0], bias=b.bias[:, 16 + dt:17 + dt],
                **({"alpha": 0.01} if LEAKY[0] == AF.Prelu else {})),
                waits=[evps[(1, dt)], prev_ev.get(dt % 2)], inc="sA")
            b.bfree[dt] = evp
            xt = b.a32((dt % 2) * 512, 512)
            xnt = b.a32(1024 + (dt % 2) * 512, 512)
            evx = P.dma("sync", xt, x_src[dt * 128:(dt + 1) * 128,
                                          q0:q0 + TB],
                        waits=out_ready + [prev_ev.get(dt % 2)],
                        cls=f"apx{dt % 2}")
            evxn = P.dma("sync", xnt, b.xns[dt * 128:(dt + 1) * 128,
                                            q0:q0 + TB],
                         waits=out_ready + [prev_ev.get(dt % 2)],
                         cls=f"apx{dt % 2}")
            e1 = P.op("vector", lambda e, xt=xt, xnt=xnt: e.tensor_add(
                xt, xt, xnt), waits=[evx, evxn], inc="sV")
            e2 = P.op("vector", lambda e, xt=xt, pj=pj: e.tensor_add(
                xt, xt, pj), waits=[evp, e1], inc="sV")
            evst = P.dma("sync", x_dst[dt * 128:(dt + 1) * 128, q0:q0 + TB],
                         xt, waits=[e2], cls=f"past{dt % 2}")
            prev_ev[dt % 2] = evst
            lastp.append(evst)
        pgate = P.snap()
    return lastp



def emit_l1attn(b: Build, x_src, x_dst):
    P = b.P
    ip = b.inp
    entry = P.snap()

    WQ, WK, WV_ = 0, 8192, 16384
    QTM, VTM = 40960, 41984
    for c in range(C):
        P.dma("gpsimd", b.a16(WQ + c * 1024, 1024),
              ip["wq"][c * 128:(c + 1) * 128, :], waits=entry, cls="l1w")
        P.dma("gpsimd", b.a16(WK + c * 1024, 1024),
              ip["wk"][c * 128:(c + 1) * 128, :], cls="l1w")
        wready = P.dma("gpsimd", b.a16(WV_ + c * 1024, 1024),
                       ip["wv"][c * 128:(c + 1) * 128, :], cls="l1w")
    wb = b.a32(4608, 512)
    evm = P.mm(b.pb[6][:, :], b.onerow_f[:], b.wrow[:], True, True,
               waits=entry + [b.bfree[6]], inc="sT", cost=2048)
    ev_wb = P.op("vector", lambda e: e.tensor_copy(wb, b.pb[6][:, :]),
                 waits=[evm] + entry, inc="sV")
    b.bfree[6] = ev_wb

    qtm = b.a32(0, 1024)
    T1 = b.a32(2048, 512)
    T2 = b.a32(2560, 512)
    ang = b.a32(3072, 512)
    sin_t = b.a32(3584, 512)
    cos_t = b.a32(4096, 512)
    C1 = 6.28125
    C2 = 0.0019353071795864769

    gate = [P.snap(), P.snap()]
    bb_rd, vtm_st = {}, {}
    for tb in range(NTB):
        ss = tb % 2
        t0 = tb * TB
        g = gate[ss]
        QFM = 24576 + ss * 8192
        KFM = 28672 + ss * 8192
        xnb = [b.a16(43008 + ss * 4096 + c * 512, 512) for c in range(C)]
        ld = [P.dma("gpsimd", xnb[c],
                    b.xns[c * 128:(c + 1) * 128, t0:t0 + TB],
                    waits=g + ([wready] if tb < 2 else []),
                    cls=f"l1xnb{ss}")
              for c in range(C)]
        ld = [ld[-1]] * C
        store_evs = []
        for tt in range(4):
            lh = [xnb[c][:, tt * 128:(tt + 1) * 128] for c in range(C)]
            evt = P.op("vector", lambda e, v=float(t0 + tt * 128):
                       e.tensor_scalar(b.tcol[:], b.pc[:, 0:1], 1.0, v,
                                       ALU.mult, ALU.add),
                       waits=g + [ev_wb], inc="sV")
            P.op("vector", lambda e: e.tensor_scalar(
                ang, wb, b.tcol[:], None, ALU.mult))

            def reduce_to(dst_row_sin, src, scr):
                P.op("vector", lambda e, src=src: e.tensor_scalar(
                    b.SIF[:, 0:512], src, 1.0 / (2 * math.pi), 0.5,
                    ALU.mult, ALU.add))
                P.op("vector", lambda e: e.tensor_copy(
                    b.SI[:, 0:512], b.SIF[:, 0:512]))
                P.op("vector", lambda e: e.tensor_copy(
                    b.SIF[:, 0:512], b.SI[:, 0:512]))
                P.op("vector", lambda e, src=src, scr=scr:
                     e.scalar_tensor_tensor(
                         scr, b.SIF[:, 0:512], -C1, src, ALU.mult, ALU.add))
                P.op("vector", lambda e, scr=scr:
                     e.scalar_tensor_tensor(
                         scr, b.SIF[:, 0:512], -C2, scr, ALU.mult,
                         ALU.add))
                ev = P.op("vector", lambda e, scr=scr: e.tensor_scalar(
                    scr, scr, math.pi, -math.pi, ALU.min, ALU.max),
                    inc="sV")
                return P.op("scalar", lambda e, dst=dst_row_sin, scr=scr:
                            e.activation(dst, scr, AF.Sin),
                            waits=[ev], inc="sA")
            P.op("vector", lambda e: e.tensor_scalar(
                T2, ang, math.pi / 2.0, None, ALU.add))
            ev_cos = reduce_to(cos_t, T2, T1)
            ev_sin = reduce_to(sin_t, ang, T2)

            for isq, (Wo_, FM) in enumerate(((WQ, QFM), (WK, KFM))):
                for fh in range(2):
                    bk = b.pb[isq]
                    bb = b.a32(1024 + fh * 512, 512)
                    evbb = P.dma("sync", bb,
                                 b.qbias_bc[:, isq * 1024 + fh * 512:
                                            isq * 1024 + (fh + 1) * 512],
                                 waits=g + [bb_rd.get(fh)],
                                 cls=f"l1bb{fh}")
                    mmv = None
                    for c in range(C):
                        mmv = P.mm(bk[:, :], lh[c],
                                   b.a16(Wo_ + c * 1024 + fh * 512, 512),
                                   start=(c == 0), stop=(c == 7),
                                   waits=[ld[c]] +
                                   ([b.bfree[isq]] if c == 0 else []),
                                   inc="sT")
                    ev = P.op("vector", lambda e, fh=fh, bk=bk, bb=bb:
                              e.tensor_add(
                                  qtm[:, fh * 512:(fh + 1) * 512],
                                  bk[:, :], bb),
                              waits=[mmv, evbb, ev_sin] + g, inc="sV")
                    bb_rd[fh] = ev
                    b.bfree[isq] = ev
                x1 = qtm.rearrange("p (i two) -> p i two", two=2)[:, :, 0]
                x2 = qtm.rearrange("p (i two) -> p i two", two=2)[:, :, 1]
                P.op("vector", lambda e, x1=x1: e.tensor_mul(T1, x1, cos_t),
                     waits=[ev_cos, ev_sin])
                P.op("vector", lambda e, x1=x1: e.tensor_mul(T2, x1, sin_t))
                P.op("vector", lambda e, x2=x2: e.tensor_mul(
                    b.SIF[:, 0:512], x2, sin_t))
                P.op("vector", lambda e, x1=x1: e.tensor_sub(
                    x1, T1, b.SIF[:, 0:512]))
                P.op("vector", lambda e, x2=x2: e.tensor_mul(T1, x2, cos_t))
                P.op("vector", lambda e, x2=x2: e.tensor_add(x2, T2, T1))
                evb16 = P.op("vector", lambda e: e.tensor_copy(
                    b.a16(QTM, 1024), qtm), inc="sV")
                for c in range(C):
                    bkt = b.pb[2 + c % 2][:].bitcast(BF16)
                    evtr = P.tr(bkt[:, 0:128],
                                b.A16[:, QTM + c * 128:QTM + (c + 1) * 128],
                                b.ident_b[:],
                                waits=[evb16, b.bfree[2 + c % 2]], inc="sT")
                    ev2 = P.op("vector", lambda e, FM=FM, c=c, tt=tt,
                               bkt=bkt: e.tensor_copy(
                                   b.A16[:, FM + c * 512 + tt * 128:
                                         FM + c * 512 + tt * 128 + 128],
                                   bkt[:, 0:128]),
                               waits=[evtr] + g, inc="sV")
                    b.bfree[2 + c % 2] = ev2
            for fh in range(2):
                bk = b.pb[4 + fh]
                bb = b.a32(1024 + fh * 512, 512)
                evbb = P.dma("sync", bb,
                             b.qbias_bc[:, 2048 + fh * 512:2048 +
                                        (fh + 1) * 512],
                             waits=g + [bb_rd.get(fh)], cls=f"l1bb{fh}")
                mmv = None
                for c in range(C):
                    mmv = P.mm(bk[:, :], lh[c],
                               b.a16(WV_ + c * 1024 + fh * 512, 512),
                               start=(c == 0), stop=(c == 7),
                               waits=[ld[c]] +
                               ([b.bfree[4 + fh]] if c == 0 else []),
                               inc="sT")
                vt = b.a16(VTM + fh * 512, 512)
                ev = P.op("vector", lambda e, vt=vt, bk=bk, bb=bb:
                          e.tensor_add(vt, bk[:, :], bb),
                          waits=[mmv, evbb, vtm_st.get(fh)] + g, inc="sV")
                bb_rd[fh] = ev
                b.bfree[4 + fh] = ev
                sev = P.dma("sync", b.vs[t0 + tt * 128:t0 + (tt + 1) * 128,
                                         fh * 512:(fh + 1) * 512], vt,
                            waits=[ev], cls=f"l1vst{fh}")
                vtm_st[fh] = sev
                store_evs.append(sev)
        evq_all = Ev("sV", P.cnt["sV"])
        for c in range(C):
            P.dma("sync", b.qks[c * 128:(c + 1) * 128, t0:t0 + TB],
                  b.a16(QFM + c * 512, 512), waits=[evq_all],
                  cls=f"l1qst{ss}")
            P.dma("sync", b.qks[D + c * 128:D + (c + 1) * 128, t0:t0 + TB],
                  b.a16(KFM + c * 512, 512), waits=[evq_all],
                  cls=f"l1qst{ss}")
        gate[ss] = P.snap()

    ATT, VT, KT, QT2, OUT = 0, 32768, 65536, 73728, 74752
    vgate = P.snap()
    vld = [P.dma("gpsimd", b.a16(VT + kc * 1024, 1024),
                 b.vs[kc * 128:(kc + 1) * 128, 0:1024], waits=vgate,
                 cls="l1v")
           for kc in range(32)]
    vld = [vld[-1]] * 32
    pgate = P.snap()
    lastp = []
    for qb in range(NTB):
        q0 = qb * TB
        for hp in range(H // 2):
            evzi2 = []
            for hi in range(2):
                h = hp * 2 + hi
                hh = h % 2
                kbuf = b.A16[0:64, KT + hh * 4096:KT + (hh + 1) * 4096]
                qbuf = b.A16[0:64, QT2 + hh * 512:QT2 + hh * 512 + 512]
                evk = P.dma("sync", kbuf,
                            b.qks[D + h * 64:D + (h + 1) * 64, :],
                            waits=pgate, cls=f"l1kq{h % 2}")
                evq = P.dma("sync", qbuf,
                            b.qks[h * 64:(h + 1) * 64, q0:q0 + TB],
                            waits=pgate, cls=f"l1kq{h % 2}")
                ev_at = []
                for kc in range(32):
                    bk = b.pb[kc % 2]
                    at = b.a16(ATT + hi * 16384 + kc * 512, 512)
                    mmv = P.mm(bk[:, :], kbuf[:, kc * 128:(kc + 1) * 128],
                               qbuf, True, True,
                               waits=[evk, evq, b.bfree[kc % 2]], inc="sT")
                    ev = P.op("scalar", lambda e, at=at, bk=bk:
                              e.activation(at, bk[:, :], AF.Exp,
                                           scale=b.cp("oct")),
                              waits=[mmv] + pgate, inc="sA")
                    ev_at.append(ev)
                    b.bfree[kc % 2] = ev
                zb = b.pb[2]
                mmz = None
                for kc in range(32):
                    mmz = P.mm(zb[0:1, :], b.ones_b[:],
                               b.a16(ATT + hi * 16384 + kc * 512, 512),
                               start=(kc == 0), stop=(kc == 31),
                               waits=[ev_at[kc]] +
                               ([b.bfree[2]] if kc == 0 else []),
                               inc="sT", cost=512)
                zrow = b.row(R_ATT)
                evz = P.op("vector", lambda e, zrow=zrow, zb=zb:
                           e.reciprocal(zrow, zb[0:1, :]),
                           waits=[mmz], inc="sV")
                b.bfree[2] = evz
                zbank = b.pb[3]
                evzm = P.mm(zbank[hi * 64:(hi + 1) * 64, :],
                            b.onerow_f[0:1, 0:64], zrow,
                            start=True, stop=True,
                            waits=[evz] +
                            ([b.bfree[3]] if hi == 0 else []),
                            inc="sT", cost=2048,
                            tile_position=(0, hi * 64))
                evzi2.append(evzm)
            zinv = b.a32(2048 + (hp % 2) * 512, 512)
            evzi = P.op("vector", lambda e, zinv=zinv: e.tensor_copy(
                zinv, b.pb[3][:, :]), waits=evzi2 + pgate, inc="sV")
            b.bfree[3] = evzi
            bkA = b.pb[4 + 2 * (hp % 2)]
            bkB = b.pb[5 + 2 * (hp % 2)]
            mmv = [None, None]
            for kc in range(32):
                for hi in range(2):
                    h = hp * 2 + hi
                    bk_ = bkA if hi == 0 else bkB
                    mmv[hi] = P.mm(
                        bk_[hi * 64:(hi + 1) * 64, :],
                        b.a16(VT + kc * 1024 + h * 64, 64),
                        b.a16(ATT + hi * 16384 + kc * 512, 512),
                        start=(kc == 0), stop=(kc == 31),
                        waits=[vld[kc], ev_at[kc]] +
                        ([b.bfree[4 + 2 * (hp % 2)],
                          b.bfree[5 + 2 * (hp % 2)]] if kc == 0 else []),
                        inc="sT", cost=512, tile_position=(0, hi * 64))
            ev0 = P.op("vector", lambda e, hp=hp, bkA=bkA, zinv=zinv:
                       e.tensor_mul(b.A16[0:64, OUT + hp * 512:
                                          OUT + hp * 512 + 512],
                                    bkA[0:64, :], zinv[0:64, :]),
                       waits=mmv + [evzi] + pgate, inc="sV")
            ev = P.op("vector", lambda e, hp=hp, bkB=bkB, zinv=zinv:
                      e.tensor_mul(b.A16[64:128, OUT + hp * 512:
                                         OUT + hp * 512 + 512],
                                   bkB[64:128, :], zinv[64:128, :]),
                      inc="sV")
            b.bfree[4 + 2 * (hp % 2)] = ev
            b.bfree[5 + 2 * (hp % 2)] = ev
            pgate = P.snap()
        out_ready = pgate
        w0ld = [P.dma("gpsimd", b.a16(ATT + c * 1024, 1024),
                      ip["w0"][c * 128:(c + 1) * 128, :],
                      waits=out_ready, cls="l1w0") for c in range(C)]
        w0ld = [w0ld[-1]] * C
        prev_ev = {}
        for dt in range(8):
            bk = b.pb[6 + dt % 2]
            mmv = None
            for c in range(C):
                mmv = P.mm(bk[:, :],
                           b.a16(ATT + c * 1024 + dt * 128, 128),
                           b.a16(OUT + c * 512, 512),
                           start=(c == 0), stop=(c == 7),
                           waits=[w0ld[c]] +
                           ([b.bfree[6 + dt % 2]] if c == 0 else []),
                           inc="sT")
            pj = b.a32(3072 + (dt % 2) * 512, 512)
            evp = P.op("vector", lambda e, pj=pj, bk=bk, dt=dt:
                       e.tensor_scalar(pj, bk[:, :],
                                       b.bias[:, 72 + dt:73 + dt], None,
                                       ALU.add),
                       waits=[mmv, prev_ev.get(dt % 2)] + out_ready,
                       inc="sV")
            b.bfree[6 + dt % 2] = evp
            xt = b.a32((dt % 2) * 512, 512)
            evx = P.dma("sync", xt,
                        x_src[dt * 128:(dt + 1) * 128, q0:q0 + TB],
                        waits=out_ready + [prev_ev.get(dt % 2)],
                        cls=f"l1px{dt % 2}")
            e2 = P.op("vector", lambda e, xt=xt, pj=pj:
                      e.tensor_add(xt, xt, pj), waits=[evx, evp], inc="sV")
            evst = P.dma("sync", x_dst[dt * 128:(dt + 1) * 128, q0:q0 + TB],
                         xt, waits=[e2], cls=f"l1st{dt % 2}")
            prev_ev[dt % 2] = evst
            lastp.append(evst)
        pgate = P.snap()
    return lastp



def emit_l1ffn(b: Build, x_src, x_dst):
    P = b.P
    ip = b.inp
    entry = P.snap()
    W1o = 0
    W2o = 32768
    H1o = 65536
    XNB = 81920

    ggate = [entry, entry]
    prev_tr = None
    prev_mm = {}
    ct_st = {}
    for tb in range(NTB):
        ss = tb % 2
        t0 = tb * TB
        g = ggate[ss]
        for tt in range(4):
            t00 = t0 + tt * 128
            xl = []
            for c in range(C):
                xt = b.a32(ss * 2048 + c * 256 + (tt % 2) * 128, 128)
                xl.append((xt, P.dma("sync", xt,
                                     b.xns[c * 128:(c + 1) * 128,
                                           t00:t00 + 128],
                                     waits=g + [prev_mm.get((ss, tt % 2))],
                                     cls=f"rxl{ss}{tt % 2}")))
            bk = b.pb[ss]
            mmv = None
            xlall = xl[-1][1]
            for c in range(C):
                mmv = P.mm(bk[:, 0:8], xl[c][0],
                           b.wr_t[:, c * 8:(c + 1) * 8],
                           start=(c == 0), stop=(c == 7),
                           waits=[xlall] +
                           ([b.bfree[ss]] if c == 0 else []),
                           inc="sT", cost=64)
            prev_mm[(ss, tt % 2)] = mmv
            lt = b.a32(4096 + ss * 512, 8)
            mx = b.a32(4096 + ss * 512 + 8, 1)
            sm = b.a32(4096 + ss * 512 + 9, 1)
            gtv = b.a32(4096 + ss * 512 + 10, 1)
            tk = b.a32(4096 + ss * 512 + 11, 1)
            ep = b.a32(4096 + ss * 512 + 16, 8)
            eq = b.a32(4096 + ss * 512 + 24, 8)
            ct = b.a32(4096 + ss * 512 + 32, 8)
            e0 = P.op("vector", lambda e, lt=lt, bk=bk: e.tensor_add(
                lt, bk[:, 0:8], b.br_b[:]),
                waits=[mmv, prev_tr] + g, inc="sV")
            b.bfree[ss] = e0
            P.op("vector", lambda e, lt=lt, mx=mx: e.reduce_max(
                mx, lt, axis=AX.X))
            e1 = P.op("vector", lambda e, lt=lt, mx=mx, ep=ep:
                      e.tensor_scalar(ep, lt, mx, None, ALU.subtract),
                      inc="sV")
            e2 = P.op("scalar", lambda e, ep=ep: e.activation(
                ep, ep, AF.Exp), waits=[e1], inc="sA")
            e3 = P.op("vector", lambda e, ep=ep, sm=sm: e.reduce_sum(
                sm, ep, axis=AX.X), waits=[e2], inc="sV")
            P.op("vector", lambda e, sm=sm, gtv=gtv: e.reciprocal(gtv, sm))
            P.op("vector", lambda e, lt=lt, mx=mx, eq=eq:
                 e.tensor_scalar(eq, lt, mx, None, ALU.is_ge))
            P.op("vector", lambda e, tk=tk: e.memset(tk, 0.0))
            for ee in range(E):
                P.op("vector", lambda e, tk=tk, ct=ct, ee=ee:
                     e.tensor_scalar(ct[:, ee:ee + 1], tk, -1.0, 1.0,
                                     ALU.mult, ALU.add))
                P.op("vector", lambda e, eq=eq, ct=ct, ee=ee:
                     e.tensor_mul(ct[:, ee:ee + 1], ct[:, ee:ee + 1],
                                  eq[:, ee:ee + 1]))
                P.op("vector", lambda e, tk=tk, ct=ct, ee=ee:
                     e.tensor_add(tk, tk, ct[:, ee:ee + 1]))
                P.op("vector", lambda e, ct=ct, gtv=gtv, ee=ee:
                     e.tensor_mul(ct[:, ee:ee + 1], ct[:, ee:ee + 1], gtv))
            e5 = P.op("vector", lambda e: e.engine_nop(), inc="sV")
            bkt = b.pb[2 + ss]
            evtr = P.tr(bkt[0:8, 0:128], ct, b.ident_f[:],
                        waits=[e5, b.bfree[2 + ss]], inc="sT")
            prev_tr = evtr
            evc = P.op("vector", lambda e, bkt=bkt: e.tensor_copy(
                b.c_t[:, :], bkt[0:8, 0:128]),
                waits=[evtr, ct_st[0] if ct_st else None] + g, inc="sV")
            b.bfree[2 + ss] = evc
            ct_st[0] = P.dma("sync", b.c_dram[:, t00:t00 + 128], b.c_t[:, :],
                             waits=[evc], cls="rct")
        ggate[ss] = P.snap()
    router_done = P.snap()

    ygate = [router_done, router_done]
    evst = None
    yt_st, ya_rd = {}, {}
    xnb_pe_gate = [None]
    for e_i in range(E):
        eg = P.snap()
        w1ld = [P.dma("gpsimd", b.a16(W1o + c * 4096, 4096),
                      ip["w1e"][e_i, c * 128:(c + 1) * 128, :], waits=eg,
                      cls="mw1")
                for c in range(C)]
        w1ld = [w1ld[-1]] * C
        w2ld = [P.dma("gpsimd",
                      b.a16(W2o + blk * 4096, 4096).rearrange(
                          "p (cc d) -> p cc d", cc=4),
                      ip["w2e"][e_i, blk * 512:(blk + 1) * 512, :].rearrange(
                          "(cc p) d -> p cc d", p=128), waits=eg,
                      cls="mw2")
                for blk in range(8)]
        w2ld = [w2ld[-1]] * 8
        evb1 = P.dma("sync", b.b1t[:, :], ip["b1e_p"][e_i], waits=eg,
                     cls="mb")
        evb2 = P.dma("sync", b.bias[:, 80:88], ip["b2e_p"][e_i], waits=eg,
                     cls="mb")
        evb1 = evb2

        def w2tile(f, ht):
            blk, cc = divmod(f, 4)
            return b.a16(W2o + blk * 4096 + cc * 1024 + ht * 128, 128), blk

        for T in range(NTB):
            ss = T % 2
            t0 = T * TB
            g = ygate[ss]
            xnb = [b.a16(XNB + c * 512, 512) for c in range(C)]
            xld = [P.dma("gpsimd", xnb[c],
                         b.xns[c * 128:(c + 1) * 128, t0:t0 + TB],
                         waits=g + [xnb_pe_gate[0]], cls="mxnb")
                   for c in range(C)]
            xld = [xld[-1]] * C
            ev_h1 = []
            for f in range(32):
                bk = b.pb[f % 2]
                mmv = None
                for c in range(C):
                    mmv = P.mm(bk[:, :],
                               b.a16(W1o + c * 4096 + f * 128, 128), xnb[c],
                               start=(c == 0), stop=(c == 7),
                               waits=[xld[c], w1ld[c]] +
                               ([b.bfree[f % 2]] if c == 0 else []),
                               inc="sT")
                ev = P.op("scalar", lambda e, f=f, bk=bk: e.activation(
                    b.a16(H1o + f * 512, 512), bk[:, :], AF.Relu,
                    bias=b.b1t[:, f:f + 1]),
                    waits=[mmv, evb1] + g, inc="sA")
                ev_h1.append(ev)
                b.bfree[f % 2] = ev
            xnb_pe_gate[0] = Ev("sT", P.cnt["sT"])
            cer = b.row(R_CE)
            evce = P.dma("sync", cer, b.c_dram[e_i:e_i + 1, t0:t0 + TB],
                         waits=g, cls=f"mce{ss}")
            cbank = b.pb[3]
            evcm = P.mm(cbank[:, :], b.onerow_f[:], cer, True, True,
                        waits=[evce, b.bfree[3]], inc="sT", cost=2048)
            cb = b.a32(2048 + ss * 512, 512)
            evcb = P.op("vector", lambda e, cb=cb, cbank=cbank:
                        e.tensor_copy(cb, cbank[:, :]),
                        waits=[evcm] + g, inc="sV")
            b.bfree[3] = evcb
            for ht in range(8):
                bk = b.pb[4 + ht % 2]
                mmv = None
                for f in range(32):
                    wt, blk = w2tile(f, ht)
                    mmv = P.mm(bk[:, :], wt, b.a16(H1o + f * 512, 512),
                               start=(f == 0), stop=(f == 31),
                               waits=[ev_h1[f], w2ld[blk]] +
                               ([b.bfree[4 + ht % 2]] if f == 0 else []),
                               inc="sT")
                yt = b.a32(3072 + (ht % 2) * 512, 512)
                evp = P.op("vector", lambda e, yt=yt, bk=bk, ht=ht:
                           e.tensor_scalar(yt, bk[:, :],
                                           b.bias[:, 80 + ht:81 + ht], None,
                                           ALU.add),
                           waits=[mmv, evb2, yt_st.get(ht % 2)] + g,
                           inc="sV")
                b.bfree[4 + ht % 2] = evp
                e1 = P.op("vector", lambda e, yt=yt, cb=cb:
                          e.tensor_mul(yt, yt, cb),
                          waits=[evp, evcb], inc="sV")
                if e_i > 0:
                    ya = b.a32(4096 + (ht % 2) * 512, 512)
                    evy = P.dma("sync", ya,
                                b.alts[ht * 128:(ht + 1) * 128, t0:t0 + TB],
                                waits=g + [ya_rd.get(ht % 2)],
                                cls=f"mya{ht % 2}")
                    e1 = P.op("vector", lambda e, yt=yt, ya=ya:
                              e.tensor_add(yt, yt, ya),
                              waits=[evy], inc="sV")
                    ya_rd[ht % 2] = e1
                if e_i == E - 1:
                    xr = b.a32(0 + (ht % 2) * 512, 512)
                    evx = P.dma("sync", xr,
                                x_src[ht * 128:(ht + 1) * 128, t0:t0 + TB],
                                waits=g + [ya_rd.get(ht % 2)],
                                cls=f"mxr{ht % 2}")
                    e2 = P.op("vector", lambda e, yt=yt, xr=xr:
                              e.scalar_tensor_tensor(yt, yt, 0.5, xr,
                                                     ALU.mult, ALU.add),
                              waits=[evx, e1], inc="sV")
                    ya_rd[ht % 2] = e2
                    evst = P.dma("sync",
                                 x_dst[ht * 128:(ht + 1) * 128, t0:t0 + TB],
                                 yt, waits=[e2], cls=f"myst{ht % 2}")
                else:
                    evst = P.dma("sync",
                                 b.alts[ht * 128:(ht + 1) * 128, t0:t0 + TB],
                                 yt, waits=[e1], cls=f"myst{ht % 2}")
                yt_st[ht % 2] = evst
            ygate[ss] = P.snap()
    return [evst]



def build_full(parts, detect_races=False, pace=True):
    b = Build(detect_races, pace=pace)
    P = b.P
    pre = emit_preamble(b)
    cur = b.inp["x_in"]
    finals = []
    n = len(parts)
    for i, part in enumerate(parts):
        dst = b.y_out if i == n - 1 else b.xs
        emit_ln(b, cur)
        if part == "ln":
            g = P.snap()
            finals = [P.dma("sync", b.y_out[:], b.xns[:], waits=g,
                            cls="dump")]
        elif part.startswith("l0f"):
            nc_ = int(part[3:]) if len(part) > 3 else 3
            finals = emit_l0ffn(b, cur, dst, ncycles=nc_)
        elif part == "l0a":
            finals = emit_l0attn(b, cur, dst)
        elif part == "l1a":
            finals = emit_l1attn(b, cur, dst)
        elif part == "l1f":
            finals = emit_l1ffn(b, cur, dst)
        cur = dst
    end = P.snap()
    P.op("gpsimd", lambda e: e.nop(), waits=end)
    sems = {}
    for s in P.cnt:
        sems[s] = b.nc.alloc_semaphore(name=s)
    b.sems = sems
    P.emit(b.nc, sems)
    return b


PARTS_ALL = ("l0f", "l0a", "l0f", "l1f", "l1a", "l1f")


def prep_inputs(inputs):
    f = np.ascontiguousarray
    x = inputs["L0_attn_representations"]
    maps = []
    base = {}
    base["w_cT"] = f(inputs["L0_attn_Wc"].T)
    bc = inputs["L0_attn_bc"]
    base["bc_p"] = f(bc[:2048].reshape(16, 128).T)
    base["bc_v"] = f(bc[2048:].reshape(1, 8192))
    base["w_oT"] = f(inputs["L0_attn_Wo"].T)
    base["bo_p"] = f(inputs["L0_attn_bo"].reshape(8, 128).T)
    base["cn_w1"] = f(inputs["L0_ffn_cn_w1"])
    base["cn_w2"] = f(inputs["L0_ffn_cn_w2"])
    base["cn_w3"] = f(inputs["L0_ffn_cn_w3"])
    base["cn_w4"] = f(inputs["L0_ffn_cn_w4"])
    base["gc_w1"] = f(inputs["L0_ffn_gc_w1"])
    base["gc_w2"] = f(inputs["L0_ffn_gc_w2"])
    base["gc_w3"] = f(inputs["L0_ffn_gc_w3"])
    l0fb = np.zeros((128, 24), np.float32)
    l0fb[:, 0:8] = inputs["L0_ffn_cn_b1"].reshape(8, 128).T
    l0fb[:, 8:12] = inputs["L0_ffn_cn_b2"].reshape(4, 128).T
    l0fb[:, 12:14] = inputs["L0_ffn_cn_b3"].reshape(2, 128).T
    l0fb[:, 14:18] = inputs["L0_ffn_gc_b1"].reshape(4, 128).T
    l0fb[:, 18:20] = inputs["L0_ffn_gc_b2"].reshape(2, 128).T
    l0fb[0, 20] = inputs["L0_ffn_cn_b4"][0]
    l0fb[0, 21] = inputs["L0_ffn_gc_b3"][0]
    l0fb[0, 22] = inputs["L0_ffn_adaptive_range"][0]
    base["l0f_b"] = l0fb
    base["wq"] = f(inputs["L1_attn_Wq"])
    base["wk"] = f(inputs["L1_attn_Wk"])
    base["wv"] = f(inputs["L1_attn_Wv"])
    base["w0"] = f(inputs["L1_attn_W0"])
    qb = np.zeros((128, 32), np.float32)
    qb[:, 0:8] = inputs["L1_attn_bq"].reshape(8, 128).T
    qb[:, 8:16] = inputs["L1_attn_bk"].reshape(8, 128).T
    qb[:, 16:24] = inputs["L1_attn_bv"].reshape(8, 128).T
    qb[:, 24:32] = inputs["L1_attn_b0"].reshape(8, 128).T
    base["qkvb_p"] = qb
    base["qkvb_rows"] = f(np.stack([inputs["L1_attn_bq"],
                                    inputs["L1_attn_bk"],
                                    inputs["L1_attn_bv"],
                                    inputs["L1_attn_b0"]]))
    base["wr"] = f(inputs["L1_ffn_Wr"])
    base["br_row"] = f(inputs["L1_ffn_br"].reshape(1, 8))
    base["w1e"] = f(inputs["L1_ffn_W1"])
    base["w2e"] = f(inputs["L1_ffn_W2"])
    base["b1e_p"] = f(inputs["L1_ffn_b1"].reshape(8, 32, 128)
                      .transpose(0, 2, 1))
    base["b2e_p"] = f(inputs["L1_ffn_b2"].reshape(8, 8, 128)
                      .transpose(0, 2, 1))
    for core in range(NCORES):
        m = dict(base)
        m["x_in"] = f(np.asarray(x[core]).T)
        maps.append(m)
    return maps


def kernel(**inputs):
    maps = prep_inputs(inputs)
    b = build_full(PARTS_ALL)
    res = run_bass_kernel_spmd(b.nc, maps, core_ids=list(range(NCORES)))
    out = np.stack([np.asarray(r["y_out"]).T for r in res.results])
    return out.astype(np.float32)


# version 03
# speedup vs baseline: 1.5119x; 1.5119x over previous
import math

import numpy as np
import concourse.bass as bass
import concourse.mybir as mybir
from concourse.bass_utils import run_bass_kernel_spmd

F32 = mybir.dt.float32
BF16 = mybir.dt.bfloat16
I32 = mybir.dt.int32
AF = mybir.ActivationFunctionType
ALU = mybir.AluOpType
AX = mybir.AxisListType

NCORES = 8
S = 4096
D = 1024
C = 8
TB = 512
NTB = S // TB
E = 8
DFF = 4096
H = 16
HD = 64
VD0 = 512
CDIM = 2 * D + H * VD0
LN_EPS = 1e-5
DEBUG_L0F = False
LEAKY = [AF.Prelu]

ENGINES = ("tensor", "scalar", "vector", "gpsimd", "sync")
BASE_SEMS = ("sT", "sA", "sV", "sG")


class Ev:
    __slots__ = ("sem", "val")

    def __init__(self, sem, val):
        self.sem = sem
        self.val = val


class Prog:
    def __init__(self, pace=True):
        self.pace_on = pace
        self.q = {e: [] for e in ENGINES}
        self.cnt = {s: 0 for s in BASE_SEMS}
        self._mmg = 0
        self._mmc = 0

    def snap(self):
        return [Ev(s, v) for s, v in self.cnt.items() if v > 0]

    def op(self, eng, fn, waits=(), inc=None, inc_by=1, is_dma=False):
        if eng == "tensor":
            self.flush_pace()
        ws = [(w.sem, w.val) for w in waits if w is not None]
        if not is_dma and eng in ("vector", "scalar", "gpsimd"):
            sem = {"vector": "sV", "scalar": "sA", "gpsimd": "sG"}[eng]
            prev = self.cnt[sem]
            if prev > 0:
                ws.append((sem, prev))
            self.cnt[sem] += 1
            ev_self = Ev(sem, self.cnt[sem])
            if inc is not None and inc != sem:
                self.cnt[inc] += inc_by
                ev = Ev(inc, self.cnt[inc])
                self.q[eng].append((fn, ws, [(sem, 1), (inc, inc_by)]))
                return ev
            self.q[eng].append((fn, ws, [(sem, 1)]))
            return ev_self
        ev = None
        incs = []
        if inc is not None:
            self.cnt[inc] += inc_by
            ev = Ev(inc, self.cnt[inc])
            incs = [(inc, inc_by)]
        self.q[eng].append((fn, ws, incs))
        return ev

    def dma(self, queue, out, in_, waits=(), cls="g"):
        sem = "d_" + cls
        if sem not in self.cnt:
            self.cnt[sem] = 0

        def fn(e, out=out, in_=in_):
            return e.dma_start(out=out, in_=in_)
        return self.op(queue, fn, waits, inc=sem, inc_by=16, is_dma=True)

    def flush_pace(self):
        if self._mmg:
            cyc = int(0.5 * self._mmc) + 100
            if self.pace_on:
                self.q["tensor"].append(
                    (lambda e, cyc=cyc: e.nop(cycle_cnt=cyc), [], []))
            self._mmg = 0
            self._mmc = 0

    def _pe(self, fn, waits, inc, cost):
        ws = [(w.sem, w.val) for w in waits if w is not None]
        ev = None
        incs = []
        if inc is not None:
            self.cnt[inc] += 1
            ev = Ev(inc, self.cnt[inc])
            incs = [(inc, 1)]
        self.q["tensor"].append((fn, ws, incs))
        self._mmg += 1
        self._mmc += cost
        if self._mmg >= 3:
            self.flush_pace()
        return ev

    def mm(self, out, lhsT, rhs, start, stop, waits=(), inc=None,
           cost=None, tile_position=None):
        if cost is None:
            n = rhs.shape[-1]
            cost = n * (4 if lhsT.dtype == F32 else 1)

        def fn(e, out=out, lhsT=lhsT, rhs=rhs, start=start, stop=stop,
               tp=tile_position):
            return e.matmul(out, lhsT, rhs, start=start, stop=stop,
                            tile_position=tp)
        return self._pe(fn, waits, inc, cost)

    def tr(self, out, in_, ident, waits=(), inc=None):
        def fn(e, out=out, in_=in_, ident=ident):
            return e.transpose(out, in_, ident)
        return self._pe(fn, waits, inc, 700)

    def emit(self, nc, sems):
        self.flush_pace()
        with nc.Block() as block:
            def make(eng_name):
                def body(eng):
                    hw = {}
                    for fn, waits, incs in self.q[eng_name]:
                        wm = {}
                        for s, v in waits:
                            if wm.get(s, 0) < v:
                                wm[s] = v
                        for s, v in wm.items():
                            if hw.get(s, 0) < v:
                                eng.wait_ge(sems[s], v)
                                hw[s] = v
                        instr = fn(eng)
                        for s, by in incs:
                            instr = instr.then_inc(sems[s], by)
                return body
            block.tensor(make("tensor"))
            block.scalar(make("scalar"))
            block.vector(make("vector"))
            block.gpsimd(make("gpsimd"))
            block.sync(make("sync"))



class Build:
    CP = {"eps": 0, "mhalf": 1, "oct": 2, "ropea": 3, "halfpi": 4,
          "inv2pi": 5}

    def __init__(self, detect_races=False, pace=True):
        nc = bass.Bass("TRN2", target_bir_lowering=False,
                       detect_race_conditions=detect_races)
        self.nc = nc
        self.P = Prog(pace=pace)
        d = {}

        def inp(name, shape):
            d[name] = nc.dram_tensor(name, list(shape), F32,
                                     kind="ExternalInput").ap()
        inp("x_in", (D, S))
        inp("w_cT", (D, CDIM))
        inp("bc_p", (128, 16))
        inp("bc_v", (1, H * VD0))
        inp("w_oT", (H * VD0, D))
        inp("bo_p", (128, 8))
        inp("cn_w1", (D, D)); inp("cn_w2", (D, 512))
        inp("cn_w3", (512, 256)); inp("cn_w4", (256, 1))
        inp("gc_w1", (D, 512)); inp("gc_w2", (512, 256)); inp("gc_w3", (256, 1))
        inp("l0f_b", (128, 24))
        inp("wq", (D, D)); inp("wk", (D, D)); inp("wv", (D, D)); inp("w0", (D, D))
        inp("qkvb_p", (128, 32))
        inp("qkvb_rows", (4, D))
        inp("wr", (D, E)); inp("br_row", (1, E))
        inp("w1e", (E, D, DFF)); inp("w2e", (E, DFF, D))
        inp("b1e_p", (E, 128, DFF // 128)); inp("b2e_p", (E, 128, D // 128))
        self.inp = d
        self.y_out = nc.dram_tensor("y_out", [D, S], F32,
                                    kind="ExternalOutput").ap()
        self.dbg = nc.dram_tensor("dbg", [4, S], F32,
                                  kind="ExternalOutput").ap()

        T = nc.dram_tensor
        self.xs = T("xs", [D, S], F32, kind="Internal").ap()
        self.xns = T("xns", [D, S], F32, kind="Internal").ap()
        self.alts = T("alts", [D, S], F32, kind="Internal").ap()
        self.qks = T("qks", [2 * D, S], BF16, kind="Internal").ap()
        self.vs = T("vs", [S, H * VD0], BF16, kind="Internal").ap()
        self.bcvs = T("bcvs", [128, H * VD0], F32, kind="Internal").ap()
        self.qbias_bc = T("qbias_bc", [128, 3 * D], F32, kind="Internal").ap()
        self.c_dram = T("c_dram", [E, S], F32, kind="Internal").ap()

        A = nc.alloc_sbuf_tensor
        self.A16 = A("A16", [128, 86016], BF16)
        self.A32 = A("A32", [128, 7168], F32)
        self.SI = A("SI", [128, 512], I32)
        self.SIF = A("SIF", [128, 512], F32)
        self.cpool = A("cpool", [128, 8], F32)
        self.ones_b = A("ones_b", [128, 1], BF16)
        self.onerow_f = A("onerow_f", [1, 128], F32)
        self.ident_b = A("ident_b", [128, 128], BF16)
        self.ident_f = A("ident_f", [128, 128], F32)
        self.pc = A("pc", [128, 2], F32)
        self.pci = A("pci", [128, 2], I32)
        self.bias = A("biaspool", [128, 88], F32)
        self.b1t = A("b1t", [128, 32], F32)
        self.wr_t = A("wr_t", [128, 64], F32)
        self.br_b = A("br_b", [128, 8], F32)
        self.c_t = A("c_t", [8, 128], F32)
        self.tcol = A("tcol", [128, 1], F32)
        self.irow = A("irow", [1, 512], F32)
        self.wrow = A("wrow", [1, 512], F32)

        self.pb = [nc.alloc_psum_tensor(f"pb{i}", [128, 512], F32)
                   for i in range(8)]
        self.bfree = [None] * 8

        self.sems = {}

    def a16(self, off, cols):
        assert off + cols <= 86016, (off, cols)
        return self.A16[:, off:off + cols]

    def a32(self, off, cols):
        assert off + cols <= 5120, (off, cols)
        return self.A32[:, off:off + cols]

    def row(self, off, cols=512):
        assert 5120 <= off and off + cols <= 7168, (off, cols)
        return self.A32[0:1, off:off + cols]

    def cp(self, name, p=128):
        i = Build.CP[name]
        return self.cpool[:, i:i + 1] if p == 128 else self.cpool[0:1, i:i + 1]


R_LN = 5120
R_FFN = 5120
R_ATT = 5632
R_CE = 6144
R_BR = 6656


def emit_preamble(b: Build):
    P = b.P
    vals = {"eps": LN_EPS, "mhalf": -0.5, "oct": 0.125,
            "ropea": -2.0 * math.log(10000.0) / 1024.0,
            "halfpi": math.pi / 2.0, "inv2pi": 1.0 / (2 * math.pi)}
    for name, i in Build.CP.items():
        P.op("vector", lambda e, i=i, v=vals[name]:
             e.memset(b.cpool[:, i:i + 1], float(v)))
    P.op("vector", lambda e: e.memset(b.onerow_f[:], 1.0))
    ev_ones = P.op("vector", lambda e: e.memset(b.ones_b[:], 1.0), inc="sV")
    ev_ir = P.op("vector", lambda e: e.memset(b.irow[0:1, 0:1], 0.0))
    k = 1
    while k < 512:
        ev_ir = P.op("vector", lambda e, k=k: e.tensor_scalar(
            b.irow[0:1, k:2 * k], b.irow[0:1, 0:k], float(k), None, ALU.add))
        k *= 2
    ev_wrow = P.op("scalar", lambda e: e.activation(
        b.wrow[:], b.irow[:], AF.Exp, scale=b.cp("ropea", p=1)),
        waits=[ev_ones, ev_ir], inc="sA")

    ev_i0 = P.op("gpsimd", lambda e: e.iota(
        b.pci[:, 0:1], pattern=[[0, 1]], base=0, channel_multiplier=1),
        inc="sG")
    P.op("vector", lambda e: e.tensor_copy(b.pc[:, 0:1], b.pci[:, 0:1]),
         waits=[ev_i0])
    ev_pc = P.op("vector", lambda e: e.tensor_copy(b.pc[:, 1:2], b.pci[:, 0:1]),
                 inc="sV")

    ev1 = P.op("gpsimd", lambda e: e.memset(b.ident_b[:], 1.0), inc="sG")
    ev2 = P.op("gpsimd", lambda e: e.affine_select(
        b.ident_b[:], b.ident_b[:], pattern=[[-1, 128]],
        compare_op=ALU.is_equal, fill=0.0, base=0, channel_multiplier=1),
        waits=[ev1], inc="sG")
    ev3 = P.op("gpsimd", lambda e: e.memset(b.ident_f[:], 1.0),
               waits=[ev2], inc="sG")
    ev_id = P.op("gpsimd", lambda e: e.affine_select(
        b.ident_f[:], b.ident_f[:], pattern=[[-1, 128]],
        compare_op=ALU.is_equal, fill=0.0, base=0, channel_multiplier=1),
        waits=[ev3], inc="sG")

    evb = []
    evb.append(P.dma("sync", b.bias[:, 0:16], b.inp["bc_p"][:], cls="pre"))
    evb.append(P.dma("sync", b.bias[:, 16:24], b.inp["bo_p"][:], cls="pre"))
    evb.append(P.dma("sync", b.bias[:, 24:48], b.inp["l0f_b"][:], cls="pre"))
    evb.append(P.dma("sync", b.bias[:, 48:80], b.inp["qkvb_p"][:],
                     cls="pre"))
    evb.append(P.dma("sync",
                     b.wr_t[:].rearrange("p (c e) -> p c e", e=8),
                     b.inp["wr"].rearrange("(c p) e -> p c e", p=128),
                     cls="pre"))
    evb.append(P.dma("sync", b.row(R_BR, 8), b.inp["br_row"][:], cls="pre"))
    evb = evb[-1:]
    bank = b.pb[7]
    evm = P.mm(bank[:, 0:8], b.onerow_f[:], b.row(R_BR, 8), True, True,
               waits=evb + [ev_pc], inc="sT", cost=64)
    evc = P.op("vector", lambda e: e.tensor_copy(b.br_b[:], bank[:, 0:8]),
               waits=[evm], inc="sV")
    b.bfree[7] = evc

    evr = P.dma("sync", b.row(R_ATT, 512), b.inp["bc_v"][0:1, 0:512],
                cls="prr")
    prev = None
    for vb in range(16):
        bk = b.pb[6 if vb % 2 == 0 else 7]
        evr2 = evr
        evm = P.mm(bk[:, :], b.onerow_f[:], b.row(R_ATT, 512), True, True,
                   waits=[evr2, b.bfree[6 if vb % 2 == 0 else 7]],
                   inc="sT", cost=2048)
        evc = P.op("vector", lambda e, bk=bk: e.tensor_copy(
            b.a32(4096, 512), bk[:, :]), waits=[evm, prev], inc="sV")
        b.bfree[6 if vb % 2 == 0 else 7] = evc
        evs = P.dma("sync", b.bcvs[:, vb * 512:(vb + 1) * 512],
                    b.a32(4096, 512), waits=[evc], cls="pst")
        prev = evs
        if vb < 15:
            evr = P.dma("sync", b.row(R_ATT, 512),
                        b.inp["bc_v"][0:1, (vb + 1) * 512:(vb + 2) * 512],
                        waits=[evm], cls="prr")
    for j in range(6):
        proj, fh = divmod(j, 2)
        bk = b.pb[6 if j % 2 == 0 else 7]
        evr = P.dma("sync", b.row(R_ATT, 512),
                    b.inp["qkvb_rows"][proj:proj + 1, fh * 512:(fh + 1) * 512],
                    waits=[prev], cls="prr")
        evm = P.mm(bk[:, :], b.onerow_f[:], b.row(R_ATT, 512), True, True,
                   waits=[evr, b.bfree[6 if j % 2 == 0 else 7]],
                   inc="sT", cost=2048)
        evc = P.op("vector", lambda e, bk=bk: e.tensor_copy(
            b.a32(4096, 512), bk[:, :]), waits=[evm], inc="sV")
        b.bfree[6 if j % 2 == 0 else 7] = evc
        evs = P.dma("sync", b.qbias_bc[:, j * 512:(j + 1) * 512],
                    b.a32(4096, 512), waits=[evc], cls="pst")
        prev = evs
    return [ev_wrow, ev_id, prev]



def emit_ln(b: Build, src):
    P = b.P
    gate = [P.snap(), P.snap()]
    for tb in range(NTB):
        ss = tb % 2
        t0 = tb * TB
        g = gate[ss]
        xb = [b.a16(ss * 4096 + c * 512, 512) for c in range(C)]
        sq2 = [b.a16(8192 + ss * 1024 + j * 512, 512) for j in range(2)]
        bank = b.pb[ss]
        bankq = b.pb[2 + ss]
        bbank = b.pb[4 + ss]
        ld = [P.dma("gpsimd", xb[c],
                    src[c * 128:(c + 1) * 128, t0:t0 + TB], waits=g,
                    cls=f"lnxb{ss}")
              for c in range(C)]
        ldall = ld[-1]
        mmq = {}
        mmv = []
        for c in range(C):
            mmv.append(P.mm(bank[0:1, :], b.ones_b[:], xb[c],
                            start=(c == 0), stop=(c == C - 1),
                            waits=[ldall] + ([b.bfree[ss]] if c == 0 else []),
                            inc="sT", cost=512))
            sq = sq2[c % 2]
            evq = P.op("scalar", lambda e, sq=sq, x=xb[c]:
                       e.activation(sq, x, AF.Square),
                       waits=[ldall, mmq.get(c % 2)] + (g if c == 0 else []),
                       inc="sA")
            mq = P.mm(bankq[0:1, :], b.ones_b[:], sq,
                      start=(c == 0), stop=(c == C - 1),
                      waits=[evq] +
                      ([b.bfree[2 + ss]] if c == 0 else []),
                      inc="sT", cost=512)
            mmq[c % 2] = mq
            mmv.append(mq)
        ev_st = mmv[-1]
        mrow = b.row(R_LN + ss * 1024)
        vrow = b.row(R_LN + ss * 1024 + 512)
        rrow = mrow
        e1 = P.op("vector", lambda e, mrow=mrow, bank=bank: e.tensor_scalar(
            mrow, bank[0:1, :], 1.0 / D, None, ALU.mult),
            waits=[ev_st] + g, inc="sV")
        P.op("vector", lambda e, mrow=mrow, vrow=vrow:
             e.tensor_mul(vrow, mrow, mrow))
        e3 = P.op("vector", lambda e, vrow=vrow, bankq=bankq:
                  e.scalar_tensor_tensor(vrow, bankq[0:1, :], 1.0 / D, vrow,
                                         ALU.mult, ALU.subtract), inc="sV")
        P.op("scalar", lambda e, vrow=vrow: e.activation(
            vrow, vrow, AF.Ln, bias=b.cp("eps", p=1)), waits=[e3])
        b.bfree[ss] = e1
        b.bfree[2 + ss] = e3
        mb = b.a32(2048 + ss * 1024, 512)
        rb = b.a32(2048 + ss * 1024 + 512, 512)
        em1 = P.mm(bbank[:, :], b.onerow_f[:], mrow, True, True,
                   waits=[e1, b.bfree[4 + ss]], inc="sT", cost=2048)
        e5 = P.op("scalar", lambda e, vrow=vrow, rrow=rrow: e.activation(
            rrow, vrow, AF.Exp, scale=b.cp("mhalf", p=1)),
            waits=[em1], inc="sA")
        eb1 = P.op("vector", lambda e, mb=mb, bbank=bbank:
                   e.tensor_copy(mb, bbank[:, :]), waits=[em1] + g, inc="sV")
        em2 = P.mm(bbank[:, :], b.onerow_f[:], rrow, True, True,
                   waits=[e5, eb1], inc="sT", cost=2048)
        eb2 = P.op("vector", lambda e, rb=rb, bbank=bbank:
                   e.tensor_copy(rb, bbank[:, :]), waits=[em2], inc="sV")
        b.bfree[4 + ss] = eb2
        last = None
        rd = {}
        for c in range(C):
            xt = b.a32((ss * 2 + (c % 2)) * 512, 512)
            xnt = b.a32(4096 + ss * 512, 512)
            evx = P.dma("sync", xt, src[c * 128:(c + 1) * 128, t0:t0 + TB],
                        waits=g + [rd.get(c % 2)], cls=f"lnxt{ss}{c % 2}")
            e6 = P.op("vector", lambda e, xnt=xnt, xt=xt, mb=mb:
                      e.tensor_sub(xnt, xt, mb), waits=[evx, eb1, eb2,
                                                        last], inc="sV")
            rd[c % 2] = e6
            e7 = P.op("vector", lambda e, xnt=xnt, rb=rb:
                      e.tensor_mul(xnt, xnt, rb), inc="sV")
            last = P.dma("sync", b.xns[c * 128:(c + 1) * 128, t0:t0 + TB],
                         xnt, waits=[e7], cls=f"lnst{ss}")
        gate[ss] = P.snap()
    return P.snap()



def emit_l0ffn(b: Build, x_src, x_dst, ncycles=3):
    P = b.P
    ip = b.inp
    W1o, W2o, W3o, W4o = 0, 8192, 12288, 13312
    G1o, G2o, G3o = 13440, 17536, 18560
    CURB = 18688
    H1o, H2o, H3o = 26880, 30976, 33024
    GCV = 34048

    entry = P.snap()
    wev = []
    for c in range(C):
        wev.append(P.dma("gpsimd", b.a16(W1o + c * 1024, 1024),
                         ip["cn_w1"][c * 128:(c + 1) * 128, :], waits=entry,
                         cls="l0w"))
        wev.append(P.dma("gpsimd", b.a16(W2o + c * 512, 512),
                         ip["cn_w2"][c * 128:(c + 1) * 128, :], cls="l0w"))
        wev.append(P.dma("gpsimd", b.a16(G1o + c * 512, 512),
                         ip["gc_w1"][c * 128:(c + 1) * 128, :], cls="l0w"))
    for c in range(4):
        wev.append(P.dma("gpsimd", b.a16(W3o + c * 256, 256),
                         ip["cn_w3"][c * 128:(c + 1) * 128, :], cls="l0w"))
        wev.append(P.dma("gpsimd", b.a16(G2o + c * 256, 256),
                         ip["gc_w2"][c * 128:(c + 1) * 128, :], cls="l0w"))
    for c in range(2):
        wev.append(P.dma("gpsimd", b.a16(W4o + c, 1),
                         ip["cn_w4"][c * 128:(c + 1) * 128, :], cls="l0w"))
        wev.append(P.dma("gpsimd", b.a16(G3o + c, 1),
                         ip["gc_w3"][c * 128:(c + 1) * 128, :], cls="l0w"))
    wready = wev[-1]

    def bias_ap(col):
        return b.bias[:, 24 + col:25 + col]

    def bias1(col, row=0):
        return b.bias[row:row + 1, 24 + col:25 + col]

    gacc = b.a32(4608, 8)
    gtmp = b.a32(4616, 8)
    prev = [None, None]
    evr = None
    for c in range(C):
        for tb in range(NTB):
            xt = b.a32((tb % 2) * 512, 512)
            evl = P.dma("sync", xt,
                        b.xns[c * 128:(c + 1) * 128, tb * TB:(tb + 1) * TB],
                        waits=entry + [prev[tb % 2]], cls=f"gfx{tb % 2}")
            tgt = gacc if (tb == 0) else gtmp
            evr = P.op("vector", lambda e, c=c, xt=xt, tgt=tgt:
                       e.reduce_sum(tgt[:, c:c + 1], xt, axis=AX.X),
                       waits=[evl], inc="sV")
            if tb:
                evr = P.op("vector", lambda e, c=c: e.tensor_add(
                    gacc[:, c:c + 1], gacc[:, c:c + 1], gtmp[:, c:c + 1]),
                    inc="sV")
            prev[tb % 2] = evr
    ev_gcv = []
    for c in range(C):
        ev_gcv.append(P.op("vector", lambda e, c=c: e.tensor_scalar(
            b.a16(GCV + c, 1), gacc[:, c:c + 1], 1.0 / S, None, ALU.mult),
            waits=[evr], inc="sV"))
    bank = b.pb[4]
    g1 = b.a16(GCV + 8, 4)
    g2 = b.a16(GCV + 12, 2)
    evm = None
    for m in range(4):
        for c in range(C):
            evm = P.mm(bank[:, m:m + 1], b.a16(G1o + c * 512 + m * 128, 128),
                       b.a16(GCV + c, 1), start=(c == 0), stop=(c == 7),
                       waits=([wready] + ev_gcv + [b.bfree[4]])
                       if (m == 0 and c == 0) else (), inc="sT", cost=64)
    ev_g1 = None
    for m in range(4):
        ev_g1 = P.op("scalar", lambda e, m=m: e.activation(
            g1[:, m:m + 1], bank[:, m:m + 1], AF.Gelu, bias=bias_ap(14 + m)),
            waits=[evm], inc="sA")
    for m in range(2):
        for c in range(4):
            evm = P.mm(bank[:, 8 + m:9 + m],
                       b.a16(G2o + c * 256 + m * 128, 128), g1[:, c:c + 1],
                       start=(c == 0), stop=(c == 3),
                       waits=[ev_g1], inc="sT", cost=64)
    ev_g2 = None
    for m in range(2):
        ev_g2 = P.op("scalar", lambda e, m=m: e.activation(
            g2[:, m:m + 1], bank[:, 8 + m:9 + m], AF.Gelu,
            bias=bias_ap(18 + m)), waits=[evm], inc="sA")
    for c in range(2):
        evm = P.mm(bank[0:1, 12:13], b.a16(G3o + c, 1), g2[:, c:c + 1],
                   start=(c == 0), stop=(c == 1), waits=[ev_g2],
                   inc="sT", cost=16)
    gf = b.row(R_FFN, 1)
    gf07 = b.row(R_FFN + 1, 1)
    a2 = b.row(R_FFN + 2, 1)
    c0 = b.row(R_FFN + 3, 1)
    ev_gf = P.op("scalar", lambda e: e.activation(
        gf, bank[0:1, 12:13], AF.Sigmoid, bias=bias1(21)),
        waits=[evm], inc="sA")
    ev_gf2 = P.op("vector", lambda e: e.tensor_scalar(
        gf07, gf, 0.7, None, ALU.mult), waits=[ev_gf], inc="sV")
    P.op("vector", lambda e: e.tensor_scalar(
        a2, bias1(22), 2.0, None, ALU.mult))
    ev_c0 = P.op("vector", lambda e: e.tensor_scalar(
        c0, bias1(22), -1.0, 0.1, ALU.mult, ALU.add), inc="sV")
    b.bfree[4] = ev_gf

    TWP = [4, 5, 6, 7, 0, 1, 2, 3]
    TWS = [1.0, 1.0, -1.0, -1.0, -1.0, -1.0, 1.0, 1.0]
    gate = [P.snap(), P.snap()]
    out_evs = []
    xf_rd, pf_rd, xr_rd = {}, {}, {}
    for r in range(ncycles):
        cur = b.xns if r % 2 == 0 else b.alts
        nxt = b.alts if r % 2 == 0 else b.xns
        final = (r == ncycles - 1)
        for tb in range(NTB):
            ss = tb % 2
            t0 = tb * TB
            g = gate[ss]
            curb = [b.a16(CURB + ss * 4096 + c * 512, 512) for c in range(C)]
            ld = [P.dma("gpsimd", curb[c],
                        cur[c * 128:(c + 1) * 128, t0:t0 + TB], waits=g,
                        cls=f"cur{ss}")
                  for c in range(C)]
            ld = [ld[-1]] * C
            h1 = [b.a16(H1o + m * 512, 512) for m in range(8)]
            h2 = [b.a16(H2o + m * 512, 512) for m in range(4)]
            h3 = [b.a16(H3o + m * 512, 512) for m in range(2)]
            ev_h1 = []
            for m in range(8):
                bk = b.pb[m % 2]
                mmv = None
                for c in range(C):
                    mmv = P.mm(bk[:, :], b.a16(W1o + c * 1024 + m * 128, 128),
                               curb[c], start=(c == 0), stop=(c == 7),
                               waits=[ld[c]] + ([b.bfree[m % 2], wready,
                                                 ev_c0] if c == 0 else []),
                               inc="sT")
                ev = P.op("scalar", lambda e, m=m, bk=bk: e.activation(
                    h1[m], bk[:, :], AF.Gelu, bias=bias_ap(m)),
                    waits=[mmv], inc="sA")
                ev_h1.append(ev)
                b.bfree[m % 2] = ev
            ev_h2 = []
            for m in range(4):
                bk = b.pb[m % 2]
                mmv = None
                for c in range(C):
                    mmv = P.mm(bk[:, :], b.a16(W2o + c * 512 + m * 128, 128),
                               h1[c], start=(c == 0), stop=(c == 7),
                               waits=[ev_h1[c]] +
                               ([b.bfree[m % 2]] if c == 0 else []),
                               inc="sT")
                ev = P.op("scalar", lambda e, m=m, bk=bk: e.activation(
                    h2[m], bk[:, :], AF.Gelu, bias=bias_ap(8 + m)),
                    waits=[mmv], inc="sA")
                ev_h2.append(ev)
                b.bfree[m % 2] = ev
            ev_h3 = []
            for m in range(2):
                bk = b.pb[m % 2]
                mmv = None
                for c in range(4):
                    mmv = P.mm(bk[:, :], b.a16(W3o + c * 256 + m * 128, 128),
                               h2[c], start=(c == 0), stop=(c == 3),
                               waits=[ev_h2[c]] +
                               ([b.bfree[m % 2]] if c == 0 else []),
                               inc="sT")
                ev = P.op("scalar", lambda e, m=m, bk=bk: e.activation(
                    h3[m], bk[:, :], AF.Gelu, bias=bias_ap(12 + m)),
                    waits=[mmv], inc="sA")
                ev_h3.append(ev)
                b.bfree[m % 2] = ev
            bk = b.pb[2]
            mmv = None
            for c in range(2):
                mmv = P.mm(bk[0:1, :], b.a16(W4o + c, 1), h3[c],
                           start=(c == 0), stop=(c == 1),
                           waits=[ev_h3[c]] +
                           ([b.bfree[2]] if c == 0 else []), inc="sT")
            tf = b.row(R_FFN + 4 + ss * 512)
            ev_tf = P.op("scalar", lambda e, tf=tf, bk=bk: e.activation(
                tf, bk[0:1, :], AF.Sigmoid, bias=bias1(20)),
                waits=[mmv], inc="sA")
            b.bfree[2] = ev_tf
            if DEBUG_L0F and r == 0:
                P.dma("sync", b.dbg[0:1, t0:t0 + TB], tf, waits=[ev_tf],
                      cls="dbg")
            ev_cb = P.op("vector", lambda e, tf=tf: e.tensor_scalar(
                tf, tf, 0.3, gf07, ALU.mult, ALU.add),
                waits=[ev_tf, ev_gf2, ev_c0], inc="sV")
            if DEBUG_L0F and r == 0:
                P.dma("sync", b.dbg[1:2, t0:t0 + TB], tf, waits=[ev_cb],
                      cls="dbg")
            ev_cp = P.op("vector", lambda e, tf=tf: e.tensor_scalar(
                tf, tf, a2, c0, ALU.mult, ALU.add), waits=[ev_cb], inc="sV")
            if DEBUG_L0F and r == 0:
                P.dma("sync", b.dbg[2:3, t0:t0 + TB], tf, waits=[ev_cp],
                      cls="dbg")
            bk3 = b.pb[3]
            coup = b.a32(2048 + ss * 512, 512)
            evbm = P.mm(bk3[:, :], b.onerow_f[:], tf, True, True,
                        waits=[ev_cp, b.bfree[3]], inc="sT", cost=2048)
            ev_cpb = P.op("vector", lambda e, coup=coup, bk3=bk3:
                          e.tensor_copy(coup, bk3[:, :]),
                          waits=[evbm] + g, inc="sV")
            b.bfree[3] = ev_cpb
            for c in range(C):
                pc_, s_ = TWP[c], TWS[c]
                xf = b.a32((ss * 2 + (c % 2)) * 512, 512)
                pf = b.a32(3072 + (c % 2) * 512, 512)
                evx = P.dma("sync", xf,
                            cur[c * 128:(c + 1) * 128, t0:t0 + TB],
                            waits=g + [xf_rd.get((ss, c % 2))],
                            cls=f"txf{ss}{c % 2}")
                evp = P.dma("sync", pf,
                            cur[pc_ * 128:(pc_ + 1) * 128, t0:t0 + TB],
                            waits=g + [pf_rd.get(c % 2)], cls=f"tpf{c % 2}")
                ev1 = P.op("vector", lambda e, pf=pf, coup=coup:
                           e.tensor_mul(pf, pf, coup),
                           waits=[evp, ev_cpb], inc="sV")
                pf_rd[c % 2] = ev1
                if not final:
                    ev2 = P.op("vector", lambda e, xf=xf, pf=pf, s_=s_:
                               e.scalar_tensor_tensor(
                                   xf, pf, s_, xf, ALU.mult, ALU.add),
                               waits=[evx, ev1], inc="sV")
                    evst = P.dma("sync",
                                 nxt[c * 128:(c + 1) * 128, t0:t0 + TB],
                                 xf, waits=[ev2], cls=f"tst{ss}{c % 2}")
                    xf_rd[(ss, c % 2)] = evst
                else:
                    xr = b.a32(4096 + (c % 2) * 512, 512)
                    evxr = P.dma("sync", xr,
                                 x_src[c * 128:(c + 1) * 128, t0:t0 + TB],
                                 waits=g + [xr_rd.get(c % 2)],
                                 cls=f"txr{c % 2}")
                    ev2 = P.op("vector", lambda e, xf=xf, xr=xr:
                               e.scalar_tensor_tensor(
                                   xf, xf, 0.5, xr, ALU.mult, ALU.add),
                               waits=[evx, evxr], inc="sV")
                    xr_rd[c % 2] = ev2
                    ev3 = P.op("vector", lambda e, xf=xf, pf=pf, s_=s_:
                               e.scalar_tensor_tensor(
                                   xf, pf, 0.5 * s_, xf, ALU.mult, ALU.add),
                               waits=[ev1, ev2], inc="sV")
                    evst = P.dma("sync",
                                 x_dst[c * 128:(c + 1) * 128, t0:t0 + TB],
                                 xf, waits=[ev3], cls=f"tst{ss}{c % 2}")
                    xf_rd[(ss, c % 2)] = evst
                if final:
                    out_evs.append(evst)
            gate[ss] = P.snap()
    return out_evs



def emit_l0attn(b: Build, x_src, x_dst):
    P = b.P
    ip = b.inp
    entry = P.snap()

    WQK = 0
    WVB = 16384
    XNB = 24576
    QT = 36864
    VT_ = 37888
    wev = []
    for c in range(C):
        wev.append(P.dma("gpsimd", b.a16(WQK + c * 2048, 2048),
                         ip["w_cT"][c * 128:(c + 1) * 128, 0:2048],
                         waits=entry, cls="l0aw"))
    wready = wev[-1]

    gate = [P.snap(), P.snap()]
    qt_st, vt_st = {}, {}
    for tb in range(NTB):
        ss = tb % 2
        t0 = tb * TB
        g = gate[ss]
        xnb = [b.a16(XNB + ss * 4096 + c * 512, 512) for c in range(C)]
        ld = [P.dma("gpsimd", xnb[c],
                    b.xns[c * 128:(c + 1) * 128, t0:t0 + TB],
                    waits=g + ([wready] if tb < 2 else []),
                    cls=f"axnb{ss}")
              for c in range(C)]
        ld = [ld[-1]] * C
        for oc in range(16):
            bk = b.pb[oc % 2]
            qt = b.a16(QT + (oc % 2) * 512, 512)
            mmv = None
            for c in range(C):
                mmv = P.mm(bk[:, :],
                           b.a16(WQK + c * 2048 + oc * 128, 128), xnb[c],
                           start=(c == 0), stop=(c == 7),
                           waits=[ld[c]] +
                           ([b.bfree[oc % 2]] if c == 0 else []), inc="sT")
            ev = P.op("scalar", lambda e, qt=qt, bk=bk, oc=oc: e.activation(
                qt, bk[:, :], LEAKY[0], bias=b.bias[:, oc:oc + 1],
                **({"alpha": 0.01} if LEAKY[0] == AF.Prelu else {})),
                waits=[mmv, qt_st.get(oc % 2)] + g, inc="sA")
            b.bfree[oc % 2] = ev
            qt_st[oc % 2] = P.dma("sync",
                                  b.qks[oc * 128:(oc + 1) * 128, t0:t0 + TB],
                                  qt, waits=[ev], cls=f"aqst{oc % 2}")
        gate[ss] = P.snap()

    gate = [P.snap(), P.snap()]
    xnb_gate = Ev("sT", P.cnt["sT"]) if P.cnt["sT"] else None
    vt32_rd = {}
    for vb in range(16):
        ss = vb % 2
        g = gate[ss]
        wv = [b.a16(WVB + ss * 4096 + c * 512, 512) for c in range(C)]
        wld = [P.dma("gpsimd", wv[c],
                     ip["w_cT"][c * 128:(c + 1) * 128,
                                2048 + vb * 512:2048 + (vb + 1) * 512],
                     waits=g, cls=f"awv{ss}") for c in range(C)]
        wld = [wld[-1]] * C
        bcv = b.a32(2048 + ss * 512, 512)
        evb = P.dma("sync", bcv, b.bcvs[:, vb * 512:(vb + 1) * 512], waits=g,
                    cls=f"abcv{ss}")
        for tb in range(NTB):
            t0 = tb * TB
            xnb = [b.a16(XNB + c * 512, 512) for c in range(C)]
            ld = [P.dma("gpsimd", xnb[c],
                        b.xns[c * 128:(c + 1) * 128, t0:t0 + TB],
                        waits=g + [xnb_gate], cls="a2xnb")
                  for c in range(C)]
            ld = [ld[-1]] * C
            for tt in range(4):
                bk = b.pb[2 + (tt % 2)]
                vt32 = b.a32(3072 + (tt % 2) * 512, 512)
                vt = b.a16(VT_ + (tt % 2) * 512, 512)
                mmv = None
                for c in range(C):
                    mmv = P.mm(bk[:, :], xnb[c][:, tt * 128:(tt + 1) * 128],
                               wv[c],
                               start=(c == 0), stop=(c == 7),
                               waits=[ld[c], wld[c]] +
                               ([b.bfree[2 + (tt % 2)]] if c == 0 else []),
                               inc="sT")
                e1 = P.op("vector", lambda e, vt32=vt32, bk=bk, bcv=bcv:
                          e.tensor_add(vt32, bk[:, :], bcv),
                          waits=[mmv, evb, vt32_rd.get(tt % 2)] + g,
                          inc="sV")
                b.bfree[2 + (tt % 2)] = e1
                vt32_rd[tt % 2] = None
                e2 = P.op("scalar", lambda e, vt=vt, vt32=vt32:
                          e.activation(vt, vt32, LEAKY[0],
                                       **({"alpha": 0.01}
                                          if LEAKY[0] == AF.Prelu else {})),
                          waits=[e1, vt_st.get(tt % 2)] + g, inc="sA")
                vt_st[tt % 2] = P.dma(
                    "sync", b.vs[t0 + tt * 128:t0 + (tt + 1) * 128,
                                 vb * 512:(vb + 1) * 512], vt, waits=[e2],
                    cls=f"avst{tt % 2}")
                vt32_rd[tt % 2] = e2
            xnb_gate = Ev("sT", P.cnt["sT"])
        gate[ss] = P.snap()

    prev_zm = [None]
    ATT = 0
    VT = 16384
    KT = 32768
    QTB = 40960
    OUT = 41984
    pgate = P.snap()
    pe_done = pgate[0]
    lastp = []
    for qb in range(NTB):
        q0 = qb * TB
        for h in range(H):
            hh = h % 2
            kbuf = b.A16[0:64, KT + hh * 4096:KT + hh * 4096 + 4096]
            qbuf = b.A16[0:64, QTB + hh * 512:QTB + hh * 512 + 512]
            evk = P.dma("sync", kbuf,
                        b.qks[D + h * 64:D + (h + 1) * 64, :],
                        waits=pgate + [pe_done], cls=f"akq{h % 2}")
            evq = P.dma("sync", qbuf,
                        b.qks[h * 64:(h + 1) * 64, q0:q0 + TB],
                        waits=pgate + [pe_done], cls=f"akq{h % 2}")
            vld = [P.dma("gpsimd",
                         b.a16(VT + kc * 512, 512),
                         b.vs[kc * 128:(kc + 1) * 128,
                              h * 512:(h + 1) * 512],
                         waits=pgate + [pe_done], cls="av0")
                   for kc in range(32)]
            vld = [vld[-1]] * 32
            ev_at = []
            for kc in range(32):
                bk = b.pb[kc % 2]
                at = b.a16(ATT + kc * 512, 512)
                mmv = P.mm(bk[:, :], kbuf[:, kc * 128:(kc + 1) * 128], qbuf,
                           True, True,
                           waits=[evk, evq, b.bfree[kc % 2]], inc="sT")
                ev = P.op("scalar", lambda e, at=at, bk=bk: e.activation(
                    at, bk[:, :], AF.Exp, scale=b.cp("oct")),
                    waits=[mmv, pe_done] + (pgate if h < 2 else []),
                    inc="sA")
                ev_at.append(ev)
                b.bfree[kc % 2] = ev
            zb = b.pb[2]
            mmz = None
            for kc in range(32):
                mmz = P.mm(zb[0:1, :], b.ones_b[:], b.a16(ATT + kc * 512, 512),
                           start=(kc == 0), stop=(kc == 31),
                           waits=[ev_at[kc]] +
                           ([b.bfree[2]] if kc == 0 else []),
                           inc="sT", cost=512)
            zrow = b.row(R_ATT)
            evz = P.op("vector", lambda e, zrow=zrow, zb=zb: e.reciprocal(
                zrow, zb[0:1, :]), waits=[mmz, prev_zm[0]], inc="sV")
            b.bfree[2] = evz
            zbank = b.pb[3]
            evzm = P.mm(zbank[:, :], b.onerow_f[:], zrow, True, True,
                        waits=[evz, b.bfree[3]], inc="sT", cost=2048)
            prev_zm[0] = evzm
            zinv = b.a32(2048 + (h % 2) * 512, 512)
            evzi = P.op("vector", lambda e, zinv=zinv, zbank=zbank:
                        e.tensor_copy(zinv, zbank[:, :]),
                        waits=[evzm] + (pgate if h < 2 else []), inc="sV")
            b.bfree[3] = evzi
            for vd in range(4):
                bk = b.pb[4 + vd % 2]
                mmv = None
                for kc in range(32):
                    mmv = P.mm(bk[:, :],
                               b.a16(VT + kc * 512 + vd * 128, 128),
                               b.a16(ATT + kc * 512, 512),
                               start=(kc == 0), stop=(kc == 31),
                               waits=[vld[kc], ev_at[kc]] +
                               ([b.bfree[4 + vd % 2]] if kc == 0 else []),
                               inc="sT")
                ev = P.op("vector", lambda e, h=h, vd=vd, bk=bk, zinv=zinv:
                          e.tensor_mul(
                              b.a16(OUT + (h * 4 + vd) * 512, 512),
                              bk[:, :], zinv),
                          waits=[mmv, evzi] + (pgate if h < 2 else []),
                          inc="sV")
                b.bfree[4 + vd % 2] = ev
            pe_done = Ev("sT", P.cnt["sT"])
        out_ready = P.snap()
        pgate = out_ready
        evps = {}
        half_mm_done = None
        for half in range(2):
            wq_ld = []
            for blk in range(2):
                wo_blk = b.a16(blk * 16384, 16384)
                wq_ld.append(P.dma(
                    "gpsimd", wo_blk.rearrange("p (cc d) -> p cc d", cc=16),
                    ip["w_oT"][(half * 2 + blk) * 2048:
                               (half * 2 + blk + 1) * 2048, :].rearrange(
                        "(cc p) d -> p cc d", p=128),
                    waits=out_ready + [half_mm_done], cls="awo"))
            wlast = wq_ld[-1]
            for dt in range(8):
                bk = b.pb[dt]
                mmv = None
                for vc in range(32):
                    blk, cc = divmod(vc, 16)
                    mmv = P.mm(bk[:, :],
                               b.a16(blk * 16384 + cc * 1024 + dt * 128,
                                     128),
                               b.a16(OUT + (half * 32 + vc) * 512, 512),
                               start=(half == 0 and vc == 0),
                               stop=(half == 1 and vc == 31),
                               waits=[wlast] +
                               ([b.bfree[dt]] if (half == 0 and vc == 0)
                                else []),
                               inc="sT")
                evps[(half, dt)] = mmv
            half_mm_done = Ev("sT", P.cnt["sT"])
        prev_ev = {}
        for dt in range(8):
            bk = b.pb[dt]
            pj = b.a32(3072 + (dt % 2) * 512, 512)
            evp = P.op("scalar", lambda e, pj=pj, bk=bk, dt=dt: e.activation(
                pj, bk[:, :], LEAKY[0], bias=b.bias[:, 16 + dt:17 + dt],
                **({"alpha": 0.01} if LEAKY[0] == AF.Prelu else {})),
                waits=[evps[(1, dt)], prev_ev.get(dt % 2)], inc="sA")
            b.bfree[dt] = evp
            xt = b.a32((dt % 2) * 512, 512)
            xnt = b.a32(1024 + (dt % 2) * 512, 512)
            evx = P.dma("sync", xt, x_src[dt * 128:(dt + 1) * 128,
                                          q0:q0 + TB],
                        waits=out_ready + [prev_ev.get(dt % 2)],
                        cls=f"apx{dt % 2}")
            evxn = P.dma("sync", xnt, b.xns[dt * 128:(dt + 1) * 128,
                                            q0:q0 + TB],
                         waits=out_ready + [prev_ev.get(dt % 2)],
                         cls=f"apx{dt % 2}")
            e1 = P.op("vector", lambda e, xt=xt, xnt=xnt: e.tensor_add(
                xt, xt, xnt), waits=[evx, evxn], inc="sV")
            e2 = P.op("vector", lambda e, xt=xt, pj=pj: e.tensor_add(
                xt, xt, pj), waits=[evp, e1], inc="sV")
            evst = P.dma("sync", x_dst[dt * 128:(dt + 1) * 128, q0:q0 + TB],
                         xt, waits=[e2], cls=f"past{dt % 2}")
            prev_ev[dt % 2] = evst
            lastp.append(evst)
        pgate = P.snap()
    return lastp



def emit_l1attn(b: Build, x_src, x_dst):
    P = b.P
    ip = b.inp
    entry = P.snap()

    WQ, WK, WV_ = 0, 8192, 16384
    QTM, VTM = 40960, 41984
    for c in range(C):
        P.dma("gpsimd", b.a16(WQ + c * 1024, 1024),
              ip["wq"][c * 128:(c + 1) * 128, :], waits=entry, cls="l1w")
        P.dma("gpsimd", b.a16(WK + c * 1024, 1024),
              ip["wk"][c * 128:(c + 1) * 128, :], cls="l1w")
        wready = P.dma("gpsimd", b.a16(WV_ + c * 1024, 1024),
                       ip["wv"][c * 128:(c + 1) * 128, :], cls="l1w")
    wb = b.a32(4608, 512)
    evm = P.mm(b.pb[6][:, :], b.onerow_f[:], b.wrow[:], True, True,
               waits=entry + [b.bfree[6]], inc="sT", cost=2048)
    ev_wb = P.op("vector", lambda e: e.tensor_copy(wb, b.pb[6][:, :]),
                 waits=[evm] + entry, inc="sV")
    b.bfree[6] = ev_wb

    qtm = b.a32(0, 1024)
    T1 = b.a32(2048, 512)
    T2 = b.a32(2560, 512)
    ang = b.a32(3072, 512)
    sin_t = b.a32(3584, 512)
    cos_t = b.a32(4096, 512)
    C1 = 6.28125
    C2 = 0.0019353071795864769

    gate = [P.snap(), P.snap()]
    bb_rd, vtm_st = {}, {}
    for tb in range(NTB):
        ss = tb % 2
        t0 = tb * TB
        g = gate[ss]
        QFM = 24576 + ss * 8192
        KFM = 28672 + ss * 8192
        xnb = [b.a16(43008 + ss * 4096 + c * 512, 512) for c in range(C)]
        ld = [P.dma("gpsimd", xnb[c],
                    b.xns[c * 128:(c + 1) * 128, t0:t0 + TB],
                    waits=g + ([wready] if tb < 2 else []),
                    cls=f"l1xnb{ss}")
              for c in range(C)]
        ld = [ld[-1]] * C
        store_evs = []
        for tt in range(4):
            lh = [xnb[c][:, tt * 128:(tt + 1) * 128] for c in range(C)]
            evt = P.op("vector", lambda e, v=float(t0 + tt * 128):
                       e.tensor_scalar(b.tcol[:], b.pc[:, 0:1], 1.0, v,
                                       ALU.mult, ALU.add),
                       waits=g + [ev_wb], inc="sV")
            P.op("vector", lambda e: e.tensor_scalar(
                ang, wb, b.tcol[:], None, ALU.mult))

            def reduce_to(dst_row_sin, src, scr):
                P.op("vector", lambda e, src=src: e.tensor_scalar(
                    b.SIF[:, 0:512], src, 1.0 / (2 * math.pi), 0.5,
                    ALU.mult, ALU.add))
                P.op("vector", lambda e: e.tensor_copy(
                    b.SI[:, 0:512], b.SIF[:, 0:512]))
                P.op("vector", lambda e: e.tensor_copy(
                    b.SIF[:, 0:512], b.SI[:, 0:512]))
                P.op("vector", lambda e, src=src, scr=scr:
                     e.scalar_tensor_tensor(
                         scr, b.SIF[:, 0:512], -C1, src, ALU.mult, ALU.add))
                P.op("vector", lambda e, scr=scr:
                     e.scalar_tensor_tensor(
                         scr, b.SIF[:, 0:512], -C2, scr, ALU.mult,
                         ALU.add))
                ev = P.op("vector", lambda e, scr=scr: e.tensor_scalar(
                    scr, scr, math.pi, -math.pi, ALU.min, ALU.max),
                    inc="sV")
                return P.op("scalar", lambda e, dst=dst_row_sin, scr=scr:
                            e.activation(dst, scr, AF.Sin),
                            waits=[ev], inc="sA")
            P.op("vector", lambda e: e.tensor_scalar(
                T2, ang, math.pi / 2.0, None, ALU.add))
            ev_cos = reduce_to(cos_t, T2, T1)
            ev_sin = reduce_to(sin_t, ang, T2)

            for isq, (Wo_, FM) in enumerate(((WQ, QFM), (WK, KFM))):
                for fh in range(2):
                    bk = b.pb[isq]
                    bb = b.a32(1024 + fh * 512, 512)
                    evbb = P.dma("sync", bb,
                                 b.qbias_bc[:, isq * 1024 + fh * 512:
                                            isq * 1024 + (fh + 1) * 512],
                                 waits=g + [bb_rd.get(fh)],
                                 cls=f"l1bb{fh}")
                    mmv = None
                    for c in range(C):
                        mmv = P.mm(bk[:, :], lh[c],
                                   b.a16(Wo_ + c * 1024 + fh * 512, 512),
                                   start=(c == 0), stop=(c == 7),
                                   waits=[ld[c]] +
                                   ([b.bfree[isq]] if c == 0 else []),
                                   inc="sT")
                    ev = P.op("vector", lambda e, fh=fh, bk=bk, bb=bb:
                              e.tensor_add(
                                  qtm[:, fh * 512:(fh + 1) * 512],
                                  bk[:, :], bb),
                              waits=[mmv, evbb, ev_sin] + g, inc="sV")
                    bb_rd[fh] = ev
                    b.bfree[isq] = ev
                x1 = qtm.rearrange("p (i two) -> p i two", two=2)[:, :, 0]
                x2 = qtm.rearrange("p (i two) -> p i two", two=2)[:, :, 1]
                P.op("vector", lambda e, x1=x1: e.tensor_mul(T1, x1, cos_t),
                     waits=[ev_cos, ev_sin])
                P.op("vector", lambda e, x1=x1: e.tensor_mul(T2, x1, sin_t))
                P.op("vector", lambda e, x2=x2: e.tensor_mul(
                    b.SIF[:, 0:512], x2, sin_t))
                P.op("vector", lambda e, x1=x1: e.tensor_sub(
                    x1, T1, b.SIF[:, 0:512]))
                P.op("vector", lambda e, x2=x2: e.tensor_mul(T1, x2, cos_t))
                P.op("vector", lambda e, x2=x2: e.tensor_add(x2, T2, T1))
                evb16 = P.op("vector", lambda e: e.tensor_copy(
                    b.a16(QTM, 1024), qtm), inc="sV")
                for c in range(C):
                    bkt = b.pb[2 + c % 2][:].bitcast(BF16)
                    evtr = P.tr(bkt[:, 0:128],
                                b.A16[:, QTM + c * 128:QTM + (c + 1) * 128],
                                b.ident_b[:],
                                waits=[evb16, b.bfree[2 + c % 2]], inc="sT")
                    ev2 = P.op("vector", lambda e, FM=FM, c=c, tt=tt,
                               bkt=bkt: e.tensor_copy(
                                   b.A16[:, FM + c * 512 + tt * 128:
                                         FM + c * 512 + tt * 128 + 128],
                                   bkt[:, 0:128]),
                               waits=[evtr] + g, inc="sV")
                    b.bfree[2 + c % 2] = ev2
            for fh in range(2):
                bk = b.pb[4 + fh]
                bb = b.a32(1024 + fh * 512, 512)
                evbb = P.dma("sync", bb,
                             b.qbias_bc[:, 2048 + fh * 512:2048 +
                                        (fh + 1) * 512],
                             waits=g + [bb_rd.get(fh)], cls=f"l1bb{fh}")
                mmv = None
                for c in range(C):
                    mmv = P.mm(bk[:, :], lh[c],
                               b.a16(WV_ + c * 1024 + fh * 512, 512),
                               start=(c == 0), stop=(c == 7),
                               waits=[ld[c]] +
                               ([b.bfree[4 + fh]] if c == 0 else []),
                               inc="sT")
                vt = b.a16(VTM + fh * 512, 512)
                ev = P.op("vector", lambda e, vt=vt, bk=bk, bb=bb:
                          e.tensor_add(vt, bk[:, :], bb),
                          waits=[mmv, evbb, vtm_st.get(fh)] + g, inc="sV")
                bb_rd[fh] = ev
                b.bfree[4 + fh] = ev
                sev = P.dma("sync", b.vs[t0 + tt * 128:t0 + (tt + 1) * 128,
                                         fh * 512:(fh + 1) * 512], vt,
                            waits=[ev], cls=f"l1vst{fh}")
                vtm_st[fh] = sev
                store_evs.append(sev)
        evq_all = Ev("sV", P.cnt["sV"])
        for c in range(C):
            P.dma("sync", b.qks[c * 128:(c + 1) * 128, t0:t0 + TB],
                  b.a16(QFM + c * 512, 512), waits=[evq_all],
                  cls=f"l1qst{ss}")
            P.dma("sync", b.qks[D + c * 128:D + (c + 1) * 128, t0:t0 + TB],
                  b.a16(KFM + c * 512, 512), waits=[evq_all],
                  cls=f"l1qst{ss}")
        gate[ss] = P.snap()

    ATT, VT, KT, QT2, OUT = 0, 32768, 65536, 73728, 74752
    vgate = P.snap()
    vld = [P.dma("gpsimd", b.a16(VT + kc * 1024, 1024),
                 b.vs[kc * 128:(kc + 1) * 128, 0:1024], waits=vgate,
                 cls="l1v")
           for kc in range(32)]
    vld = [vld[-1]] * 32
    pgate = P.snap()
    pe_done = pgate[0]
    prev_zm1 = [None]
    lastp = []
    for qb in range(NTB):
        q0 = qb * TB
        for hp in range(H // 2):
            evzi2 = []
            for hi in range(2):
                h = hp * 2 + hi
                hh = h % 2
                kbuf = b.A16[0:64, KT + hh * 4096:KT + (hh + 1) * 4096]
                qbuf = b.A16[0:64, QT2 + hh * 512:QT2 + hh * 512 + 512]
                evk = P.dma("sync", kbuf,
                            b.qks[D + h * 64:D + (h + 1) * 64, :],
                            waits=pgate + [pe_done], cls=f"l1kq{h % 2}")
                evq = P.dma("sync", qbuf,
                            b.qks[h * 64:(h + 1) * 64, q0:q0 + TB],
                            waits=pgate + [pe_done], cls=f"l1kq{h % 2}")
                ev_at = []
                for kc in range(32):
                    bk = b.pb[kc % 2]
                    at = b.a16(ATT + hi * 16384 + kc * 512, 512)
                    mmv = P.mm(bk[:, :], kbuf[:, kc * 128:(kc + 1) * 128],
                               qbuf, True, True,
                               waits=[evk, evq, b.bfree[kc % 2]], inc="sT")
                    ev = P.op("scalar", lambda e, at=at, bk=bk:
                              e.activation(at, bk[:, :], AF.Exp,
                                           scale=b.cp("oct")),
                              waits=[mmv, pe_done] +
                              (pgate if hp == 0 else []), inc="sA")
                    ev_at.append(ev)
                    b.bfree[kc % 2] = ev
                zb = b.pb[2]
                mmz = None
                for kc in range(32):
                    mmz = P.mm(zb[0:1, :], b.ones_b[:],
                               b.a16(ATT + hi * 16384 + kc * 512, 512),
                               start=(kc == 0), stop=(kc == 31),
                               waits=[ev_at[kc]] +
                               ([b.bfree[2]] if kc == 0 else []),
                               inc="sT", cost=512)
                zrow = b.row(R_ATT)
                evz = P.op("vector", lambda e, zrow=zrow, zb=zb:
                           e.reciprocal(zrow, zb[0:1, :]),
                           waits=[mmz, prev_zm1[0]], inc="sV")
                b.bfree[2] = evz
                zbank = b.pb[3]
                evzm = P.mm(zbank[hi * 64:(hi + 1) * 64, :],
                            b.onerow_f[0:1, 0:64], zrow,
                            start=True, stop=True,
                            waits=[evz] +
                            ([b.bfree[3]] if hi == 0 else []),
                            inc="sT", cost=2048,
                            tile_position=(0, hi * 64))
                evzi2.append(evzm)
                prev_zm1[0] = evzm
            zinv = b.a32(2048 + (hp % 2) * 512, 512)
            evzi = P.op("vector", lambda e, zinv=zinv: e.tensor_copy(
                zinv, b.pb[3][:, :]),
                waits=evzi2 + (pgate if hp == 0 else []), inc="sV")
            b.bfree[3] = evzi
            bkA = b.pb[4 + 2 * (hp % 2)]
            bkB = b.pb[5 + 2 * (hp % 2)]
            mmv = [None, None]
            for kc in range(32):
                for hi in range(2):
                    h = hp * 2 + hi
                    bk_ = bkA if hi == 0 else bkB
                    mmv[hi] = P.mm(
                        bk_[hi * 64:(hi + 1) * 64, :],
                        b.a16(VT + kc * 1024 + h * 64, 64),
                        b.a16(ATT + hi * 16384 + kc * 512, 512),
                        start=(kc == 0), stop=(kc == 31),
                        waits=[vld[kc], ev_at[kc]] +
                        ([b.bfree[4 + 2 * (hp % 2)],
                          b.bfree[5 + 2 * (hp % 2)]] if kc == 0 else []),
                        inc="sT", cost=512, tile_position=(0, hi * 64))
            ev0 = P.op("vector", lambda e, hp=hp, bkA=bkA, zinv=zinv:
                       e.tensor_mul(b.A16[0:64, OUT + hp * 512:
                                          OUT + hp * 512 + 512],
                                    bkA[0:64, :], zinv[0:64, :]),
                       waits=mmv + [evzi] + (pgate if hp == 0 else []),
                       inc="sV")
            ev = P.op("vector", lambda e, hp=hp, bkB=bkB, zinv=zinv:
                      e.tensor_mul(b.A16[64:128, OUT + hp * 512:
                                         OUT + hp * 512 + 512],
                                   bkB[64:128, :], zinv[64:128, :]),
                      inc="sV")
            b.bfree[4 + 2 * (hp % 2)] = ev
            b.bfree[5 + 2 * (hp % 2)] = ev
            pe_done = Ev("sT", P.cnt["sT"])
        out_ready = P.snap()
        pgate = out_ready
        w0ld = [P.dma("gpsimd", b.a16(ATT + c * 1024, 1024),
                      ip["w0"][c * 128:(c + 1) * 128, :],
                      waits=out_ready, cls="l1w0") for c in range(C)]
        w0ld = [w0ld[-1]] * C
        prev_ev = {}
        for dt in range(8):
            bk = b.pb[6 + dt % 2]
            mmv = None
            for c in range(C):
                mmv = P.mm(bk[:, :],
                           b.a16(ATT + c * 1024 + dt * 128, 128),
                           b.a16(OUT + c * 512, 512),
                           start=(c == 0), stop=(c == 7),
                           waits=[w0ld[c]] +
                           ([b.bfree[6 + dt % 2]] if c == 0 else []),
                           inc="sT")
            pj = b.a32(3072 + (dt % 2) * 512, 512)
            evp = P.op("vector", lambda e, pj=pj, bk=bk, dt=dt:
                       e.tensor_scalar(pj, bk[:, :],
                                       b.bias[:, 72 + dt:73 + dt], None,
                                       ALU.add),
                       waits=[mmv, prev_ev.get(dt % 2)] + out_ready,
                       inc="sV")
            b.bfree[6 + dt % 2] = evp
            xt = b.a32((dt % 2) * 512, 512)
            evx = P.dma("sync", xt,
                        x_src[dt * 128:(dt + 1) * 128, q0:q0 + TB],
                        waits=out_ready + [prev_ev.get(dt % 2)],
                        cls=f"l1px{dt % 2}")
            e2 = P.op("vector", lambda e, xt=xt, pj=pj:
                      e.tensor_add(xt, xt, pj), waits=[evx, evp], inc="sV")
            evst = P.dma("sync", x_dst[dt * 128:(dt + 1) * 128, q0:q0 + TB],
                         xt, waits=[e2], cls=f"l1st{dt % 2}")
            prev_ev[dt % 2] = evst
            lastp.append(evst)
        pgate = P.snap()
    return lastp



def emit_l1ffn(b: Build, x_src, x_dst):
    P = b.P
    ip = b.inp
    entry = P.snap()
    W1o = 0
    W2o = 32768
    H1o = 65536
    XNB = 81920

    ggate = [entry, entry]
    prev_tr = None
    prev_mm = {}
    ct_st = {}
    for tb in range(NTB):
        ss = tb % 2
        t0 = tb * TB
        g = ggate[ss]
        for tt in range(4):
            t00 = t0 + tt * 128
            xl = []
            for c in range(C):
                xt = b.a32(ss * 2048 + c * 256 + (tt % 2) * 128, 128)
                xl.append((xt, P.dma("sync", xt,
                                     b.xns[c * 128:(c + 1) * 128,
                                           t00:t00 + 128],
                                     waits=g + [prev_mm.get((ss, tt % 2))],
                                     cls=f"rxl{ss}{tt % 2}")))
            bk = b.pb[ss]
            mmv = None
            xlall = xl[-1][1]
            for c in range(C):
                mmv = P.mm(bk[:, 0:8], xl[c][0],
                           b.wr_t[:, c * 8:(c + 1) * 8],
                           start=(c == 0), stop=(c == 7),
                           waits=[xlall] +
                           ([b.bfree[ss]] if c == 0 else []),
                           inc="sT", cost=64)
            prev_mm[(ss, tt % 2)] = mmv
            lt = b.a32(4096 + ss * 512, 8)
            mx = b.a32(4096 + ss * 512 + 8, 1)
            sm = b.a32(4096 + ss * 512 + 9, 1)
            gtv = b.a32(4096 + ss * 512 + 10, 1)
            tk = b.a32(4096 + ss * 512 + 11, 1)
            ep = b.a32(4096 + ss * 512 + 16, 8)
            eq = b.a32(4096 + ss * 512 + 24, 8)
            ct = b.a32(4096 + ss * 512 + 32, 8)
            e0 = P.op("vector", lambda e, lt=lt, bk=bk: e.tensor_add(
                lt, bk[:, 0:8], b.br_b[:]),
                waits=[mmv, prev_tr] + g, inc="sV")
            b.bfree[ss] = e0
            P.op("vector", lambda e, lt=lt, mx=mx: e.reduce_max(
                mx, lt, axis=AX.X))
            e1 = P.op("vector", lambda e, lt=lt, mx=mx, ep=ep:
                      e.tensor_scalar(ep, lt, mx, None, ALU.subtract),
                      inc="sV")
            e2 = P.op("scalar", lambda e, ep=ep: e.activation(
                ep, ep, AF.Exp), waits=[e1], inc="sA")
            e3 = P.op("vector", lambda e, ep=ep, sm=sm: e.reduce_sum(
                sm, ep, axis=AX.X), waits=[e2], inc="sV")
            P.op("vector", lambda e, sm=sm, gtv=gtv: e.reciprocal(gtv, sm))
            P.op("vector", lambda e, lt=lt, mx=mx, eq=eq:
                 e.tensor_scalar(eq, lt, mx, None, ALU.is_ge))
            P.op("vector", lambda e, tk=tk: e.memset(tk, 0.0))
            for ee in range(E):
                P.op("vector", lambda e, tk=tk, ct=ct, ee=ee:
                     e.tensor_scalar(ct[:, ee:ee + 1], tk, -1.0, 1.0,
                                     ALU.mult, ALU.add))
                P.op("vector", lambda e, eq=eq, ct=ct, ee=ee:
                     e.tensor_mul(ct[:, ee:ee + 1], ct[:, ee:ee + 1],
                                  eq[:, ee:ee + 1]))
                P.op("vector", lambda e, tk=tk, ct=ct, ee=ee:
                     e.tensor_add(tk, tk, ct[:, ee:ee + 1]))
                P.op("vector", lambda e, ct=ct, gtv=gtv, ee=ee:
                     e.tensor_mul(ct[:, ee:ee + 1], ct[:, ee:ee + 1], gtv))
            e5 = P.op("vector", lambda e: e.engine_nop(), inc="sV")
            bkt = b.pb[2 + ss]
            evtr = P.tr(bkt[0:8, 0:128], ct, b.ident_f[:],
                        waits=[e5, b.bfree[2 + ss]], inc="sT")
            prev_tr = evtr
            evc = P.op("vector", lambda e, bkt=bkt: e.tensor_copy(
                b.c_t[:, :], bkt[0:8, 0:128]),
                waits=[evtr, ct_st[0] if ct_st else None] + g, inc="sV")
            b.bfree[2 + ss] = evc
            ct_st[0] = P.dma("sync", b.c_dram[:, t00:t00 + 128], b.c_t[:, :],
                             waits=[evc], cls="rct")
        ggate[ss] = P.snap()
    router_done = P.snap()

    ygate = [router_done, router_done]
    evst = None
    yt_st, ya_rd = {}, {}
    xnb_pe_gate = [None]
    w1_free = [router_done[0]]
    w2_free = [router_done[0]]
    bias_free = [router_done[0], router_done[0]]
    for e_i in range(E):
        eg = P.snap()
        w1ld = [P.dma("gpsimd", b.a16(W1o + c * 4096, 4096),
                      ip["w1e"][e_i, c * 128:(c + 1) * 128, :],
                      waits=[w1_free[0]], cls="mw1")
                for c in range(C)]
        w1ld = [w1ld[-1]] * C
        w2ld = [P.dma("gpsimd",
                      b.a16(W2o + blk * 4096, 4096).rearrange(
                          "p (cc d) -> p cc d", cc=4),
                      ip["w2e"][e_i, blk * 512:(blk + 1) * 512, :].rearrange(
                          "(cc p) d -> p cc d", p=128), waits=[w2_free[0]],
                      cls="mw2")
                for blk in range(8)]
        w2ld = [w2ld[-1]] * 8
        evb1 = P.dma("sync", b.b1t[:, :], ip["b1e_p"][e_i],
                     waits=bias_free, cls="mb")
        evb2 = P.dma("sync", b.bias[:, 80:88], ip["b2e_p"][e_i],
                     waits=bias_free, cls="mb")
        evb1 = evb2

        def w2tile(f, ht):
            blk, cc = divmod(f, 4)
            return b.a16(W2o + blk * 4096 + cc * 1024 + ht * 128, 128), blk

        for T in range(NTB):
            ss = T % 2
            t0 = T * TB
            g = ygate[ss]
            xnb = [b.a16(XNB + c * 512, 512) for c in range(C)]
            xld = [P.dma("gpsimd", xnb[c],
                         b.xns[c * 128:(c + 1) * 128, t0:t0 + TB],
                         waits=g + [xnb_pe_gate[0]], cls="mxnb")
                   for c in range(C)]
            xld = [xld[-1]] * C
            ev_h1 = []
            for f in range(32):
                bk = b.pb[f % 2]
                mmv = None
                for c in range(C):
                    mmv = P.mm(bk[:, :],
                               b.a16(W1o + c * 4096 + f * 128, 128), xnb[c],
                               start=(c == 0), stop=(c == 7),
                               waits=[xld[c], w1ld[c]] +
                               ([b.bfree[f % 2]] if c == 0 else []),
                               inc="sT")
                ev = P.op("scalar", lambda e, f=f, bk=bk: e.activation(
                    b.a16(H1o + f * 512, 512), bk[:, :], AF.Relu,
                    bias=b.b1t[:, f:f + 1]),
                    waits=[mmv, evb1] + g, inc="sA")
                ev_h1.append(ev)
                b.bfree[f % 2] = ev
            xnb_pe_gate[0] = Ev("sT", P.cnt["sT"])
            cer = b.row(R_CE)
            evce = P.dma("sync", cer, b.c_dram[e_i:e_i + 1, t0:t0 + TB],
                         waits=g, cls=f"mce{ss}")
            cbank = b.pb[3]
            evcm = P.mm(cbank[:, :], b.onerow_f[:], cer, True, True,
                        waits=[evce, b.bfree[3]], inc="sT", cost=2048)
            cb = b.a32(2048 + ss * 512, 512)
            evcb = P.op("vector", lambda e, cb=cb, cbank=cbank:
                        e.tensor_copy(cb, cbank[:, :]),
                        waits=[evcm] + g, inc="sV")
            b.bfree[3] = evcb
            for ht in range(8):
                bk = b.pb[4 + ht % 2]
                mmv = None
                for f in range(32):
                    wt, blk = w2tile(f, ht)
                    mmv = P.mm(bk[:, :], wt, b.a16(H1o + f * 512, 512),
                               start=(f == 0), stop=(f == 31),
                               waits=[ev_h1[f], w2ld[blk]] +
                               ([b.bfree[4 + ht % 2]] if f == 0 else []),
                               inc="sT")
                yt = b.a32(3072 + (ht % 2) * 512, 512)
                evp = P.op("vector", lambda e, yt=yt, bk=bk, ht=ht:
                           e.tensor_scalar(yt, bk[:, :],
                                           b.bias[:, 80 + ht:81 + ht], None,
                                           ALU.add),
                           waits=[mmv, evb2, yt_st.get(ht % 2)] + g,
                           inc="sV")
                b.bfree[4 + ht % 2] = evp
                e1 = P.op("vector", lambda e, yt=yt, cb=cb:
                          e.tensor_mul(yt, yt, cb),
                          waits=[evp, evcb], inc="sV")
                if e_i > 0:
                    ya = b.a32(4096 + (ht % 2) * 512, 512)
                    evy = P.dma("sync", ya,
                                b.alts[ht * 128:(ht + 1) * 128, t0:t0 + TB],
                                waits=g + [ya_rd.get(ht % 2)],
                                cls=f"mya{ht % 2}")
                    e1 = P.op("vector", lambda e, yt=yt, ya=ya:
                              e.tensor_add(yt, yt, ya),
                              waits=[evy], inc="sV")
                    ya_rd[ht % 2] = e1
                if e_i == E - 1:
                    xr = b.a32(0 + (ht % 2) * 512, 512)
                    evx = P.dma("sync", xr,
                                x_src[ht * 128:(ht + 1) * 128, t0:t0 + TB],
                                waits=g + [ya_rd.get(ht % 2)],
                                cls=f"mxr{ht % 2}")
                    e2 = P.op("vector", lambda e, yt=yt, xr=xr:
                              e.scalar_tensor_tensor(yt, yt, 0.5, xr,
                                                     ALU.mult, ALU.add),
                              waits=[evx, e1], inc="sV")
                    ya_rd[ht % 2] = e2
                    evst = P.dma("sync",
                                 x_dst[ht * 128:(ht + 1) * 128, t0:t0 + TB],
                                 yt, waits=[e2], cls=f"myst{ht % 2}")
                else:
                    evst = P.dma("sync",
                                 b.alts[ht * 128:(ht + 1) * 128, t0:t0 + TB],
                                 yt, waits=[e1], cls=f"myst{ht % 2}")
                yt_st[ht % 2] = evst
            if T == NTB - 1:
                w1_free[0] = xnb_pe_gate[0]
            ygate[ss] = P.snap()
        w2_free[0] = Ev("sT", P.cnt["sT"])
        bias_free[0] = Ev("sA", P.cnt["sA"])
        bias_free[1] = Ev("sV", P.cnt["sV"])
    return [evst]



def build_full(parts, detect_races=False, pace=True):
    b = Build(detect_races, pace=pace)
    P = b.P
    pre = emit_preamble(b)
    cur = b.inp["x_in"]
    finals = []
    n = len(parts)
    for i, part in enumerate(parts):
        dst = b.y_out if i == n - 1 else b.xs
        emit_ln(b, cur)
        if part == "ln":
            g = P.snap()
            finals = [P.dma("sync", b.y_out[:], b.xns[:], waits=g,
                            cls="dump")]
        elif part.startswith("l0f"):
            nc_ = int(part[3:]) if len(part) > 3 else 3
            finals = emit_l0ffn(b, cur, dst, ncycles=nc_)
        elif part == "l0a":
            finals = emit_l0attn(b, cur, dst)
        elif part == "l1a":
            finals = emit_l1attn(b, cur, dst)
        elif part == "l1f":
            finals = emit_l1ffn(b, cur, dst)
        cur = dst
    end = P.snap()
    P.op("gpsimd", lambda e: e.nop(), waits=end)
    sems = {}
    for s in P.cnt:
        sems[s] = b.nc.alloc_semaphore(name=s)
    b.sems = sems
    P.emit(b.nc, sems)
    return b


PARTS_ALL = ("l0f", "l0a", "l0f", "l1f", "l1a", "l1f")


def prep_inputs(inputs):
    f = np.ascontiguousarray
    x = inputs["L0_attn_representations"]
    maps = []
    base = {}
    base["w_cT"] = f(inputs["L0_attn_Wc"].T)
    bc = inputs["L0_attn_bc"]
    base["bc_p"] = f(bc[:2048].reshape(16, 128).T)
    base["bc_v"] = f(bc[2048:].reshape(1, 8192))
    base["w_oT"] = f(inputs["L0_attn_Wo"].T)
    base["bo_p"] = f(inputs["L0_attn_bo"].reshape(8, 128).T)
    base["cn_w1"] = f(inputs["L0_ffn_cn_w1"])
    base["cn_w2"] = f(inputs["L0_ffn_cn_w2"])
    base["cn_w3"] = f(inputs["L0_ffn_cn_w3"])
    base["cn_w4"] = f(inputs["L0_ffn_cn_w4"])
    base["gc_w1"] = f(inputs["L0_ffn_gc_w1"])
    base["gc_w2"] = f(inputs["L0_ffn_gc_w2"])
    base["gc_w3"] = f(inputs["L0_ffn_gc_w3"])
    l0fb = np.zeros((128, 24), np.float32)
    l0fb[:, 0:8] = inputs["L0_ffn_cn_b1"].reshape(8, 128).T
    l0fb[:, 8:12] = inputs["L0_ffn_cn_b2"].reshape(4, 128).T
    l0fb[:, 12:14] = inputs["L0_ffn_cn_b3"].reshape(2, 128).T
    l0fb[:, 14:18] = inputs["L0_ffn_gc_b1"].reshape(4, 128).T
    l0fb[:, 18:20] = inputs["L0_ffn_gc_b2"].reshape(2, 128).T
    l0fb[0, 20] = inputs["L0_ffn_cn_b4"][0]
    l0fb[0, 21] = inputs["L0_ffn_gc_b3"][0]
    l0fb[0, 22] = inputs["L0_ffn_adaptive_range"][0]
    base["l0f_b"] = l0fb
    base["wq"] = f(inputs["L1_attn_Wq"])
    base["wk"] = f(inputs["L1_attn_Wk"])
    base["wv"] = f(inputs["L1_attn_Wv"])
    base["w0"] = f(inputs["L1_attn_W0"])
    qb = np.zeros((128, 32), np.float32)
    qb[:, 0:8] = inputs["L1_attn_bq"].reshape(8, 128).T
    qb[:, 8:16] = inputs["L1_attn_bk"].reshape(8, 128).T
    qb[:, 16:24] = inputs["L1_attn_bv"].reshape(8, 128).T
    qb[:, 24:32] = inputs["L1_attn_b0"].reshape(8, 128).T
    base["qkvb_p"] = qb
    base["qkvb_rows"] = f(np.stack([inputs["L1_attn_bq"],
                                    inputs["L1_attn_bk"],
                                    inputs["L1_attn_bv"],
                                    inputs["L1_attn_b0"]]))
    base["wr"] = f(inputs["L1_ffn_Wr"])
    base["br_row"] = f(inputs["L1_ffn_br"].reshape(1, 8))
    base["w1e"] = f(inputs["L1_ffn_W1"])
    base["w2e"] = f(inputs["L1_ffn_W2"])
    base["b1e_p"] = f(inputs["L1_ffn_b1"].reshape(8, 32, 128)
                      .transpose(0, 2, 1))
    base["b2e_p"] = f(inputs["L1_ffn_b2"].reshape(8, 8, 128)
                      .transpose(0, 2, 1))
    for core in range(NCORES):
        m = dict(base)
        m["x_in"] = f(np.asarray(x[core]).T)
        maps.append(m)
    return maps


def kernel(**inputs):
    maps = prep_inputs(inputs)
    b = build_full(PARTS_ALL)
    res = run_bass_kernel_spmd(b.nc, maps, core_ids=list(range(NCORES)))
    out = np.stack([np.asarray(r["y_out"]).T for r in res.results])
    return out.astype(np.float32)


# version 04
# speedup vs baseline: 1.7405x; 1.1512x over previous
import math

import numpy as np
import concourse.bass as bass
import concourse.mybir as mybir
from concourse.bass_utils import run_bass_kernel_spmd

F32 = mybir.dt.float32
BF16 = mybir.dt.bfloat16
I32 = mybir.dt.int32
AF = mybir.ActivationFunctionType
ALU = mybir.AluOpType
AX = mybir.AxisListType

NCORES = 8
S = 4096
D = 1024
C = 8
TB = 512
NTB = S // TB
E = 8
DFF = 4096
H = 16
HD = 64
VD0 = 512
CDIM = 2 * D + H * VD0
LN_EPS = 1e-5
DEBUG_L0F = False
LEAKY = [AF.Prelu]

ENGINES = ("tensor", "scalar", "vector", "gpsimd", "sync")
BASE_SEMS = ("sT", "sA", "sV", "sG")


class Ev:
    __slots__ = ("sem", "val")

    def __init__(self, sem, val):
        self.sem = sem
        self.val = val


class Prog:
    def __init__(self, pace=True):
        self.pace_on = pace
        self.q = {e: [] for e in ENGINES}
        self.cnt = {s: 0 for s in BASE_SEMS}
        self._mmg = 0
        self._mmc = 0

    def snap(self):
        return [Ev(s, v) for s, v in self.cnt.items() if v > 0]

    def op(self, eng, fn, waits=(), inc=None, inc_by=1, is_dma=False):
        if eng == "tensor":
            self.flush_pace()
        ws = [(w.sem, w.val) for w in waits if w is not None]
        if not is_dma and eng in ("vector", "scalar", "gpsimd"):
            sem = {"vector": "sV", "scalar": "sA", "gpsimd": "sG"}[eng]
            prev = self.cnt[sem]
            if prev > 0:
                ws.append((sem, prev))
            self.cnt[sem] += 1
            ev_self = Ev(sem, self.cnt[sem])
            if inc is not None and inc != sem:
                self.cnt[inc] += inc_by
                ev = Ev(inc, self.cnt[inc])
                self.q[eng].append((fn, ws, [(sem, 1), (inc, inc_by)]))
                return ev
            self.q[eng].append((fn, ws, [(sem, 1)]))
            return ev_self
        ev = None
        incs = []
        if inc is not None:
            self.cnt[inc] += inc_by
            ev = Ev(inc, self.cnt[inc])
            incs = [(inc, inc_by)]
        self.q[eng].append((fn, ws, incs))
        return ev

    def dma(self, queue, out, in_, waits=(), cls="g"):
        sem = "d_" + cls
        if sem not in self.cnt:
            self.cnt[sem] = 0

        def fn(e, out=out, in_=in_):
            return e.dma_start(out=out, in_=in_)
        return self.op(queue, fn, waits, inc=sem, inc_by=16, is_dma=True)

    def flush_pace(self):
        if self._mmg:
            cyc = int(0.15 * self._mmc) + 100
            if self.pace_on:
                self.q["tensor"].append(
                    (lambda e, cyc=cyc: e.nop(cycle_cnt=cyc), [], []))
            self._mmg = 0
            self._mmc = 0

    def _pe(self, fn, waits, inc, cost):
        ws = [(w.sem, w.val) for w in waits if w is not None]
        ev = None
        incs = []
        if inc is not None:
            self.cnt[inc] += 1
            ev = Ev(inc, self.cnt[inc])
            incs = [(inc, 1)]
        self.q["tensor"].append((fn, ws, incs))
        self._mmg += 1
        self._mmc += cost
        if self._mmg >= 3:
            self.flush_pace()
        return ev

    def mm(self, out, lhsT, rhs, start, stop, waits=(), inc=None,
           cost=None, tile_position=None):
        if cost is None:
            n = rhs.shape[-1]
            cost = n * (4 if lhsT.dtype == F32 else 1)

        def fn(e, out=out, lhsT=lhsT, rhs=rhs, start=start, stop=stop,
               tp=tile_position):
            return e.matmul(out, lhsT, rhs, start=start, stop=stop,
                            tile_position=tp)
        return self._pe(fn, waits, inc, cost)

    def tr(self, out, in_, ident, waits=(), inc=None):
        def fn(e, out=out, in_=in_, ident=ident):
            return e.transpose(out, in_, ident)
        return self._pe(fn, waits, inc, 700)

    def emit(self, nc, sems):
        self.flush_pace()
        with nc.Block() as block:
            def make(eng_name):
                def body(eng):
                    hw = {}
                    for fn, waits, incs in self.q[eng_name]:
                        wm = {}
                        for s, v in waits:
                            if wm.get(s, 0) < v:
                                wm[s] = v
                        for s, v in wm.items():
                            if hw.get(s, 0) < v:
                                eng.wait_ge(sems[s], v)
                                hw[s] = v
                        instr = fn(eng)
                        for s, by in incs:
                            instr = instr.then_inc(sems[s], by)
                return body
            block.tensor(make("tensor"))
            block.scalar(make("scalar"))
            block.vector(make("vector"))
            block.gpsimd(make("gpsimd"))
            block.sync(make("sync"))



class Build:
    CP = {"eps": 0, "mhalf": 1, "oct": 2, "ropea": 3, "halfpi": 4,
          "inv2pi": 5}

    def __init__(self, detect_races=False, pace=True):
        nc = bass.Bass("TRN2", target_bir_lowering=False,
                       detect_race_conditions=detect_races)
        self.nc = nc
        self.P = Prog(pace=pace)
        d = {}

        def inp(name, shape):
            d[name] = nc.dram_tensor(name, list(shape), F32,
                                     kind="ExternalInput").ap()
        inp("x_in", (D, S))
        inp("w_cT", (D, CDIM))
        inp("bc_p", (128, 16))
        inp("bc_v", (1, H * VD0))
        inp("w_oT", (H * VD0, D))
        inp("bo_p", (128, 8))
        inp("cn_w1", (D, D)); inp("cn_w2", (D, 512))
        inp("cn_w3", (512, 256)); inp("cn_w4", (256, 1))
        inp("gc_w1", (D, 512)); inp("gc_w2", (512, 256)); inp("gc_w3", (256, 1))
        inp("l0f_b", (128, 24))
        inp("wq", (D, D)); inp("wk", (D, D)); inp("wv", (D, D)); inp("w0", (D, D))
        inp("qkvb_p", (128, 32))
        inp("qkvb_rows", (4, D))
        inp("wr", (D, E)); inp("br_row", (1, E))
        inp("w1e", (E, D, DFF)); inp("w2e", (E, DFF, D))
        inp("b1e_p", (E, 128, DFF // 128)); inp("b2e_p", (E, 128, D // 128))
        self.inp = d
        self.y_out = nc.dram_tensor("y_out", [D, S], F32,
                                    kind="ExternalOutput").ap()
        self.dbg = nc.dram_tensor("dbg", [4, S], F32,
                                  kind="ExternalOutput").ap()

        T = nc.dram_tensor
        self.xs = T("xs", [D, S], F32, kind="Internal").ap()
        self.xns = T("xns", [D, S], F32, kind="Internal").ap()
        self.alts = T("alts", [D, S], F32, kind="Internal").ap()
        self.qks = T("qks", [2 * D, S], BF16, kind="Internal").ap()
        self.vs = T("vs", [S, H * VD0], BF16, kind="Internal").ap()
        self.bcvs = T("bcvs", [128, H * VD0], F32, kind="Internal").ap()
        self.qbias_bc = T("qbias_bc", [128, 3 * D], F32, kind="Internal").ap()
        self.c_dram = T("c_dram", [E, S], F32, kind="Internal").ap()

        A = nc.alloc_sbuf_tensor
        self.A16 = A("A16", [128, 86016], BF16)
        self.A32 = A("A32", [128, 7168], F32)
        self.SI = A("SI", [128, 512], I32)
        self.SIF = A("SIF", [128, 512], F32)
        self.cpool = A("cpool", [128, 8], F32)
        self.ones_b = A("ones_b", [128, 1], BF16)
        self.onerow_f = A("onerow_f", [1, 128], F32)
        self.ident_b = A("ident_b", [128, 128], BF16)
        self.ident_f = A("ident_f", [128, 128], F32)
        self.pc = A("pc", [128, 2], F32)
        self.pci = A("pci", [128, 2], I32)
        self.bias = A("biaspool", [128, 88], F32)
        self.b1t = A("b1t", [128, 32], F32)
        self.wr_t = A("wr_t", [128, 64], F32)
        self.br_b = A("br_b", [128, 8], F32)
        self.c_t = A("c_t", [8, 128], F32)
        self.tcol = A("tcol", [128, 1], F32)
        self.irow = A("irow", [1, 512], F32)
        self.wrow = A("wrow", [1, 512], F32)

        self.pb = [nc.alloc_psum_tensor(f"pb{i}", [128, 512], F32)
                   for i in range(8)]
        self.bfree = [None] * 8

        self.sems = {}

    def a16(self, off, cols):
        assert off + cols <= 86016, (off, cols)
        return self.A16[:, off:off + cols]

    def a32(self, off, cols):
        assert off + cols <= 5120, (off, cols)
        return self.A32[:, off:off + cols]

    def row(self, off, cols=512):
        assert 5120 <= off and off + cols <= 7168, (off, cols)
        return self.A32[0:1, off:off + cols]

    def cp(self, name, p=128):
        i = Build.CP[name]
        return self.cpool[:, i:i + 1] if p == 128 else self.cpool[0:1, i:i + 1]


R_LN = 5120
R_FFN = 5120
R_ATT = 5632
R_CE = 6144
R_BR = 6656


def emit_preamble(b: Build):
    P = b.P
    vals = {"eps": LN_EPS, "mhalf": -0.5, "oct": 0.125,
            "ropea": -2.0 * math.log(10000.0) / 1024.0,
            "halfpi": math.pi / 2.0, "inv2pi": 1.0 / (2 * math.pi)}
    for name, i in Build.CP.items():
        P.op("vector", lambda e, i=i, v=vals[name]:
             e.memset(b.cpool[:, i:i + 1], float(v)))
    P.op("vector", lambda e: e.memset(b.onerow_f[:], 1.0))
    ev_ones = P.op("vector", lambda e: e.memset(b.ones_b[:], 1.0), inc="sV")
    ev_ir = P.op("vector", lambda e: e.memset(b.irow[0:1, 0:1], 0.0))
    k = 1
    while k < 512:
        ev_ir = P.op("vector", lambda e, k=k: e.tensor_scalar(
            b.irow[0:1, k:2 * k], b.irow[0:1, 0:k], float(k), None, ALU.add))
        k *= 2
    ev_wrow = P.op("scalar", lambda e: e.activation(
        b.wrow[:], b.irow[:], AF.Exp, scale=b.cp("ropea", p=1)),
        waits=[ev_ones, ev_ir], inc="sA")

    ev_i0 = P.op("gpsimd", lambda e: e.iota(
        b.pci[:, 0:1], pattern=[[0, 1]], base=0, channel_multiplier=1),
        inc="sG")
    P.op("vector", lambda e: e.tensor_copy(b.pc[:, 0:1], b.pci[:, 0:1]),
         waits=[ev_i0])
    ev_pc = P.op("vector", lambda e: e.tensor_copy(b.pc[:, 1:2], b.pci[:, 0:1]),
                 inc="sV")

    ev1 = P.op("gpsimd", lambda e: e.memset(b.ident_b[:], 1.0), inc="sG")
    ev2 = P.op("gpsimd", lambda e: e.affine_select(
        b.ident_b[:], b.ident_b[:], pattern=[[-1, 128]],
        compare_op=ALU.is_equal, fill=0.0, base=0, channel_multiplier=1),
        waits=[ev1], inc="sG")
    ev3 = P.op("gpsimd", lambda e: e.memset(b.ident_f[:], 1.0),
               waits=[ev2], inc="sG")
    ev_id = P.op("gpsimd", lambda e: e.affine_select(
        b.ident_f[:], b.ident_f[:], pattern=[[-1, 128]],
        compare_op=ALU.is_equal, fill=0.0, base=0, channel_multiplier=1),
        waits=[ev3], inc="sG")

    evb = []
    evb.append(P.dma("sync", b.bias[:, 0:16], b.inp["bc_p"][:], cls="pre"))
    evb.append(P.dma("sync", b.bias[:, 16:24], b.inp["bo_p"][:], cls="pre"))
    evb.append(P.dma("sync", b.bias[:, 24:48], b.inp["l0f_b"][:], cls="pre"))
    evb.append(P.dma("sync", b.bias[:, 48:80], b.inp["qkvb_p"][:],
                     cls="pre"))
    evb.append(P.dma("sync",
                     b.wr_t[:].rearrange("p (c e) -> p c e", e=8),
                     b.inp["wr"].rearrange("(c p) e -> p c e", p=128),
                     cls="pre"))
    evb.append(P.dma("sync", b.row(R_BR, 8), b.inp["br_row"][:], cls="pre"))
    evb = evb[-1:]
    bank = b.pb[7]
    evm = P.mm(bank[:, 0:8], b.onerow_f[:], b.row(R_BR, 8), True, True,
               waits=evb + [ev_pc], inc="sT", cost=64)
    evc = P.op("vector", lambda e: e.tensor_copy(b.br_b[:], bank[:, 0:8]),
               waits=[evm], inc="sV")
    b.bfree[7] = evc

    evr = P.dma("sync", b.row(R_ATT, 512), b.inp["bc_v"][0:1, 0:512],
                cls="prr")
    prev = None
    for vb in range(16):
        bk = b.pb[6 if vb % 2 == 0 else 7]
        evr2 = evr
        evm = P.mm(bk[:, :], b.onerow_f[:], b.row(R_ATT, 512), True, True,
                   waits=[evr2, b.bfree[6 if vb % 2 == 0 else 7]],
                   inc="sT", cost=2048)
        evc = P.op("vector", lambda e, bk=bk: e.tensor_copy(
            b.a32(4096, 512), bk[:, :]), waits=[evm, prev], inc="sV")
        b.bfree[6 if vb % 2 == 0 else 7] = evc
        evs = P.dma("sync", b.bcvs[:, vb * 512:(vb + 1) * 512],
                    b.a32(4096, 512), waits=[evc], cls="pst")
        prev = evs
        if vb < 15:
            evr = P.dma("sync", b.row(R_ATT, 512),
                        b.inp["bc_v"][0:1, (vb + 1) * 512:(vb + 2) * 512],
                        waits=[evm], cls="prr")
    for j in range(6):
        proj, fh = divmod(j, 2)
        bk = b.pb[6 if j % 2 == 0 else 7]
        evr = P.dma("sync", b.row(R_ATT, 512),
                    b.inp["qkvb_rows"][proj:proj + 1, fh * 512:(fh + 1) * 512],
                    waits=[prev], cls="prr")
        evm = P.mm(bk[:, :], b.onerow_f[:], b.row(R_ATT, 512), True, True,
                   waits=[evr, b.bfree[6 if j % 2 == 0 else 7]],
                   inc="sT", cost=2048)
        evc = P.op("vector", lambda e, bk=bk: e.tensor_copy(
            b.a32(4096, 512), bk[:, :]), waits=[evm], inc="sV")
        b.bfree[6 if j % 2 == 0 else 7] = evc
        evs = P.dma("sync", b.qbias_bc[:, j * 512:(j + 1) * 512],
                    b.a32(4096, 512), waits=[evc], cls="pst")
        prev = evs
    return [ev_wrow, ev_id, prev]



def emit_ln(b: Build, src):
    P = b.P
    gate = [P.snap(), P.snap()]
    for tb in range(NTB):
        ss = tb % 2
        t0 = tb * TB
        g = gate[ss]
        xb = [b.a16(ss * 4096 + c * 512, 512) for c in range(C)]
        sq2 = [b.a16(8192 + ss * 1024 + j * 512, 512) for j in range(2)]
        bank = b.pb[ss]
        bankq = b.pb[2 + ss]
        bbank = b.pb[4 + ss]
        ld = [P.dma("gpsimd", xb[c],
                    src[c * 128:(c + 1) * 128, t0:t0 + TB], waits=g,
                    cls=f"lnxb{ss}")
              for c in range(C)]
        ldall = ld[-1]
        mmq = {}
        mmv = []
        for c in range(C):
            mmv.append(P.mm(bank[0:1, :], b.ones_b[:], xb[c],
                            start=(c == 0), stop=(c == C - 1),
                            waits=[ldall] + ([b.bfree[ss]] if c == 0 else []),
                            inc="sT", cost=512))
            sq = sq2[c % 2]
            evq = P.op("scalar", lambda e, sq=sq, x=xb[c]:
                       e.activation(sq, x, AF.Square),
                       waits=[ldall, mmq.get(c % 2)] + (g if c == 0 else []),
                       inc="sA")
            mq = P.mm(bankq[0:1, :], b.ones_b[:], sq,
                      start=(c == 0), stop=(c == C - 1),
                      waits=[evq] +
                      ([b.bfree[2 + ss]] if c == 0 else []),
                      inc="sT", cost=512)
            mmq[c % 2] = mq
            mmv.append(mq)
        ev_st = mmv[-1]
        mrow = b.row(R_LN + ss * 1024)
        vrow = b.row(R_LN + ss * 1024 + 512)
        rrow = mrow
        e1 = P.op("vector", lambda e, mrow=mrow, bank=bank: e.tensor_scalar(
            mrow, bank[0:1, :], 1.0 / D, None, ALU.mult),
            waits=[ev_st] + g, inc="sV")
        P.op("vector", lambda e, mrow=mrow, vrow=vrow:
             e.tensor_mul(vrow, mrow, mrow))
        e3 = P.op("vector", lambda e, vrow=vrow, bankq=bankq:
                  e.scalar_tensor_tensor(vrow, bankq[0:1, :], 1.0 / D, vrow,
                                         ALU.mult, ALU.subtract), inc="sV")
        P.op("scalar", lambda e, vrow=vrow: e.activation(
            vrow, vrow, AF.Ln, bias=b.cp("eps", p=1)), waits=[e3])
        b.bfree[ss] = e1
        b.bfree[2 + ss] = e3
        mb = b.a32(2048 + ss * 1024, 512)
        rb = b.a32(2048 + ss * 1024 + 512, 512)
        em1 = P.mm(bbank[:, :], b.onerow_f[:], mrow, True, True,
                   waits=[e1, b.bfree[4 + ss]], inc="sT", cost=2048)
        e5 = P.op("scalar", lambda e, vrow=vrow, rrow=rrow: e.activation(
            rrow, vrow, AF.Exp, scale=b.cp("mhalf", p=1)),
            waits=[em1], inc="sA")
        eb1 = P.op("vector", lambda e, mb=mb, bbank=bbank:
                   e.tensor_copy(mb, bbank[:, :]), waits=[em1] + g, inc="sV")
        em2 = P.mm(bbank[:, :], b.onerow_f[:], rrow, True, True,
                   waits=[e5, eb1], inc="sT", cost=2048)
        eb2 = P.op("vector", lambda e, rb=rb, bbank=bbank:
                   e.tensor_copy(rb, bbank[:, :]), waits=[em2], inc="sV")
        b.bfree[4 + ss] = eb2
        last = None
        rd = {}
        for c in range(C):
            xt = b.a32((ss * 2 + (c % 2)) * 512, 512)
            xnt = b.a32(4096 + ss * 512, 512)
            evx = P.dma("sync", xt, src[c * 128:(c + 1) * 128, t0:t0 + TB],
                        waits=g + [rd.get(c % 2)], cls=f"lnxt{ss}{c % 2}")
            e6 = P.op("vector", lambda e, xnt=xnt, xt=xt, mb=mb:
                      e.tensor_sub(xnt, xt, mb), waits=[evx, eb1, eb2,
                                                        last], inc="sV")
            rd[c % 2] = e6
            e7 = P.op("vector", lambda e, xnt=xnt, rb=rb:
                      e.tensor_mul(xnt, xnt, rb), inc="sV")
            last = P.dma("sync", b.xns[c * 128:(c + 1) * 128, t0:t0 + TB],
                         xnt, waits=[e7], cls=f"lnst{ss}")
        gate[ss] = P.snap()
    return P.snap()



def emit_l0ffn(b: Build, x_src, x_dst, ncycles=3):
    P = b.P
    ip = b.inp
    W1o, W2o, W3o, W4o = 0, 8192, 12288, 13312
    G1o, G2o, G3o = 13440, 17536, 18560
    CURB = 18688
    H1o, H2o, H3o = 26880, 30976, 33024
    GCV = 34048

    entry = P.snap()
    wev = []
    for c in range(C):
        wev.append(P.dma("gpsimd", b.a16(W1o + c * 1024, 1024),
                         ip["cn_w1"][c * 128:(c + 1) * 128, :], waits=entry,
                         cls="l0w"))
        wev.append(P.dma("gpsimd", b.a16(W2o + c * 512, 512),
                         ip["cn_w2"][c * 128:(c + 1) * 128, :], cls="l0w"))
        wev.append(P.dma("gpsimd", b.a16(G1o + c * 512, 512),
                         ip["gc_w1"][c * 128:(c + 1) * 128, :], cls="l0w"))
    for c in range(4):
        wev.append(P.dma("gpsimd", b.a16(W3o + c * 256, 256),
                         ip["cn_w3"][c * 128:(c + 1) * 128, :], cls="l0w"))
        wev.append(P.dma("gpsimd", b.a16(G2o + c * 256, 256),
                         ip["gc_w2"][c * 128:(c + 1) * 128, :], cls="l0w"))
    for c in range(2):
        wev.append(P.dma("gpsimd", b.a16(W4o + c, 1),
                         ip["cn_w4"][c * 128:(c + 1) * 128, :], cls="l0w"))
        wev.append(P.dma("gpsimd", b.a16(G3o + c, 1),
                         ip["gc_w3"][c * 128:(c + 1) * 128, :], cls="l0w"))
    wready = wev[-1]

    def bias_ap(col):
        return b.bias[:, 24 + col:25 + col]

    def bias1(col, row=0):
        return b.bias[row:row + 1, 24 + col:25 + col]

    gacc = b.a32(4608, 8)
    gtmp = b.a32(4616, 8)
    prev = [None, None]
    evr = None
    for c in range(C):
        for tb in range(NTB):
            xt = b.a32((tb % 2) * 512, 512)
            evl = P.dma("sync", xt,
                        b.xns[c * 128:(c + 1) * 128, tb * TB:(tb + 1) * TB],
                        waits=entry + [prev[tb % 2]], cls=f"gfx{tb % 2}")
            tgt = gacc if (tb == 0) else gtmp
            evr = P.op("vector", lambda e, c=c, xt=xt, tgt=tgt:
                       e.reduce_sum(tgt[:, c:c + 1], xt, axis=AX.X),
                       waits=[evl], inc="sV")
            if tb:
                evr = P.op("vector", lambda e, c=c: e.tensor_add(
                    gacc[:, c:c + 1], gacc[:, c:c + 1], gtmp[:, c:c + 1]),
                    inc="sV")
            prev[tb % 2] = evr
    ev_gcv = []
    for c in range(C):
        ev_gcv.append(P.op("vector", lambda e, c=c: e.tensor_scalar(
            b.a16(GCV + c, 1), gacc[:, c:c + 1], 1.0 / S, None, ALU.mult),
            waits=[evr], inc="sV"))
    bank = b.pb[4]
    g1 = b.a16(GCV + 8, 4)
    g2 = b.a16(GCV + 12, 2)
    evm = None
    for m in range(4):
        for c in range(C):
            evm = P.mm(bank[:, m:m + 1], b.a16(G1o + c * 512 + m * 128, 128),
                       b.a16(GCV + c, 1), start=(c == 0), stop=(c == 7),
                       waits=([wready] + ev_gcv + [b.bfree[4]])
                       if (m == 0 and c == 0) else (), inc="sT", cost=64)
    ev_g1 = None
    for m in range(4):
        ev_g1 = P.op("scalar", lambda e, m=m: e.activation(
            g1[:, m:m + 1], bank[:, m:m + 1], AF.Gelu, bias=bias_ap(14 + m)),
            waits=[evm], inc="sA")
    for m in range(2):
        for c in range(4):
            evm = P.mm(bank[:, 8 + m:9 + m],
                       b.a16(G2o + c * 256 + m * 128, 128), g1[:, c:c + 1],
                       start=(c == 0), stop=(c == 3),
                       waits=[ev_g1], inc="sT", cost=64)
    ev_g2 = None
    for m in range(2):
        ev_g2 = P.op("scalar", lambda e, m=m: e.activation(
            g2[:, m:m + 1], bank[:, 8 + m:9 + m], AF.Gelu,
            bias=bias_ap(18 + m)), waits=[evm], inc="sA")
    for c in range(2):
        evm = P.mm(bank[0:1, 12:13], b.a16(G3o + c, 1), g2[:, c:c + 1],
                   start=(c == 0), stop=(c == 1), waits=[ev_g2],
                   inc="sT", cost=16)
    gf = b.row(R_FFN, 1)
    gf07 = b.row(R_FFN + 1, 1)
    a2 = b.row(R_FFN + 2, 1)
    c0 = b.row(R_FFN + 3, 1)
    ev_gf = P.op("scalar", lambda e: e.activation(
        gf, bank[0:1, 12:13], AF.Sigmoid, bias=bias1(21)),
        waits=[evm], inc="sA")
    ev_gf2 = P.op("vector", lambda e: e.tensor_scalar(
        gf07, gf, 0.7, None, ALU.mult), waits=[ev_gf], inc="sV")
    P.op("vector", lambda e: e.tensor_scalar(
        a2, bias1(22), 2.0, None, ALU.mult))
    ev_c0 = P.op("vector", lambda e: e.tensor_scalar(
        c0, bias1(22), -1.0, 0.1, ALU.mult, ALU.add), inc="sV")
    b.bfree[4] = ev_gf

    TWP = [4, 5, 6, 7, 0, 1, 2, 3]
    TWS = [1.0, 1.0, -1.0, -1.0, -1.0, -1.0, 1.0, 1.0]
    gate = [P.snap(), P.snap()]
    out_evs = []
    xf_rd, pf_rd, xr_rd = {}, {}, {}
    for r in range(ncycles):
        cur = b.xns if r % 2 == 0 else b.alts
        nxt = b.alts if r % 2 == 0 else b.xns
        final = (r == ncycles - 1)
        for tb in range(NTB):
            ss = tb % 2
            t0 = tb * TB
            g = gate[ss]
            curb = [b.a16(CURB + ss * 4096 + c * 512, 512) for c in range(C)]
            ld = [P.dma("gpsimd", curb[c],
                        cur[c * 128:(c + 1) * 128, t0:t0 + TB], waits=g,
                        cls=f"cur{ss}")
                  for c in range(C)]
            ld = [ld[-1]] * C
            h1 = [b.a16(H1o + m * 512, 512) for m in range(8)]
            h2 = [b.a16(H2o + m * 512, 512) for m in range(4)]
            h3 = [b.a16(H3o + m * 512, 512) for m in range(2)]
            ev_h1 = []
            for m in range(8):
                bk = b.pb[m % 2]
                mmv = None
                for c in range(C):
                    mmv = P.mm(bk[:, :], b.a16(W1o + c * 1024 + m * 128, 128),
                               curb[c], start=(c == 0), stop=(c == 7),
                               waits=[ld[c]] + ([b.bfree[m % 2], wready,
                                                 ev_c0] if c == 0 else []),
                               inc="sT")
                ev = P.op("scalar", lambda e, m=m, bk=bk: e.activation(
                    h1[m], bk[:, :], AF.Gelu, bias=bias_ap(m)),
                    waits=[mmv], inc="sA")
                ev_h1.append(ev)
                b.bfree[m % 2] = ev
            ev_h2 = []
            for m in range(4):
                bk = b.pb[m % 2]
                mmv = None
                for c in range(C):
                    mmv = P.mm(bk[:, :], b.a16(W2o + c * 512 + m * 128, 128),
                               h1[c], start=(c == 0), stop=(c == 7),
                               waits=[ev_h1[c]] +
                               ([b.bfree[m % 2]] if c == 0 else []),
                               inc="sT")
                ev = P.op("scalar", lambda e, m=m, bk=bk: e.activation(
                    h2[m], bk[:, :], AF.Gelu, bias=bias_ap(8 + m)),
                    waits=[mmv], inc="sA")
                ev_h2.append(ev)
                b.bfree[m % 2] = ev
            ev_h3 = []
            for m in range(2):
                bk = b.pb[m % 2]
                mmv = None
                for c in range(4):
                    mmv = P.mm(bk[:, :], b.a16(W3o + c * 256 + m * 128, 128),
                               h2[c], start=(c == 0), stop=(c == 3),
                               waits=[ev_h2[c]] +
                               ([b.bfree[m % 2]] if c == 0 else []),
                               inc="sT")
                ev = P.op("scalar", lambda e, m=m, bk=bk: e.activation(
                    h3[m], bk[:, :], AF.Gelu, bias=bias_ap(12 + m)),
                    waits=[mmv], inc="sA")
                ev_h3.append(ev)
                b.bfree[m % 2] = ev
            bk = b.pb[2]
            mmv = None
            for c in range(2):
                mmv = P.mm(bk[0:1, :], b.a16(W4o + c, 1), h3[c],
                           start=(c == 0), stop=(c == 1),
                           waits=[ev_h3[c]] +
                           ([b.bfree[2]] if c == 0 else []), inc="sT")
            tf = b.row(R_FFN + 4 + ss * 512)
            ev_tf = P.op("scalar", lambda e, tf=tf, bk=bk: e.activation(
                tf, bk[0:1, :], AF.Sigmoid, bias=bias1(20)),
                waits=[mmv], inc="sA")
            b.bfree[2] = ev_tf
            if DEBUG_L0F and r == 0:
                P.dma("sync", b.dbg[0:1, t0:t0 + TB], tf, waits=[ev_tf],
                      cls="dbg")
            ev_cb = P.op("vector", lambda e, tf=tf: e.tensor_scalar(
                tf, tf, 0.3, gf07, ALU.mult, ALU.add),
                waits=[ev_tf, ev_gf2, ev_c0], inc="sV")
            if DEBUG_L0F and r == 0:
                P.dma("sync", b.dbg[1:2, t0:t0 + TB], tf, waits=[ev_cb],
                      cls="dbg")
            ev_cp = P.op("vector", lambda e, tf=tf: e.tensor_scalar(
                tf, tf, a2, c0, ALU.mult, ALU.add), waits=[ev_cb], inc="sV")
            if DEBUG_L0F and r == 0:
                P.dma("sync", b.dbg[2:3, t0:t0 + TB], tf, waits=[ev_cp],
                      cls="dbg")
            bk3 = b.pb[3]
            coup = b.a32(2048 + ss * 512, 512)
            evbm = P.mm(bk3[:, :], b.onerow_f[:], tf, True, True,
                        waits=[ev_cp, b.bfree[3]], inc="sT", cost=2048)
            ev_cpb = P.op("vector", lambda e, coup=coup, bk3=bk3:
                          e.tensor_copy(coup, bk3[:, :]),
                          waits=[evbm] + g, inc="sV")
            b.bfree[3] = ev_cpb
            for c in range(C):
                pc_, s_ = TWP[c], TWS[c]
                xf = b.a32((ss * 2 + (c % 2)) * 512, 512)
                pf = b.a32(3072 + (c % 2) * 512, 512)
                evx = P.dma("sync", xf,
                            cur[c * 128:(c + 1) * 128, t0:t0 + TB],
                            waits=g + [xf_rd.get((ss, c % 2))],
                            cls=f"txf{ss}{c % 2}")
                evp = P.dma("sync", pf,
                            cur[pc_ * 128:(pc_ + 1) * 128, t0:t0 + TB],
                            waits=g + [pf_rd.get(c % 2)], cls=f"tpf{c % 2}")
                ev1 = P.op("vector", lambda e, pf=pf, coup=coup:
                           e.tensor_mul(pf, pf, coup),
                           waits=[evp, ev_cpb], inc="sV")
                pf_rd[c % 2] = ev1
                if not final:
                    ev2 = P.op("vector", lambda e, xf=xf, pf=pf, s_=s_:
                               e.scalar_tensor_tensor(
                                   xf, pf, s_, xf, ALU.mult, ALU.add),
                               waits=[evx, ev1], inc="sV")
                    evst = P.dma("sync",
                                 nxt[c * 128:(c + 1) * 128, t0:t0 + TB],
                                 xf, waits=[ev2], cls=f"tst{ss}{c % 2}")
                    xf_rd[(ss, c % 2)] = evst
                else:
                    xr = b.a32(4096 + (c % 2) * 512, 512)
                    evxr = P.dma("sync", xr,
                                 x_src[c * 128:(c + 1) * 128, t0:t0 + TB],
                                 waits=g + [xr_rd.get(c % 2)],
                                 cls=f"txr{c % 2}")
                    ev2 = P.op("vector", lambda e, xf=xf, xr=xr:
                               e.scalar_tensor_tensor(
                                   xf, xf, 0.5, xr, ALU.mult, ALU.add),
                               waits=[evx, evxr], inc="sV")
                    xr_rd[c % 2] = ev2
                    ev3 = P.op("vector", lambda e, xf=xf, pf=pf, s_=s_:
                               e.scalar_tensor_tensor(
                                   xf, pf, 0.5 * s_, xf, ALU.mult, ALU.add),
                               waits=[ev1, ev2], inc="sV")
                    evst = P.dma("sync",
                                 x_dst[c * 128:(c + 1) * 128, t0:t0 + TB],
                                 xf, waits=[ev3], cls=f"tst{ss}{c % 2}")
                    xf_rd[(ss, c % 2)] = evst
                if final:
                    out_evs.append(evst)
            gate[ss] = P.snap()
    return out_evs



def emit_l0attn(b: Build, x_src, x_dst):
    P = b.P
    ip = b.inp
    entry = P.snap()

    WQK = 0
    WVB = 16384
    XNB = 24576
    QT = 36864
    VT_ = 37888
    wev = []
    for c in range(C):
        wev.append(P.dma("gpsimd", b.a16(WQK + c * 2048, 2048),
                         ip["w_cT"][c * 128:(c + 1) * 128, 0:2048],
                         waits=entry, cls="l0aw"))
    wready = wev[-1]

    gate = [P.snap(), P.snap()]
    qt_st, vt_st = {}, {}
    for tb in range(NTB):
        ss = tb % 2
        t0 = tb * TB
        g = gate[ss]
        xnb = [b.a16(XNB + ss * 4096 + c * 512, 512) for c in range(C)]
        ld = [P.dma("gpsimd", xnb[c],
                    b.xns[c * 128:(c + 1) * 128, t0:t0 + TB],
                    waits=g + ([wready] if tb < 2 else []),
                    cls=f"axnb{ss}")
              for c in range(C)]
        ld = [ld[-1]] * C
        for oc in range(16):
            bk = b.pb[oc % 2]
            qt = b.a16(QT + (oc % 2) * 512, 512)
            mmv = None
            for c in range(C):
                mmv = P.mm(bk[:, :],
                           b.a16(WQK + c * 2048 + oc * 128, 128), xnb[c],
                           start=(c == 0), stop=(c == 7),
                           waits=[ld[c]] +
                           ([b.bfree[oc % 2]] if c == 0 else []), inc="sT")
            ev = P.op("scalar", lambda e, qt=qt, bk=bk, oc=oc: e.activation(
                qt, bk[:, :], LEAKY[0], bias=b.bias[:, oc:oc + 1],
                **({"alpha": 0.01} if LEAKY[0] == AF.Prelu else {})),
                waits=[mmv, qt_st.get(oc % 2)] + g, inc="sA")
            b.bfree[oc % 2] = ev
            qt_st[oc % 2] = P.dma("sync",
                                  b.qks[oc * 128:(oc + 1) * 128, t0:t0 + TB],
                                  qt, waits=[ev], cls=f"aqst{oc % 2}")
        gate[ss] = P.snap()

    gate = [P.snap(), P.snap()]
    xnb_gate = Ev("sT", P.cnt["sT"]) if P.cnt["sT"] else None
    vt32_rd = {}
    for vb in range(16):
        ss = vb % 2
        g = gate[ss]
        wv = [b.a16(WVB + ss * 4096 + c * 512, 512) for c in range(C)]
        wld = [P.dma("gpsimd", wv[c],
                     ip["w_cT"][c * 128:(c + 1) * 128,
                                2048 + vb * 512:2048 + (vb + 1) * 512],
                     waits=g, cls=f"awv{ss}") for c in range(C)]
        wld = [wld[-1]] * C
        bcv = b.a32(2048 + ss * 512, 512)
        evb = P.dma("sync", bcv, b.bcvs[:, vb * 512:(vb + 1) * 512], waits=g,
                    cls=f"abcv{ss}")
        for tb in range(NTB):
            t0 = tb * TB
            xnb = [b.a16(XNB + c * 512, 512) for c in range(C)]
            ld = [P.dma("gpsimd", xnb[c],
                        b.xns[c * 128:(c + 1) * 128, t0:t0 + TB],
                        waits=g + [xnb_gate], cls="a2xnb")
                  for c in range(C)]
            ld = [ld[-1]] * C
            for tt in range(4):
                bk = b.pb[2 + (tt % 2)]
                vt32 = b.a32(3072 + (tt % 2) * 512, 512)
                vt = b.a16(VT_ + (tt % 2) * 512, 512)
                mmv = None
                for c in range(C):
                    mmv = P.mm(bk[:, :], xnb[c][:, tt * 128:(tt + 1) * 128],
                               wv[c],
                               start=(c == 0), stop=(c == 7),
                               waits=[ld[c], wld[c]] +
                               ([b.bfree[2 + (tt % 2)]] if c == 0 else []),
                               inc="sT")
                e1 = P.op("vector", lambda e, vt32=vt32, bk=bk, bcv=bcv:
                          e.tensor_add(vt32, bk[:, :], bcv),
                          waits=[mmv, evb, vt32_rd.get(tt % 2)] + g,
                          inc="sV")
                b.bfree[2 + (tt % 2)] = e1
                vt32_rd[tt % 2] = None
                e2 = P.op("scalar", lambda e, vt=vt, vt32=vt32:
                          e.activation(vt, vt32, LEAKY[0],
                                       **({"alpha": 0.01}
                                          if LEAKY[0] == AF.Prelu else {})),
                          waits=[e1, vt_st.get(tt % 2)] + g, inc="sA")
                vt_st[tt % 2] = P.dma(
                    "sync", b.vs[t0 + tt * 128:t0 + (tt + 1) * 128,
                                 vb * 512:(vb + 1) * 512], vt, waits=[e2],
                    cls=f"avst{tt % 2}")
                vt32_rd[tt % 2] = e2
            xnb_gate = Ev("sT", P.cnt["sT"])
        gate[ss] = P.snap()

    prev_zm = [None]
    ATT = 0
    VT = 16384
    KT = 32768
    QTB = 40960
    OUT = 41984
    pgate = P.snap()
    pe_done = pgate[0]
    lastp = []
    for qb in range(NTB):
        q0 = qb * TB
        for h in range(H):
            hh = h % 2
            kbuf = b.A16[0:64, KT + hh * 4096:KT + hh * 4096 + 4096]
            qbuf = b.A16[0:64, QTB + hh * 512:QTB + hh * 512 + 512]
            evk = P.dma("sync", kbuf,
                        b.qks[D + h * 64:D + (h + 1) * 64, :],
                        waits=pgate + [pe_done], cls=f"akq{h % 2}")
            evq = P.dma("sync", qbuf,
                        b.qks[h * 64:(h + 1) * 64, q0:q0 + TB],
                        waits=pgate + [pe_done], cls=f"akq{h % 2}")
            vld = [P.dma("gpsimd",
                         b.a16(VT + kc * 512, 512),
                         b.vs[kc * 128:(kc + 1) * 128,
                              h * 512:(h + 1) * 512],
                         waits=pgate + [pe_done], cls="av0")
                   for kc in range(32)]
            vld = [vld[-1]] * 32
            ev_at = []
            for kc in range(32):
                bk = b.pb[kc % 2]
                at = b.a16(ATT + kc * 512, 512)
                mmv = P.mm(bk[:, :], kbuf[:, kc * 128:(kc + 1) * 128], qbuf,
                           True, True,
                           waits=[evk, evq, b.bfree[kc % 2]], inc="sT")
                ev = P.op("scalar", lambda e, at=at, bk=bk: e.activation(
                    at, bk[:, :], AF.Exp, scale=b.cp("oct")),
                    waits=[mmv, pe_done] + (pgate if h < 2 else []),
                    inc="sA")
                ev_at.append(ev)
                b.bfree[kc % 2] = ev
            zb = b.pb[2]
            mmz = None
            for kc in range(32):
                mmz = P.mm(zb[0:1, :], b.ones_b[:], b.a16(ATT + kc * 512, 512),
                           start=(kc == 0), stop=(kc == 31),
                           waits=[ev_at[kc]] +
                           ([b.bfree[2]] if kc == 0 else []),
                           inc="sT", cost=512)
            zrow = b.row(R_ATT)
            evz = P.op("vector", lambda e, zrow=zrow, zb=zb: e.reciprocal(
                zrow, zb[0:1, :]), waits=[mmz, prev_zm[0]], inc="sV")
            b.bfree[2] = evz
            zbank = b.pb[3]
            evzm = P.mm(zbank[:, :], b.onerow_f[:], zrow, True, True,
                        waits=[evz, b.bfree[3]], inc="sT", cost=2048)
            prev_zm[0] = evzm
            zinv = b.a32(2048 + (h % 2) * 512, 512)
            evzi = P.op("vector", lambda e, zinv=zinv, zbank=zbank:
                        e.tensor_copy(zinv, zbank[:, :]),
                        waits=[evzm] + (pgate if h < 2 else []), inc="sV")
            b.bfree[3] = evzi
            for vd in range(4):
                bk = b.pb[4 + vd % 2]
                mmv = None
                for kc in range(32):
                    mmv = P.mm(bk[:, :],
                               b.a16(VT + kc * 512 + vd * 128, 128),
                               b.a16(ATT + kc * 512, 512),
                               start=(kc == 0), stop=(kc == 31),
                               waits=[vld[kc], ev_at[kc]] +
                               ([b.bfree[4 + vd % 2]] if kc == 0 else []),
                               inc="sT")
                ev = P.op("vector", lambda e, h=h, vd=vd, bk=bk, zinv=zinv:
                          e.tensor_mul(
                              b.a16(OUT + (h * 4 + vd) * 512, 512),
                              bk[:, :], zinv),
                          waits=[mmv, evzi] + (pgate if h < 2 else []),
                          inc="sV")
                b.bfree[4 + vd % 2] = ev
            pe_done = Ev("sT", P.cnt["sT"])
        out_ready = P.snap()
        pgate = out_ready
        evps = {}
        half_mm_done = None
        for half in range(2):
            wq_ld = []
            for blk in range(2):
                wo_blk = b.a16(blk * 16384, 16384)
                wq_ld.append(P.dma(
                    "gpsimd", wo_blk.rearrange("p (cc d) -> p cc d", cc=16),
                    ip["w_oT"][(half * 2 + blk) * 2048:
                               (half * 2 + blk + 1) * 2048, :].rearrange(
                        "(cc p) d -> p cc d", p=128),
                    waits=out_ready + [half_mm_done], cls="awo"))
            wlast = wq_ld[-1]
            for dt in range(8):
                bk = b.pb[dt]
                mmv = None
                for vc in range(32):
                    blk, cc = divmod(vc, 16)
                    mmv = P.mm(bk[:, :],
                               b.a16(blk * 16384 + cc * 1024 + dt * 128,
                                     128),
                               b.a16(OUT + (half * 32 + vc) * 512, 512),
                               start=(half == 0 and vc == 0),
                               stop=(half == 1 and vc == 31),
                               waits=[wlast] +
                               ([b.bfree[dt]] if (half == 0 and vc == 0)
                                else []),
                               inc="sT")
                evps[(half, dt)] = mmv
            half_mm_done = Ev("sT", P.cnt["sT"])
        prev_ev = {}
        for dt in range(8):
            bk = b.pb[dt]
            pj = b.a32(3072 + (dt % 2) * 512, 512)
            evp = P.op("scalar", lambda e, pj=pj, bk=bk, dt=dt: e.activation(
                pj, bk[:, :], LEAKY[0], bias=b.bias[:, 16 + dt:17 + dt],
                **({"alpha": 0.01} if LEAKY[0] == AF.Prelu else {})),
                waits=[evps[(1, dt)], prev_ev.get(dt % 2)], inc="sA")
            b.bfree[dt] = evp
            xt = b.a32((dt % 2) * 512, 512)
            xnt = b.a32(1024 + (dt % 2) * 512, 512)
            evx = P.dma("sync", xt, x_src[dt * 128:(dt + 1) * 128,
                                          q0:q0 + TB],
                        waits=out_ready + [prev_ev.get(dt % 2)],
                        cls=f"apx{dt % 2}")
            evxn = P.dma("sync", xnt, b.xns[dt * 128:(dt + 1) * 128,
                                            q0:q0 + TB],
                         waits=out_ready + [prev_ev.get(dt % 2)],
                         cls=f"apx{dt % 2}")
            e1 = P.op("vector", lambda e, xt=xt, xnt=xnt: e.tensor_add(
                xt, xt, xnt), waits=[evx, evxn], inc="sV")
            e2 = P.op("vector", lambda e, xt=xt, pj=pj: e.tensor_add(
                xt, xt, pj), waits=[evp, e1], inc="sV")
            evst = P.dma("sync", x_dst[dt * 128:(dt + 1) * 128, q0:q0 + TB],
                         xt, waits=[e2], cls=f"past{dt % 2}")
            prev_ev[dt % 2] = evst
            lastp.append(evst)
        pgate = P.snap()
    return lastp



def emit_l1attn(b: Build, x_src, x_dst):
    P = b.P
    ip = b.inp
    entry = P.snap()

    WQ, WK, WV_ = 0, 8192, 16384
    QTM, VTM = 40960, 41984
    for c in range(C):
        P.dma("gpsimd", b.a16(WQ + c * 1024, 1024),
              ip["wq"][c * 128:(c + 1) * 128, :], waits=entry, cls="l1w")
        P.dma("gpsimd", b.a16(WK + c * 1024, 1024),
              ip["wk"][c * 128:(c + 1) * 128, :], cls="l1w")
        wready = P.dma("gpsimd", b.a16(WV_ + c * 1024, 1024),
                       ip["wv"][c * 128:(c + 1) * 128, :], cls="l1w")
    wb = b.a32(4608, 512)
    evm = P.mm(b.pb[6][:, :], b.onerow_f[:], b.wrow[:], True, True,
               waits=entry + [b.bfree[6]], inc="sT", cost=2048)
    ev_wb = P.op("vector", lambda e: e.tensor_copy(wb, b.pb[6][:, :]),
                 waits=[evm] + entry, inc="sV")
    b.bfree[6] = ev_wb

    qtm = b.a32(0, 1024)
    T1 = b.a32(2048, 512)
    T2 = b.a32(2560, 512)
    ang = b.a32(3072, 512)
    sin_t = b.a32(3584, 512)
    cos_t = b.a32(4096, 512)
    C1 = 6.28125
    C2 = 0.0019353071795864769

    gate = [P.snap(), P.snap()]
    bb_rd, vtm_st = {}, {}
    for tb in range(NTB):
        ss = tb % 2
        t0 = tb * TB
        g = gate[ss]
        QFM = 24576 + ss * 8192
        KFM = 28672 + ss * 8192
        xnb = [b.a16(43008 + ss * 4096 + c * 512, 512) for c in range(C)]
        ld = [P.dma("gpsimd", xnb[c],
                    b.xns[c * 128:(c + 1) * 128, t0:t0 + TB],
                    waits=g + ([wready] if tb < 2 else []),
                    cls=f"l1xnb{ss}")
              for c in range(C)]
        ld = [ld[-1]] * C
        store_evs = []
        for tt in range(4):
            lh = [xnb[c][:, tt * 128:(tt + 1) * 128] for c in range(C)]
            evt = P.op("vector", lambda e, v=float(t0 + tt * 128):
                       e.tensor_scalar(b.tcol[:], b.pc[:, 0:1], 1.0, v,
                                       ALU.mult, ALU.add),
                       waits=g + [ev_wb], inc="sV")
            P.op("vector", lambda e: e.tensor_scalar(
                ang, wb, b.tcol[:], None, ALU.mult))

            def reduce_to(dst_row_sin, src, scr):
                P.op("vector", lambda e, src=src: e.tensor_scalar(
                    b.SIF[:, 0:512], src, 1.0 / (2 * math.pi), 0.5,
                    ALU.mult, ALU.add))
                P.op("vector", lambda e: e.tensor_copy(
                    b.SI[:, 0:512], b.SIF[:, 0:512]))
                P.op("vector", lambda e: e.tensor_copy(
                    b.SIF[:, 0:512], b.SI[:, 0:512]))
                P.op("vector", lambda e, src=src, scr=scr:
                     e.scalar_tensor_tensor(
                         scr, b.SIF[:, 0:512], -C1, src, ALU.mult, ALU.add))
                P.op("vector", lambda e, scr=scr:
                     e.scalar_tensor_tensor(
                         scr, b.SIF[:, 0:512], -C2, scr, ALU.mult,
                         ALU.add))
                ev = P.op("vector", lambda e, scr=scr: e.tensor_scalar(
                    scr, scr, math.pi, -math.pi, ALU.min, ALU.max),
                    inc="sV")
                return P.op("scalar", lambda e, dst=dst_row_sin, scr=scr:
                            e.activation(dst, scr, AF.Sin),
                            waits=[ev], inc="sA")
            P.op("vector", lambda e: e.tensor_scalar(
                T2, ang, math.pi / 2.0, None, ALU.add))
            ev_cos = reduce_to(cos_t, T2, T1)
            ev_sin = reduce_to(sin_t, ang, T2)

            for isq, (Wo_, FM) in enumerate(((WQ, QFM), (WK, KFM))):
                for fh in range(2):
                    bk = b.pb[isq]
                    bb = b.a32(1024 + fh * 512, 512)
                    evbb = P.dma("sync", bb,
                                 b.qbias_bc[:, isq * 1024 + fh * 512:
                                            isq * 1024 + (fh + 1) * 512],
                                 waits=g + [bb_rd.get(fh)],
                                 cls=f"l1bb{fh}")
                    mmv = None
                    for c in range(C):
                        mmv = P.mm(bk[:, :], lh[c],
                                   b.a16(Wo_ + c * 1024 + fh * 512, 512),
                                   start=(c == 0), stop=(c == 7),
                                   waits=[ld[c]] +
                                   ([b.bfree[isq]] if c == 0 else []),
                                   inc="sT")
                    ev = P.op("vector", lambda e, fh=fh, bk=bk, bb=bb:
                              e.tensor_add(
                                  qtm[:, fh * 512:(fh + 1) * 512],
                                  bk[:, :], bb),
                              waits=[mmv, evbb, ev_sin] + g, inc="sV")
                    bb_rd[fh] = ev
                    b.bfree[isq] = ev
                x1 = qtm.rearrange("p (i two) -> p i two", two=2)[:, :, 0]
                x2 = qtm.rearrange("p (i two) -> p i two", two=2)[:, :, 1]
                P.op("vector", lambda e, x1=x1: e.tensor_mul(T1, x1, cos_t),
                     waits=[ev_cos, ev_sin])
                P.op("vector", lambda e, x1=x1: e.tensor_mul(T2, x1, sin_t))
                P.op("vector", lambda e, x2=x2: e.tensor_mul(
                    b.SIF[:, 0:512], x2, sin_t))
                P.op("vector", lambda e, x1=x1: e.tensor_sub(
                    x1, T1, b.SIF[:, 0:512]))
                P.op("vector", lambda e, x2=x2: e.tensor_mul(T1, x2, cos_t))
                P.op("vector", lambda e, x2=x2: e.tensor_add(x2, T2, T1))
                evb16 = P.op("vector", lambda e: e.tensor_copy(
                    b.a16(QTM, 1024), qtm), inc="sV")
                for c in range(C):
                    bkt = b.pb[2 + c % 2][:].bitcast(BF16)
                    evtr = P.tr(bkt[:, 0:128],
                                b.A16[:, QTM + c * 128:QTM + (c + 1) * 128],
                                b.ident_b[:],
                                waits=[evb16, b.bfree[2 + c % 2]], inc="sT")
                    ev2 = P.op("vector", lambda e, FM=FM, c=c, tt=tt,
                               bkt=bkt: e.tensor_copy(
                                   b.A16[:, FM + c * 512 + tt * 128:
                                         FM + c * 512 + tt * 128 + 128],
                                   bkt[:, 0:128]),
                               waits=[evtr] + g, inc="sV")
                    b.bfree[2 + c % 2] = ev2
            for fh in range(2):
                bk = b.pb[4 + fh]
                bb = b.a32(1024 + fh * 512, 512)
                evbb = P.dma("sync", bb,
                             b.qbias_bc[:, 2048 + fh * 512:2048 +
                                        (fh + 1) * 512],
                             waits=g + [bb_rd.get(fh)], cls=f"l1bb{fh}")
                mmv = None
                for c in range(C):
                    mmv = P.mm(bk[:, :], lh[c],
                               b.a16(WV_ + c * 1024 + fh * 512, 512),
                               start=(c == 0), stop=(c == 7),
                               waits=[ld[c]] +
                               ([b.bfree[4 + fh]] if c == 0 else []),
                               inc="sT")
                vt = b.a16(VTM + fh * 512, 512)
                ev = P.op("vector", lambda e, vt=vt, bk=bk, bb=bb:
                          e.tensor_add(vt, bk[:, :], bb),
                          waits=[mmv, evbb, vtm_st.get(fh)] + g, inc="sV")
                bb_rd[fh] = ev
                b.bfree[4 + fh] = ev
                sev = P.dma("sync", b.vs[t0 + tt * 128:t0 + (tt + 1) * 128,
                                         fh * 512:(fh + 1) * 512], vt,
                            waits=[ev], cls=f"l1vst{fh}")
                vtm_st[fh] = sev
                store_evs.append(sev)
        evq_all = Ev("sV", P.cnt["sV"])
        for c in range(C):
            P.dma("sync", b.qks[c * 128:(c + 1) * 128, t0:t0 + TB],
                  b.a16(QFM + c * 512, 512), waits=[evq_all],
                  cls=f"l1qst{ss}")
            P.dma("sync", b.qks[D + c * 128:D + (c + 1) * 128, t0:t0 + TB],
                  b.a16(KFM + c * 512, 512), waits=[evq_all],
                  cls=f"l1qst{ss}")
        gate[ss] = P.snap()

    ATT, VT, KT, QT2, OUT = 0, 32768, 65536, 73728, 74752
    vgate = P.snap()
    vld = [P.dma("gpsimd", b.a16(VT + kc * 1024, 1024),
                 b.vs[kc * 128:(kc + 1) * 128, 0:1024], waits=vgate,
                 cls="l1v")
           for kc in range(32)]
    vld = [vld[-1]] * 32
    pgate = P.snap()
    pe_done = pgate[0]
    prev_zm1 = [None]
    lastp = []
    for qb in range(NTB):
        q0 = qb * TB
        for hp in range(H // 2):
            evzi2 = []
            for hi in range(2):
                h = hp * 2 + hi
                hh = h % 2
                kbuf = b.A16[0:64, KT + hh * 4096:KT + (hh + 1) * 4096]
                qbuf = b.A16[0:64, QT2 + hh * 512:QT2 + hh * 512 + 512]
                evk = P.dma("sync", kbuf,
                            b.qks[D + h * 64:D + (h + 1) * 64, :],
                            waits=pgate + [pe_done], cls=f"l1kq{h % 2}")
                evq = P.dma("sync", qbuf,
                            b.qks[h * 64:(h + 1) * 64, q0:q0 + TB],
                            waits=pgate + [pe_done], cls=f"l1kq{h % 2}")
                ev_at = []
                for kc in range(32):
                    bk = b.pb[kc % 2]
                    at = b.a16(ATT + hi * 16384 + kc * 512, 512)
                    mmv = P.mm(bk[:, :], kbuf[:, kc * 128:(kc + 1) * 128],
                               qbuf, True, True,
                               waits=[evk, evq, b.bfree[kc % 2]], inc="sT")
                    ev = P.op("scalar", lambda e, at=at, bk=bk:
                              e.activation(at, bk[:, :], AF.Exp,
                                           scale=b.cp("oct")),
                              waits=[mmv, pe_done] +
                              (pgate if hp == 0 else []), inc="sA")
                    ev_at.append(ev)
                    b.bfree[kc % 2] = ev
                zb = b.pb[2]
                mmz = None
                for kc in range(32):
                    mmz = P.mm(zb[0:1, :], b.ones_b[:],
                               b.a16(ATT + hi * 16384 + kc * 512, 512),
                               start=(kc == 0), stop=(kc == 31),
                               waits=[ev_at[kc]] +
                               ([b.bfree[2]] if kc == 0 else []),
                               inc="sT", cost=512)
                zrow = b.row(R_ATT)
                evz = P.op("vector", lambda e, zrow=zrow, zb=zb:
                           e.reciprocal(zrow, zb[0:1, :]),
                           waits=[mmz, prev_zm1[0]], inc="sV")
                b.bfree[2] = evz
                zbank = b.pb[3]
                evzm = P.mm(zbank[hi * 64:(hi + 1) * 64, :],
                            b.onerow_f[0:1, 0:64], zrow,
                            start=True, stop=True,
                            waits=[evz] +
                            ([b.bfree[3]] if hi == 0 else []),
                            inc="sT", cost=2048,
                            tile_position=(0, hi * 64))
                evzi2.append(evzm)
                prev_zm1[0] = evzm
            zinv = b.a32(2048 + (hp % 2) * 512, 512)
            evzi = P.op("vector", lambda e, zinv=zinv: e.tensor_copy(
                zinv, b.pb[3][:, :]),
                waits=evzi2 + (pgate if hp == 0 else []), inc="sV")
            b.bfree[3] = evzi
            bkA = b.pb[4 + 2 * (hp % 2)]
            bkB = b.pb[5 + 2 * (hp % 2)]
            mmv = [None, None]
            for kc in range(32):
                for hi in range(2):
                    h = hp * 2 + hi
                    bk_ = bkA if hi == 0 else bkB
                    mmv[hi] = P.mm(
                        bk_[hi * 64:(hi + 1) * 64, :],
                        b.a16(VT + kc * 1024 + h * 64, 64),
                        b.a16(ATT + hi * 16384 + kc * 512, 512),
                        start=(kc == 0), stop=(kc == 31),
                        waits=[vld[kc], ev_at[kc]] +
                        ([b.bfree[4 + 2 * (hp % 2)],
                          b.bfree[5 + 2 * (hp % 2)]] if kc == 0 else []),
                        inc="sT", cost=512, tile_position=(0, hi * 64))
            ev0 = P.op("vector", lambda e, hp=hp, bkA=bkA, zinv=zinv:
                       e.tensor_mul(b.A16[0:64, OUT + hp * 512:
                                          OUT + hp * 512 + 512],
                                    bkA[0:64, :], zinv[0:64, :]),
                       waits=mmv + [evzi] + (pgate if hp == 0 else []),
                       inc="sV")
            ev = P.op("vector", lambda e, hp=hp, bkB=bkB, zinv=zinv:
                      e.tensor_mul(b.A16[64:128, OUT + hp * 512:
                                         OUT + hp * 512 + 512],
                                   bkB[64:128, :], zinv[64:128, :]),
                      inc="sV")
            b.bfree[4 + 2 * (hp % 2)] = ev
            b.bfree[5 + 2 * (hp % 2)] = ev
            pe_done = Ev("sT", P.cnt["sT"])
        out_ready = P.snap()
        pgate = out_ready
        w0ld = [P.dma("gpsimd", b.a16(ATT + c * 1024, 1024),
                      ip["w0"][c * 128:(c + 1) * 128, :],
                      waits=out_ready, cls="l1w0") for c in range(C)]
        w0ld = [w0ld[-1]] * C
        prev_ev = {}
        for dt in range(8):
            bk = b.pb[6 + dt % 2]
            mmv = None
            for c in range(C):
                mmv = P.mm(bk[:, :],
                           b.a16(ATT + c * 1024 + dt * 128, 128),
                           b.a16(OUT + c * 512, 512),
                           start=(c == 0), stop=(c == 7),
                           waits=[w0ld[c]] +
                           ([b.bfree[6 + dt % 2]] if c == 0 else []),
                           inc="sT")
            pj = b.a32(3072 + (dt % 2) * 512, 512)
            evp = P.op("vector", lambda e, pj=pj, bk=bk, dt=dt:
                       e.tensor_scalar(pj, bk[:, :],
                                       b.bias[:, 72 + dt:73 + dt], None,
                                       ALU.add),
                       waits=[mmv, prev_ev.get(dt % 2)] + out_ready,
                       inc="sV")
            b.bfree[6 + dt % 2] = evp
            xt = b.a32((dt % 2) * 512, 512)
            evx = P.dma("sync", xt,
                        x_src[dt * 128:(dt + 1) * 128, q0:q0 + TB],
                        waits=out_ready + [prev_ev.get(dt % 2)],
                        cls=f"l1px{dt % 2}")
            e2 = P.op("vector", lambda e, xt=xt, pj=pj:
                      e.tensor_add(xt, xt, pj), waits=[evx, evp], inc="sV")
            evst = P.dma("sync", x_dst[dt * 128:(dt + 1) * 128, q0:q0 + TB],
                         xt, waits=[e2], cls=f"l1st{dt % 2}")
            prev_ev[dt % 2] = evst
            lastp.append(evst)
        pgate = P.snap()
    return lastp



def emit_l1ffn(b: Build, x_src, x_dst):
    P = b.P
    ip = b.inp
    entry = P.snap()
    W1o = 0
    W2o = 32768
    H1o = 65536
    XNB = 81920

    ggate = [entry, entry]
    prev_tr = None
    prev_mm = {}
    ct_st = {}
    for tb in range(NTB):
        ss = tb % 2
        t0 = tb * TB
        g = ggate[ss]
        for tt in range(4):
            t00 = t0 + tt * 128
            xl = []
            for c in range(C):
                xt = b.a32(ss * 2048 + c * 256 + (tt % 2) * 128, 128)
                xl.append((xt, P.dma("sync", xt,
                                     b.xns[c * 128:(c + 1) * 128,
                                           t00:t00 + 128],
                                     waits=g + [prev_mm.get((ss, tt % 2))],
                                     cls=f"rxl{ss}{tt % 2}")))
            bk = b.pb[ss]
            mmv = None
            xlall = xl[-1][1]
            for c in range(C):
                mmv = P.mm(bk[:, 0:8], xl[c][0],
                           b.wr_t[:, c * 8:(c + 1) * 8],
                           start=(c == 0), stop=(c == 7),
                           waits=[xlall] +
                           ([b.bfree[ss]] if c == 0 else []),
                           inc="sT", cost=64)
            prev_mm[(ss, tt % 2)] = mmv
            lt = b.a32(4096 + ss * 512, 8)
            mx = b.a32(4096 + ss * 512 + 8, 1)
            sm = b.a32(4096 + ss * 512 + 9, 1)
            gtv = b.a32(4096 + ss * 512 + 10, 1)
            tk = b.a32(4096 + ss * 512 + 11, 1)
            ep = b.a32(4096 + ss * 512 + 16, 8)
            eq = b.a32(4096 + ss * 512 + 24, 8)
            ct = b.a32(4096 + ss * 512 + 32, 8)
            e0 = P.op("vector", lambda e, lt=lt, bk=bk: e.tensor_add(
                lt, bk[:, 0:8], b.br_b[:]),
                waits=[mmv, prev_tr] + g, inc="sV")
            b.bfree[ss] = e0
            P.op("vector", lambda e, lt=lt, mx=mx: e.reduce_max(
                mx, lt, axis=AX.X))
            e1 = P.op("vector", lambda e, lt=lt, mx=mx, ep=ep:
                      e.tensor_scalar(ep, lt, mx, None, ALU.subtract),
                      inc="sV")
            e2 = P.op("scalar", lambda e, ep=ep: e.activation(
                ep, ep, AF.Exp), waits=[e1], inc="sA")
            e3 = P.op("vector", lambda e, ep=ep, sm=sm: e.reduce_sum(
                sm, ep, axis=AX.X), waits=[e2], inc="sV")
            P.op("vector", lambda e, sm=sm, gtv=gtv: e.reciprocal(gtv, sm))
            P.op("vector", lambda e, lt=lt, mx=mx, eq=eq:
                 e.tensor_scalar(eq, lt, mx, None, ALU.is_ge))
            P.op("vector", lambda e, tk=tk: e.memset(tk, 0.0))
            for ee in range(E):
                P.op("vector", lambda e, tk=tk, ct=ct, ee=ee:
                     e.tensor_scalar(ct[:, ee:ee + 1], tk, -1.0, 1.0,
                                     ALU.mult, ALU.add))
                P.op("vector", lambda e, eq=eq, ct=ct, ee=ee:
                     e.tensor_mul(ct[:, ee:ee + 1], ct[:, ee:ee + 1],
                                  eq[:, ee:ee + 1]))
                P.op("vector", lambda e, tk=tk, ct=ct, ee=ee:
                     e.tensor_add(tk, tk, ct[:, ee:ee + 1]))
                P.op("vector", lambda e, ct=ct, gtv=gtv, ee=ee:
                     e.tensor_mul(ct[:, ee:ee + 1], ct[:, ee:ee + 1], gtv))
            e5 = P.op("vector", lambda e: e.engine_nop(), inc="sV")
            bkt = b.pb[2 + ss]
            evtr = P.tr(bkt[0:8, 0:128], ct, b.ident_f[:],
                        waits=[e5, b.bfree[2 + ss]], inc="sT")
            prev_tr = evtr
            evc = P.op("vector", lambda e, bkt=bkt: e.tensor_copy(
                b.c_t[:, :], bkt[0:8, 0:128]),
                waits=[evtr, ct_st[0] if ct_st else None] + g, inc="sV")
            b.bfree[2 + ss] = evc
            ct_st[0] = P.dma("sync", b.c_dram[:, t00:t00 + 128], b.c_t[:, :],
                             waits=[evc], cls="rct")
        ggate[ss] = P.snap()
    router_done = P.snap()

    ygate = [router_done, router_done]
    evst = None
    yt_st, ya_rd = {}, {}
    xnb_pe_gate = [None]
    w1_free = [router_done[0]]
    w2_free = [router_done[0]]
    bias_free = [router_done[0], router_done[0]]
    for e_i in range(E):
        eg = P.snap()
        w1ld = [P.dma("gpsimd", b.a16(W1o + c * 4096, 4096),
                      ip["w1e"][e_i, c * 128:(c + 1) * 128, :],
                      waits=[w1_free[0]], cls="mw1")
                for c in range(C)]
        w1ld = [w1ld[-1]] * C
        w2ld = [P.dma("gpsimd",
                      b.a16(W2o + blk * 4096, 4096).rearrange(
                          "p (cc d) -> p cc d", cc=4),
                      ip["w2e"][e_i, blk * 512:(blk + 1) * 512, :].rearrange(
                          "(cc p) d -> p cc d", p=128), waits=[w2_free[0]],
                      cls="mw2")
                for blk in range(8)]
        w2ld = [w2ld[-1]] * 8
        evb1 = P.dma("sync", b.b1t[:, :], ip["b1e_p"][e_i],
                     waits=bias_free, cls="mb")
        evb2 = P.dma("sync", b.bias[:, 80:88], ip["b2e_p"][e_i],
                     waits=bias_free, cls="mb")
        evb1 = evb2

        def w2tile(f, ht):
            blk, cc = divmod(f, 4)
            return b.a16(W2o + blk * 4096 + cc * 1024 + ht * 128, 128), blk

        for T in range(NTB):
            ss = T % 2
            t0 = T * TB
            g = ygate[ss]
            xnb = [b.a16(XNB + c * 512, 512) for c in range(C)]
            xld = [P.dma("gpsimd", xnb[c],
                         b.xns[c * 128:(c + 1) * 128, t0:t0 + TB],
                         waits=g + [xnb_pe_gate[0]], cls="mxnb")
                   for c in range(C)]
            xld = [xld[-1]] * C
            ev_h1 = []
            for f in range(32):
                bk = b.pb[f % 2]
                mmv = None
                for c in range(C):
                    mmv = P.mm(bk[:, :],
                               b.a16(W1o + c * 4096 + f * 128, 128), xnb[c],
                               start=(c == 0), stop=(c == 7),
                               waits=[xld[c], w1ld[c]] +
                               ([b.bfree[f % 2]] if c == 0 else []),
                               inc="sT")
                ev = P.op("scalar", lambda e, f=f, bk=bk: e.activation(
                    b.a16(H1o + f * 512, 512), bk[:, :], AF.Relu,
                    bias=b.b1t[:, f:f + 1]),
                    waits=[mmv, evb1] + g, inc="sA")
                ev_h1.append(ev)
                b.bfree[f % 2] = ev
            xnb_pe_gate[0] = Ev("sT", P.cnt["sT"])
            cer = b.row(R_CE)
            evce = P.dma("sync", cer, b.c_dram[e_i:e_i + 1, t0:t0 + TB],
                         waits=g, cls=f"mce{ss}")
            cbank = b.pb[3]
            evcm = P.mm(cbank[:, :], b.onerow_f[:], cer, True, True,
                        waits=[evce, b.bfree[3]], inc="sT", cost=2048)
            cb = b.a32(2048 + ss * 512, 512)
            evcb = P.op("vector", lambda e, cb=cb, cbank=cbank:
                        e.tensor_copy(cb, cbank[:, :]),
                        waits=[evcm] + g, inc="sV")
            b.bfree[3] = evcb
            for ht in range(8):
                bk = b.pb[4 + ht % 2]
                mmv = None
                for f in range(32):
                    wt, blk = w2tile(f, ht)
                    mmv = P.mm(bk[:, :], wt, b.a16(H1o + f * 512, 512),
                               start=(f == 0), stop=(f == 31),
                               waits=[ev_h1[f], w2ld[blk]] +
                               ([b.bfree[4 + ht % 2]] if f == 0 else []),
                               inc="sT")
                yt = b.a32(3072 + (ht % 2) * 512, 512)
                evp = P.op("vector", lambda e, yt=yt, bk=bk, ht=ht:
                           e.tensor_scalar(yt, bk[:, :],
                                           b.bias[:, 80 + ht:81 + ht], None,
                                           ALU.add),
                           waits=[mmv, evb2, yt_st.get(ht % 2)] + g,
                           inc="sV")
                b.bfree[4 + ht % 2] = evp
                e1 = P.op("vector", lambda e, yt=yt, cb=cb:
                          e.tensor_mul(yt, yt, cb),
                          waits=[evp, evcb], inc="sV")
                if e_i > 0:
                    ya = b.a32(4096 + (ht % 2) * 512, 512)
                    evy = P.dma("sync", ya,
                                b.alts[ht * 128:(ht + 1) * 128, t0:t0 + TB],
                                waits=g + [ya_rd.get(ht % 2)],
                                cls=f"mya{ht % 2}")
                    e1 = P.op("vector", lambda e, yt=yt, ya=ya:
                              e.tensor_add(yt, yt, ya),
                              waits=[evy], inc="sV")
                    ya_rd[ht % 2] = e1
                if e_i == E - 1:
                    xr = b.a32(0 + (ht % 2) * 512, 512)
                    evx = P.dma("sync", xr,
                                x_src[ht * 128:(ht + 1) * 128, t0:t0 + TB],
                                waits=g + [ya_rd.get(ht % 2)],
                                cls=f"mxr{ht % 2}")
                    e2 = P.op("vector", lambda e, yt=yt, xr=xr:
                              e.scalar_tensor_tensor(yt, yt, 0.5, xr,
                                                     ALU.mult, ALU.add),
                              waits=[evx, e1], inc="sV")
                    ya_rd[ht % 2] = e2
                    evst = P.dma("sync",
                                 x_dst[ht * 128:(ht + 1) * 128, t0:t0 + TB],
                                 yt, waits=[e2], cls=f"myst{ht % 2}")
                else:
                    evst = P.dma("sync",
                                 b.alts[ht * 128:(ht + 1) * 128, t0:t0 + TB],
                                 yt, waits=[e1], cls=f"myst{ht % 2}")
                yt_st[ht % 2] = evst
            if T == NTB - 1:
                w1_free[0] = xnb_pe_gate[0]
            ygate[ss] = P.snap()
        w2_free[0] = Ev("sT", P.cnt["sT"])
        bias_free[0] = Ev("sA", P.cnt["sA"])
        bias_free[1] = Ev("sV", P.cnt["sV"])
    return [evst]



def build_full(parts, detect_races=False, pace=True):
    b = Build(detect_races, pace=pace)
    P = b.P
    pre = emit_preamble(b)
    cur = b.inp["x_in"]
    finals = []
    n = len(parts)
    for i, part in enumerate(parts):
        dst = b.y_out if i == n - 1 else b.xs
        emit_ln(b, cur)
        if part == "ln":
            g = P.snap()
            finals = [P.dma("sync", b.y_out[:], b.xns[:], waits=g,
                            cls="dump")]
        elif part.startswith("l0f"):
            nc_ = int(part[3:]) if len(part) > 3 else 3
            finals = emit_l0ffn(b, cur, dst, ncycles=nc_)
        elif part == "l0a":
            finals = emit_l0attn(b, cur, dst)
        elif part == "l1a":
            finals = emit_l1attn(b, cur, dst)
        elif part == "l1f":
            finals = emit_l1ffn(b, cur, dst)
        cur = dst
    end = P.snap()
    P.op("gpsimd", lambda e: e.nop(), waits=end)
    sems = {}
    for s in P.cnt:
        sems[s] = b.nc.alloc_semaphore(name=s)
    b.sems = sems
    P.emit(b.nc, sems)
    return b


PARTS_ALL = ("l0f", "l0a", "l0f", "l1f", "l1a", "l1f")


def prep_inputs(inputs):
    f = np.ascontiguousarray
    x = inputs["L0_attn_representations"]
    maps = []
    base = {}
    base["w_cT"] = f(inputs["L0_attn_Wc"].T)
    bc = inputs["L0_attn_bc"]
    base["bc_p"] = f(bc[:2048].reshape(16, 128).T)
    base["bc_v"] = f(bc[2048:].reshape(1, 8192))
    base["w_oT"] = f(inputs["L0_attn_Wo"].T)
    base["bo_p"] = f(inputs["L0_attn_bo"].reshape(8, 128).T)
    base["cn_w1"] = f(inputs["L0_ffn_cn_w1"])
    base["cn_w2"] = f(inputs["L0_ffn_cn_w2"])
    base["cn_w3"] = f(inputs["L0_ffn_cn_w3"])
    base["cn_w4"] = f(inputs["L0_ffn_cn_w4"])
    base["gc_w1"] = f(inputs["L0_ffn_gc_w1"])
    base["gc_w2"] = f(inputs["L0_ffn_gc_w2"])
    base["gc_w3"] = f(inputs["L0_ffn_gc_w3"])
    l0fb = np.zeros((128, 24), np.float32)
    l0fb[:, 0:8] = inputs["L0_ffn_cn_b1"].reshape(8, 128).T
    l0fb[:, 8:12] = inputs["L0_ffn_cn_b2"].reshape(4, 128).T
    l0fb[:, 12:14] = inputs["L0_ffn_cn_b3"].reshape(2, 128).T
    l0fb[:, 14:18] = inputs["L0_ffn_gc_b1"].reshape(4, 128).T
    l0fb[:, 18:20] = inputs["L0_ffn_gc_b2"].reshape(2, 128).T
    l0fb[0, 20] = inputs["L0_ffn_cn_b4"][0]
    l0fb[0, 21] = inputs["L0_ffn_gc_b3"][0]
    l0fb[0, 22] = inputs["L0_ffn_adaptive_range"][0]
    base["l0f_b"] = l0fb
    base["wq"] = f(inputs["L1_attn_Wq"])
    base["wk"] = f(inputs["L1_attn_Wk"])
    base["wv"] = f(inputs["L1_attn_Wv"])
    base["w0"] = f(inputs["L1_attn_W0"])
    qb = np.zeros((128, 32), np.float32)
    qb[:, 0:8] = inputs["L1_attn_bq"].reshape(8, 128).T
    qb[:, 8:16] = inputs["L1_attn_bk"].reshape(8, 128).T
    qb[:, 16:24] = inputs["L1_attn_bv"].reshape(8, 128).T
    qb[:, 24:32] = inputs["L1_attn_b0"].reshape(8, 128).T
    base["qkvb_p"] = qb
    base["qkvb_rows"] = f(np.stack([inputs["L1_attn_bq"],
                                    inputs["L1_attn_bk"],
                                    inputs["L1_attn_bv"],
                                    inputs["L1_attn_b0"]]))
    base["wr"] = f(inputs["L1_ffn_Wr"])
    base["br_row"] = f(inputs["L1_ffn_br"].reshape(1, 8))
    base["w1e"] = f(inputs["L1_ffn_W1"])
    base["w2e"] = f(inputs["L1_ffn_W2"])
    base["b1e_p"] = f(inputs["L1_ffn_b1"].reshape(8, 32, 128)
                      .transpose(0, 2, 1))
    base["b2e_p"] = f(inputs["L1_ffn_b2"].reshape(8, 8, 128)
                      .transpose(0, 2, 1))
    for core in range(NCORES):
        m = dict(base)
        m["x_in"] = f(np.asarray(x[core]).T)
        maps.append(m)
    return maps


def kernel(**inputs):
    maps = prep_inputs(inputs)
    b = build_full(PARTS_ALL)
    res = run_bass_kernel_spmd(b.nc, maps, core_ids=list(range(NCORES)))
    out = np.stack([np.asarray(r["y_out"]).T for r in res.results])
    return out.astype(np.float32)
